# version 16
# baseline (speedup 1.0000x reference)
import numpy as np
import ml_dtypes
import concourse.bass as bass
import concourse.mybir as mybir
from concourse.bass_utils import run_bass_kernel_spmd

F32 = mybir.dt.float32
BF16 = mybir.dt.bfloat16
F16 = mybir.dt.float16
I16 = mybir.dt.int16
AF = mybir.ActivationFunctionType
ALU = mybir.AluOpType
AX = mybir.AxisListType
BF = ml_dtypes.bfloat16

H = 2048
T = 2048
DK = 128
VE = 256
NH = 8
I_FULL = 5632
I_PAD = 3072
C = 128
NCHUNK = T // C
THALF = T // 2
NEG_BIG = -1.0e30
GROUPS = [[0, 1], [2, 3], [4, 5], [6, 7]]

PIH = np.array([8 * j + cls for cls in range(8) for j in range(H // 8)], dtype=np.int64)
_DEBUG_OUT = [False]


def _pi_i(role):
    idx = np.full(I_PAD, -1, dtype=np.int64)
    k = 0
    for hw in range(2):
        for c2 in range(4):
            cls = 4 * hw + c2
            for j in range(352 * role, 352 * role + 352):
                idx[k] = 8 * j + cls
                k += 1
            k += 32
    return idx


class P:

    ESEM = {"tensor": "s_pe", "vector": "s_ve", "scalar": "s_se", "gpsimd": "s_ge",
            "sync": "s_sy"}

    def __init__(self, sems):
        self.sems = sems
        self.cnt = {n: 0 for n in sems}
        self.plan = {e: [] for e in ("sync", "tensor", "vector", "scalar", "gpsimd")}

    def _emit(self, engine, fn, waits, inc, inc_n):
        ws = []
        seen = {}
        for t in waits:
            if t is None:
                continue
            s, v = t
            if seen.get(s, -1) < v:
                seen[s] = v
        ws = tuple(seen.items())
        self.plan[engine].append((fn, ws, inc, inc_n))
        if inc is not None:
            self.cnt[inc] += inc_n
            return (inc, self.cnt[inc])
        return None

    def dma(self, engine, dst, src, waits=(), sem=None):
        assert sem is not None
        return self._emit(engine, lambda e, dst=dst, src=src: e.dma_start(out=dst, in_=src),
                          waits=waits, inc=sem, inc_n=16)

    def op(self, engine, fn, waits=(), inc=True):
        sem = self.ESEM[engine] if inc else None
        return self._emit(engine, fn, waits=waits, inc=sem, inc_n=1)

    def coll(self, fn, waits=()):
        return self._emit("gpsimd", fn, waits=waits, inc="s_coll", inc_n=1)

    def assemble(self, block):
        for engine in self.plan:
            def make(engine):
                def body(eng):
                    for fn, waits, inc, inc_n in self.plan[engine]:
                        for (s, v) in waits:
                            eng.wait_ge(self.sems[s], v)
                        ins = fn(eng)
                        if inc is not None:
                            ins.then_inc(self.sems[inc], inc_n)
                return body
            getattr(block, engine)(make(engine))


class Buf:

    def __init__(self):
        self.writer = None
        self.readers = []

    def wwaits(self):
        w = list(self.readers)
        if self.writer is not None:
            w.append(self.writer)
        return w

    def wrote(self, tok):
        self.writer = tok
        self.readers = []

    def rwaits(self):
        return [self.writer] if self.writer is not None else []

    def read(self, tok):
        if tok is not None:
            self.readers.append(tok)


_DEFER = [False]


def build_program():
    nc = bass.Bass("TRN2", target_bir_lowering=False, debug=False)
    dt = nc.dram_tensor

    x_in = dt("x_pi", [T, H], F32, kind="ExternalInput").ap()
    wq_in = dt("wq", [H, NH * DK], F32, kind="ExternalInput").ap()
    wk_in = dt("wk", [H, NH * DK], F32, kind="ExternalInput").ap()
    wv_in = dt("wv", [H, NH * VE], F32, kind="ExternalInput").ap()
    wab_in = dt("wab", [H, 40], F32, kind="ExternalInput").ap()
    wo_in = dt("wo", [2048, H], F32, kind="ExternalInput").ap()
    qc_in = dt("qconv", [NH * DK, 8], F32, kind="ExternalInput").ap()
    kc_in = dt("kconv", [NH * DK, 8], F32, kind="ExternalInput").ap()
    vc_in = dt("vconv", [NH * VE, 8], F32, kind="ExternalInput").ap()
    gup0_in = dt("gupT_hw0", [H // 8, 2 * I_PAD], I16, kind="ExternalInput").ap()
    gup1_in = dt("gupT_hw1", [H // 8, 2 * I_PAD], I16, kind="ExternalInput").ap()
    gsc_in = dt("gup_scale", [H, 2 * I_PAD], F16, kind="ExternalInput").ap()
    dwn0_in = dt("dwnT_hw0", [384, H], I16, kind="ExternalInput").ap()
    dwn1_in = dt("dwnT_hw1", [384, H], I16, kind="ExternalInput").ap()
    dsc_in = dt("dwn_scale", [I_PAD, H], F16, kind="ExternalInput").ap()
    cst_in = dt("consts_f", [128, 640], F32, kind="ExternalInput").ap()
    cbf_in = dt("consts_bf", [128, 128], BF16, kind="ExternalInput").ap()

    y_out = dt("y2", [T, H], F32, kind="ExternalOutput").ap()
    DBG = bool(globals().get("_DEBUG_OUT", [False])[0]) if False else _DEBUG_OUT[0]
    if DBG:
        dbg_xn = dt("dbg_xn", [128, 16 * T], BF16, kind="ExternalOutput").ap()
        dbg_q = dt("dbg_q", [128, NH * T], BF16, kind="ExternalOutput").ap()
        dbg_k = dt("dbg_k", [128, NH * T], BF16, kind="ExternalOutput").ap()
        dbg_v = dt("dbg_v", [NH * VE, T], BF16, kind="ExternalOutput").ap()
        dbg_sp = dt("dbg_sp", [T, H], F32, kind="ExternalOutput").ap()
        dbg_o = dt("dbg_o", [128, 16 * T], BF16, kind="ExternalOutput").ap()
        dbg_L = dt("dbg_L", [128, 16 * NCHUNK * NH], F32, kind="ExternalOutput").ap()
        dbg_att = dt("dbg_att", [T, H], BF16, kind="ExternalOutput").ap()
        dbg_xn2 = dt("dbg_xn2", [T, H], BF16, kind="ExternalOutput").ap()
        dbg_z = dt("dbg_z", [128, 2052], F32, kind="ExternalOutput").ap()
        dbg_h = dt("dbg_h", [128, 24 * 512], BF16, kind="ExternalOutput").ap()
        dbg_w = dt("dbg_w", [128, 24 * 512], BF16, kind="ExternalOutput").ap()

    wgu_d = dt("wgu_d", [H, 2 * I_PAD], BF16, kind="Internal").ap()
    dwn_d = dt("dwn_d", [I_PAD, H], BF16, kind="Internal").ap()
    vt_d = dt("vt_d", [NH * VE, T], BF16, kind="Internal").ap()
    spill_d = dt("spill_d", [T, H], F32, kind="Internal").ap()
    ccy_in = [dt(f"ccy_in{i}", [512, H], BF16, kind="Internal").ap() for i in range(4)]
    ccy_out = [dt(f"ccy_out{i}", [512, H], BF16, kind="Internal").ap() for i in range(4)]
    cca_in = [dt(f"cca_in{i}", [512, H], BF16, kind="Internal").ap() for i in range(4)]
    cca_out = [dt(f"cca_out{i}", [512, H], BF16, kind="Internal").ap() for i in range(4)]

    sbt = nc.sbuf_tensor
    pst = nc.psum_tensor
    import contextlib
    ctx = contextlib.ExitStack()

    cstf = ctx.enter_context(sbt("cstf", [128, 640], F32))
    cstb = ctx.enter_context(sbt("cstb", [128, 128], BF16))
    IDF = cstf[:, 0:128]
    TRI = cstf[:, 128:256]
    MSK_S = cstf[:, 256:384]
    MSK_IT = cstf[:, 384:512]
    ONER = cstf[0:1, 512:640]
    IDB = cstb[:, :]
    stat = ctx.enter_context(sbt("stat", [128, 64], F32))
    cw = ctx.enter_context(sbt("cw", [128, 16], F32))

    U8 = mybir.dt.uint8
    A_ar = ctx.enter_context(sbt("A_ar", [128, 65536], U8))
    B_ar = ctx.enter_context(sbt("B_ar", [128, 32768], U8))
    C_ar = ctx.enter_context(sbt("C_ar", [128, 32768], U8))
    D_ar = ctx.enter_context(sbt("D_ar", [128, 78336], U8))

    def cv(ar, off, nel, dtp):
        sz = {F32: 4, BF16: 2, F16: 2, I16: 2}[dtp]
        return ar[:, off:off + nel * sz].bitcast(dtp)

    xnT = cv(A_ar, 0, 16 * T, BF16)
    oTa = cv(A_ar, 0, 16 * T, BF16)
    qhT = cv(B_ar, 0, NH * T, BF16)
    khT = cv(C_ar, 0, NH * T, BF16)
    dq_in = [cv(B_ar, 0, 3 * 1024, I16), cv(B_ar, 6144, 3 * 1024, I16)]
    dq_sc = [cv(B_ar, 12288, 16 * 512, F16), cv(C_ar, 0, 16 * 512, F16)]
    dq_t2 = cv(B_ar, 28672, 512, I16)
    dq_n2 = cv(B_ar, 29696, 512, I16)
    harena = cv(B_ar, 0, 24 * 512, BF16)
    ybuf = cv(C_ar, 0, 4 * H, BF16)
    xbuf = [cv(D_ar, 0, H, F32), cv(D_ar, 8192, H, F32)]
    sc1 = cv(D_ar, 16384, H, F32)
    sc2 = cv(D_ar, 24576, H, F32)
    Z = 32768
    dq_o = [cv(D_ar, Z, 16 * 512, BF16), cv(D_ar, Z + 16384, 16 * 512, BF16)]
    dq_o_d = cv(D_ar, Z, 24 * 512, BF16)
    dq_t = cv(B_ar, 30720, 512, I16)
    dq_n = cv(B_ar, 31744, 512, I16)
    dq_in_d = cv(D_ar, 24576, 3 * 1024, I16)
    dq_sc_d = cv(C_ar, 0, 12 * 512, F16)

    wslab2 = [cv(C_ar, 16384, 16 * 128, BF16), cv(C_ar, 20480, 16 * 128, BF16)]
    dslab = [cv(D_ar, Z + 8192, 24 * 512, BF16), cv(D_ar, Z + 8192, 24 * 512, BF16)]
    cvb = cv(D_ar, Z, 2052, BF16)
    cva = cv(D_ar, Z + 4160, 2048, BF16)
    wslab3 = [cv(D_ar, Z + 8320, 16 * 128, BF16), cv(D_ar, Z + 12416, 16 * 128, BF16)]
    ab_sb = cv(D_ar, 0, T, F32)[0:40, :]
    lab_sb = cv(D_ar, 8192, T, F32)[0:40, :]
    abT_sb = cv(D_ar, Z + 16512, 40, F32)
    cvb2 = cv(D_ar, Z + 16704, 2052, BF16)
    cva2 = cv(D_ar, Z + 20864, 2048, BF16)
    vstg = cv(D_ar, Z + 24960, 2048, BF16)
    vbuf = [cv(D_ar, 0, 16 * 128, BF16), cv(D_ar, 4096, 16 * 128, BF16)]
    vtok = cv(D_ar, 8192, 16 * 128, BF16)
    ktok = cv(D_ar, 12288, NH * 128, BF16)
    otk = cv(D_ar, 14336, NH * 256, BF16)
    Sm = cv(D_ar, 18432, NH * 256, F32)
    Sbb = cv(D_ar, 26624, NH * 256, BF16)
    MOFF = Z
    Nsb, NTsb, Dsb, Wsb, Xsb, XTsb, PTsb, Rsb, Usb, dUsb, T1sb = ([], [], [], [], [], [],
                                                                  [], [], [], [], [])
    LCbs = []
    mo = MOFF
    for i in range(2):
        Nsb.append(cv(D_ar, mo, 128, BF16)); mo += 256
        NTsb.append(cv(D_ar, mo, 128, BF16)); mo += 256
        Dsb.append(cv(D_ar, mo, 128, F32)); mo += 512
        Wsb.append(cv(D_ar, mo, 128, BF16)); mo += 256
        Xsb.append(cv(D_ar, mo, 11 * 128, BF16)); mo += 2816
        PTsb.append(cv(D_ar, mo, 2 * 128, BF16)); mo += 512
        Rsb.append(cv(D_ar, mo, 256, BF16)); mo += 512
        Usb.append(cv(D_ar, mo, 256, BF16)); mo += 512
        dUsb.append(cv(D_ar, mo, 256, BF16)); mo += 512
        T1sb.append(cv(D_ar, mo, 256, F32)); mo += 1024
        LCbs.append(cv(D_ar, mo, 8, F32)); mo += 64
    assert mo <= Z + 32768, mo
    wslab5 = [cv(D_ar, Z + 16384, 16 * 128, BF16), cv(D_ar, Z + 20480, 16 * 128, BF16)]
    ws5f = [cv(C_ar, 16384, 16 * 128, F32), cv(C_ar, 24576, 16 * 128, F32)]
    outb = cv(D_ar, Z + 24576, 2048, F32)
    TAIL = Z + 32768
    Larr = cv(D_ar, TAIL, NCHUNK * NH, F32)
    nLarr = cv(D_ar, TAIL + 512, NCHUNK * NH, F32)
    nBarr = cv(D_ar, TAIL + 1024, NCHUNK * NH, F32)
    Btarr = cv(D_ar, TAIL + 1536, NCHUNK * NH, F32)
    Garr = cv(D_ar, TAIL + 2048, NCHUNK * NH, F32)
    BGarr = cv(D_ar, TAIL + 2560, NCHUNK * NH, F32)
    dCarr = cv(D_ar, TAIL + 3072, NCHUNK * NH, F32)
    gCarr = cv(D_ar, TAIL + 3584, NCHUNK * NH, F32)
    LrowS = [cv(D_ar, TAIL + 4096, 128, F32)[0:1, :], cv(D_ar, TAIL + 4608, 128, F32)[0:1, :]]
    nLrowS = [cv(D_ar, TAIL + 5120, 128, F32)[0:1, :], cv(D_ar, TAIL + 5632, 128, F32)[0:1, :]]
    assert TAIL + 6144 <= 78336

    pF = [ctx.enter_context(pst(f"pF{i}", [128, 512], F32)) for i in range(2)]
    pDW = [ctx.enter_context(pst(f"pDW{i}", [128, 512], F32)) for i in range(2)]
    pXP = [ctx.enter_context(pst(f"pXP{i}", [128, 512], F32)) for i in range(2)]
    pTB = ctx.enter_context(pst("pTB", [128, 1024], BF16))
    pG = ctx.enter_context(pst("pG", [128, 512], F32))
    PTF = pTB[:, :].bitcast(F32)

    sem_names = ["s_cst", "s_x0", "s_x1", "s_dqi0", "s_dqi1", "s_dqo", "s_ws0", "s_ws1",
                 "s_ds", "s_cw0", "s_cw1", "s_wf0", "s_wf1", "s_hm", "s_yo", "s_vl0", "s_vl1", "s_sp", "s_vt", "s_st",
                 "s_pe", "s_ve", "s_se", "s_ge", "s_coll"]
    sems = {n: ctx.enter_context(nc.semaphore(n)) for n in sem_names}
    p = P(sems)

    B_pF = [Buf(), Buf()]
    B_pDW = [Buf(), Buf()]
    B_pXP = [Buf(), Buf()]
    B_pTB = Buf()
    B_pG = Buf()
    B_wslab = [Buf(), Buf()]
    B_dslab = [Buf(), Buf()]
    B_har = Buf()
    B_ybuf = Buf()
    B_xbuf = [Buf(), Buf()]
    B_sc1 = Buf()
    B_sc2 = Buf()
    B_cv = Buf()
    B_vbuf = [Buf(), Buf()]
    B_vtok = Buf()
    B_ktok = Buf()
    B_otk = Buf()
    B_dqin = [Buf(), Buf()]
    B_cwb = Buf()
    B_cw2 = Buf()
    B_dqo = [Buf(), Buf()]
    B_dqo_d = Buf()

    def dequant_block(kind, ib, b, in_toks, ntile_pt, nout_tiles, dram, sc_tile_fn,
                      obuf=None, oB=None, inbuf=None, inB=None, hw1_hook=None,
                      sc_read_cb=None, filler=None):
        obuf = dq_o[b] if obuf is None else obuf
        oB = B_dqo[b] if oB is None else oB
        inbuf = dq_in[b] if inbuf is None else inbuf
        inB = B_dqin[b] if inB is None else inB
        first_w = in_toks + oB.wwaits()
        last = None
        last_v = last_g = None
        hw1w = []
        for hw in range(2):
            eng = "vector"
            myt = dq_t if hw == 0 else dq_t2
            myn = dq_n if hw == 0 else dq_n2
            if hw == 1 and hw1_hook is not None:
                hw1w = hw1_hook()
            for c2 in range(4):
                cls = 4 * hw + c2
                sa = 12 - 4 * c2
                sbs = 9 - 4 * c2
                for pt in range(ntile_pt):
                    src = inbuf[:, (hw * ntile_pt + pt) * 512:(hw * ntile_pt + pt + 1) * 512]
                    src32 = src.bitcast(mybir.dt.int32)
                    t32 = myt[:, :].bitcast(mybir.dt.int32)
                    n32 = myn[:, :].bitcast(mybir.dt.int32)
                    fw = (first_w if hw == 0 else hw1w) if (c2 == 0 and pt == 0) else []
                    if sa > 0:
                        p.op(eng, lambda v, src32=src32, sa=sa, t32=t32: v.tensor_scalar(
                            t32, src32, sa, 0x80008000 - 2**32, ALU.logical_shift_left,
                            ALU.bitwise_and), waits=fw, inc=False)
                    else:
                        p.op(eng, lambda v, src32=src32, t32=t32: v.tensor_scalar(
                            t32, src32, 0x80008000 - 2**32, None, ALU.bitwise_and),
                            waits=fw, inc=False)
                    if sbs > 0:
                        p.op(eng, lambda v, src32=src32, sbs=sbs, n32=n32: v.tensor_scalar(
                            n32, src32, sbs, 0x0E000E00, ALU.logical_shift_left,
                            ALU.bitwise_and), inc=False)
                    else:
                        p.op(eng, lambda v, src32=src32, sbs=sbs, n32=n32: v.tensor_scalar(
                            n32, src32, -sbs, 0x0E000E00, ALU.logical_shift_right,
                            ALU.bitwise_and), inc=False)
                    p.op(eng, lambda v, n32=n32, t32=t32: v.tensor_tensor(
                        n32, n32, t32, ALU.bitwise_or), inc=False)
                    tile = cls * ntile_pt + pt
                    dst = obuf[:, tile * 512:(tile + 1) * 512]
                    scs = sc_tile_fn(tile)
                    tokx = p.op(eng, lambda v, dst=dst, scs=scs, myn=myn: v.scalar_tensor_tensor(
                        dst, myn[:, :].bitcast(F16), 16384.0, scs, ALU.mult, ALU.mult))
                    if sc_read_cb is not None:
                        sc_read_cb(tokx)
                    if filler is not None:
                        filler()
                    if hw == 0:
                        last_v = tokx
                    else:
                        last_g = tokx
        inB.read(last_v)
        inB.read(last_g)
        oB.wrote(last_v)
        oB.wrote(last_g)
        last = last_g
        tok = p.dma("gpsimd", dram, obuf[:, 0:nout_tiles * 512].rearrange("p (a n) -> p a n", a=nout_tiles),
                    waits=[last_v, last_g], sem="s_dqo")
        oB.read(tok)
        if DBG and kind == "g" and ib == 0:
            tok = p.dma("gpsimd", dbg_w.rearrange("p (a n) -> p a n", a=24),
                        obuf[:, :].rearrange("p (a n) -> p a n", a=24), waits=[last], sem="s_dqo")
            oB.read(tok)
        return tok

    NBLK_G = 2 * I_PAD // 512
    dq_g_blk = {}
    dq_d_blk = {}
    _gorder = [x for pair in zip(range(6), range(6, 12)) for x in pair]
    _dq_b = [0]
    _gorder = [x for pair in zip(range(6), range(6, 12)) for x in pair]
    DQ_AFTER_TILE = {}

    def load_gate(ib, b):
        dw = B_dqin[b].wwaits()
        w0 = p.dma("gpsimd", dq_in[b][:, 0:1024].rearrange("p (a n) -> p a n", a=2),
                   gup0_in.rearrange("(a p) n -> p a n", p=128)[:, :, ib * 512:(ib + 1) * 512],
                   waits=dw, sem=f"s_dqi{b}")
        w1 = p.dma("gpsimd", dq_in[b][:, 1024:2048].rearrange("p (a n) -> p a n", a=2),
                   gup1_in.rearrange("(a p) n -> p a n", p=128)[:, :, ib * 512:(ib + 1) * 512],
                   sem=f"s_dqi{b}")
        wsc = p.dma("gpsimd", dq_sc[b][:, :].rearrange("p (a n) -> p a n", a=16),
                    gsc_in.rearrange("(a p) n -> p a n", p=128)[:, :, ib * 512:(ib + 1) * 512],
                    sem=f"s_dqi{b}")
        B_dqin[b].wrote(wsc)
        return [w0, w1, wsc]

    def ops_gate(ib, b, toks):
        tok = dequant_block(
            "g", ib, b, toks, 2, 16,
            wgu_d.rearrange("(a p) n -> p a n", p=128)[:, :, ib * 512:(ib + 1) * 512],
            lambda tile, b=b: dq_sc[b][:, tile * 512:(tile + 1) * 512])
        dq_g_blk[ib] = tok

    B_dqind = Buf()
    B_dscd = Buf()

    def emit_dq_gates():
        pend = {}
        for i, ib in enumerate(_gorder):
            if i == 0:
                pend[0] = load_gate(_gorder[0], 0)
                pend[1] = load_gate(_gorder[1], 1)
            ops_gate(ib, i % 2, pend.pop(i))
            if i + 2 < len(_gorder):
                pend[i + 2] = load_gate(_gorder[i + 2], (i + 2) % 2)

    def emit_dq_downs(fillers):
        B_dqo_d.read(dq_g_blk[5])
        B_dqo_d.read(dq_g_blk[11])
        fq = list(fillers)
        fi = [0]
        cnt = [0]

        def filler():
            cnt[0] += 1
            if cnt[0] % 4 == 0 and fi[0] < len(fq):
                fq[fi[0]]()
                fi[0] += 1

        def load_in(ib):
            dwi = B_dqind.wwaits()
            w0 = p.dma("gpsimd", dq_in_d[:, 0:3 * 512].rearrange("p (a n) -> p a n", a=3),
                       dwn0_in.rearrange("(a p) n -> p a n", p=128)[:, :, ib * 512:(ib + 1) * 512],
                       waits=dwi, sem="s_dqi0")
            w1 = p.dma("gpsimd", dq_in_d[:, 3 * 512:6 * 512].rearrange("p (a n) -> p a n", a=3),
                       dwn1_in.rearrange("(a p) n -> p a n", p=128)[:, :, ib * 512:(ib + 1) * 512],
                       sem="s_dqi0")
            B_dqind.wrote(w1)
            return [w0, w1]

        pend_in = load_in(0)
        for ib in range(4):
            sc0 = p.dma("gpsimd", dq_sc_d[:, :].rearrange("p (a n) -> p a n", a=12),
                        dsc_in.rearrange("(a p) n -> p a n", p=128)[:, 0:12, ib * 512:(ib + 1) * 512],
                        waits=B_dscd.wwaits() + (B_dqin[1].wwaits() if ib == 0 else []),
                        sem="s_dqi1")
            B_dscd.wrote(sc0)

            def hw1_hook(ib=ib):
                tok = p.dma("gpsimd", dq_sc_d[:, :].rearrange("p (a n) -> p a n", a=12),
                            dsc_in.rearrange("(a p) n -> p a n", p=128)[:, 12:24, ib * 512:(ib + 1) * 512],
                            waits=B_dscd.wwaits(), sem="s_dqi1")
                B_dscd.wrote(tok)
                return [tok]
            tok = dequant_block(
                "d", ib, 0, pend_in + [sc0], 3, 24,
                dwn_d.rearrange("(a p) n -> p a n", p=128)[:, :, ib * 512:(ib + 1) * 512],
                lambda tile: dq_sc_d[:, (tile % 12) * 512:(tile % 12 + 1) * 512],
                obuf=dq_o_d, oB=B_dqo_d, inbuf=dq_in_d, inB=B_dqind,
                hw1_hook=hw1_hook, sc_read_cb=B_dscd.read, filler=filler)
            dq_d_blk[ib] = tok
            if ib + 1 < 4:
                pend_in = load_in(ib + 1)
        while fi[0] < len(fq):
            fq[fi[0]]()
            fi[0] += 1

    t_cst = p.dma("sync", cstf[:, :], cst_in, sem="s_cst")
    t_cstb = p.dma("sync", cstb[:, :], cbf_in, sem="s_cst")
    x_tok = []
    xn_tile_tok = []

    def emit_x_dma(tt):
        b = tt % 2
        tok = p.dma("sync", xbuf[b][:, :], x_in[tt * 128:(tt + 1) * 128, :],
                    waits=B_xbuf[b].wwaits(), sem=f"s_x{b}")
        B_xbuf[b].wrote(tok)
        x_tok.append(tok)
    emit_x_dma(0)
    emit_x_dma(1)

    evac_last = None
    for tt in range(16):
        b = tt % 2
        xb = xbuf[b]
        r1 = p.op("vector", lambda v, xb=xb: v.reduce_sum(stat[:, 0:1], xb[:, :], axis=AX.X),
                  waits=[x_tok[tt]])
        r2 = p.op("vector", lambda v: v.tensor_scalar(stat[:, 1:2], stat[:, 0:1], 1.0 / H, None, ALU.mult),
                  waits=[r1])
        t_sc1 = p.op("vector", lambda v, xb=xb: v.tensor_scalar(sc1[:, :], xb[:, :], stat[:, 1:2], None, ALU.subtract),
                     waits=[r2] + B_sc1.wwaits())
        B_sc1.wrote(t_sc1)
        tsq = p.op("scalar", lambda s: s.activation(sc2[:, :], sc1[:, :], AF.Square),
                   waits=B_sc2.wwaits() + B_sc1.rwaits())
        B_sc1.read(tsq)
        B_sc2.wrote(tsq)
        r3 = p.op("vector", lambda v: v.reduce_sum(stat[:, 2:3], sc2[:, :], axis=AX.X), waits=[tsq])
        v4 = p.op("vector", lambda v: v.tensor_scalar(stat[:, 3:4], stat[:, 2:3], 1.0 / H, 1e-5, ALU.mult, ALU.add),
                  waits=[r3])
        tsd = p.op("scalar", lambda s: s.activation(stat[:, 4:5], stat[:, 3:4], AF.Sqrt), waits=[v4])
        vrc = p.op("vector", lambda v: v.reciprocal(stat[:, 5:6], stat[:, 4:5]), waits=[tsd])
        tln = p.op("vector", lambda v: v.tensor_scalar(sc2[:, 0:1024].bitcast(BF16)[:, 0:H], sc1[:, :],
                                                       stat[:, 5:6], None, ALU.mult),
                   waits=[tsq, vrc] + B_sc2.wwaits())
        B_sc2.wrote(tln)
        B_xbuf[b].read(tln)
        B_sc1.read(tln)
        if tt + 2 < 16:
            emit_x_dma(tt + 2)
        if DBG:
            dtk = p.dma("gpsimd", dbg_xn2[tt * 128:(tt + 1) * 128, :],
                        sc2[:, 0:1024].bitcast(BF16)[:, 0:H], waits=[tln], sem="s_st")
            B_sc2.read(dtk)
        for bh in range(2):
            ttr = None
            for hh in range(8 * bh, 8 * bh + 8):
                slot = hh % 8
                tw = ([tln, t_cstb] + B_pTB.wwaits()) if slot == 0 else []
                ttr = p.op("tensor", lambda t, hh=hh, slot=slot: t.transpose(
                    pTB[:, slot * 128:(slot + 1) * 128],
                    sc2[:, 0:1024].bitcast(BF16)[:, hh * 128:(hh + 1) * 128], IDB), waits=tw,
                    inc=(slot == 7))
            B_pTB.wrote(ttr)
            for hh in range(8 * bh, 8 * bh + 8):
                slot = hh % 8
                ev = p.op("scalar", lambda s, hh=hh, slot=slot, tt=tt: s.copy(
                    xnT[:, hh * T + tt * 128: hh * T + tt * 128 + 128],
                    pTB[:, slot * 128:(slot + 1) * 128]), waits=([ttr] if slot == 0 else []))
                B_pTB.read(ev)
            evac_last = ev
        B_sc2.read(evac_last)
        xn_tile_tok.append(evac_last)
        if tt in DQ_AFTER_TILE:
            for _ib in DQ_AFTER_TILE[tt]:
                emit_dq_gate(_ib)
    xn_ready = evac_last
    xn_tc = [xn_tile_tok[3], xn_tile_tok[7], xn_tile_tok[11], xn_tile_tok[15]]
    emit_dq_gates()

    y_ar = []
    ws3_pre = {}
    hm_fill = []

    def ffn_gates(tc):
        defer = (tc == 0) and _DEFER[0]
        hm_last = None
        for ipair in range(24):
            mm_done = []
            for which in range(2):
                it = ipair + 24 * which
                b = (tc * 48 + ipair * 2 + which) % 2
                wt = p.dma("sync", wslab2[b][:, :].rearrange("p (a n) -> p a n", a=16),
                           wgu_d.rearrange("(a p) n -> p a n", p=128)[:, :, it * 128:(it + 1) * 128],
                           waits=[dq_g_blk[it // 4 if it < 24 else 6 + (it - 24) // 4]]
                                 + B_wslab[b].wwaits(), sem=f"s_ws{b}")
                B_wslab[b].wrote(wt)
                bank = pF[which]
                mw = [wt, xn_tc[tc]] + B_pF[which].wwaits()
                if defer and which == 1 and ipair > 0:
                    mw.append(("s_hm", ipair))
                lm = None
                for ht in range(16):
                    lm = p.op("tensor", lambda t, bank=bank, b=b, ht=ht, tc=tc: t.matmul(
                        bank[:, 0:512], wslab2[b][:, ht * 128:(ht + 1) * 128],
                        xnT[:, ht * T + tc * 512: ht * T + (tc + 1) * 512],
                        start=(ht == 0), stop=(ht == 15)), waits=(mw if ht == 0 else []),
                        inc=(ht == 15))
                B_pF[which].wrote(lm)
                B_wslab[b].read(lm)
                mm_done.append(lm)
            sil_w = [mm_done[0]] + B_sc1.wwaits()
            if defer and ipair > 0:
                sil_w.append(("s_hm", ipair))
            tsil = p.op("scalar", lambda s: s.activation(sc1[:, 0:512], pF[0][:, 0:512], AF.Silu),
                        waits=sil_w)
            B_pF[0].read(tsil)
            B_sc1.wrote(tsil)
            if defer:
                def mk(ipair=ipair, tsil=tsil, mm1=mm_done[1]):
                    def emit():
                        w = [tsil, mm1] + (B_har.wwaits() if ipair == 0 else [])
                        tok = p._emit("vector", lambda v, ipair=ipair: v.tensor_tensor(
                            harena[:, ipair * 512:(ipair + 1) * 512], sc1[:, 0:512],
                            pF[1][:, 0:512], ALU.mult), waits=w, inc="s_hm", inc_n=1)
                        B_pF[1].read(tok)
                        B_sc1.read(tok)
                    return emit
                hm_fill.append(mk())
                hm_last = ("s_hm", ipair + 1)
            else:
                hm = p.op("vector", lambda v, ipair=ipair: v.tensor_tensor(
                    harena[:, ipair * 512:(ipair + 1) * 512], sc1[:, 0:512], pF[1][:, 0:512],
                    ALU.mult), waits=[tsil, mm_done[1]] + (B_har.wwaits() if ipair == 0 else []))
                B_pF[1].read(hm)
                B_sc1.read(hm)
                hm_last = hm
        B_har.wrote(hm_last)
        return hm_last

    def ffn_down(tc, hm_last):
        ev_last = None
        for hc in range(4):
            b = 0
            wt = p.dma("sync", dslab[b][:, :].rearrange("p (a n) -> p a n", a=24),
                       dwn_d.rearrange("(a p) n -> p a n", p=128)[:, :, hc * 512:(hc + 1) * 512],
                       waits=[dq_d_blk[3]] + B_dslab[0].wwaits(), sem="s_ds")
            B_dslab[0].wrote(wt)
            for tt4 in range(4):
                bank = pDW[tt4 % 2]
                mw = [wt, hm_last] + B_pDW[tt4 % 2].wwaits()
                lm = None
                for it in range(24):
                    lm = p.op("tensor", lambda t, bank=bank, it=it, tt4=tt4, b=b: t.matmul(
                        bank[:, 0:512],
                        harena[:, it * 512 + tt4 * 128: it * 512 + tt4 * 128 + 128],
                        dslab[b][:, it * 512:(it + 1) * 512],
                        start=(it == 0), stop=(it == 23)), waits=(mw if it == 0 else []),
                        inc=(it == 23))
                B_pDW[tt4 % 2].wrote(lm)
                B_dslab[0].read(lm)
                B_har.read(lm)
                ev = p.op("vector", lambda v, bank=bank, tt4=tt4, hc=hc: v.tensor_copy(
                    ybuf[:, tt4 * H + hc * 512: tt4 * H + (hc + 1) * 512], bank[:, 0:512]),
                    waits=[lm] + (B_ybuf.wwaits() if (hc == 0 and tt4 == 0) else []))
                B_pDW[tt4 % 2].read(ev)
                ev_last = ev
        B_ybuf.wrote(ev_last)
        if tc == 3:
            for _nm, _sl, _wi in (("q", 0, wq_in), ("v", 1, wv_in)):
                ws3_pre[(_nm, 0)] = p.dma(
                    "gpsimd", wslab3[_sl][:, :].rearrange("p (a n) -> p a n", a=16),
                    _wi.rearrange("(a p) n -> p a n", p=128)[:, :, 0:128],
                    waits=B_dslab[0].wwaits(), sem=f"s_ws{_sl}")
        yd = p.dma("gpsimd", ccy_in[tc].rearrange("(a p) n -> p a n", p=128),
                   ybuf[:, :].rearrange("p (a n) -> p a n", a=4), waits=[ev_last], sem="s_st")
        B_ybuf.read(yd)
        ar = p.coll(lambda g, tc=tc: g.collective_compute(
            "AllReduce", ALU.add, replica_groups=GROUPS, ins=[ccy_in[tc]], outs=[ccy_out[tc]]),
            waits=[yd])
        y_ar.append(ar)

    if _DEFER[0]:
        ffn_gates(0)
        emit_dq_downs(hm_fill)
        ffn_down(0, ("s_hm", 24))
        for tc in range(1, 4):
            hm_last_t = ffn_gates(tc)
            ffn_down(tc, hm_last_t)
    else:
        emit_dq_downs([])
        for tc in range(4):
            hm_last_t = ffn_gates(tc)
            ffn_down(tc, hm_last_t)

    sp_tok = {}
    for blk in range(4):
        ld = p.dma("sync", ybuf[:, :].rearrange("p (a n) -> p a n", a=4),
                   ccy_out[blk].rearrange("(a p) n -> p a n", p=128),
                   waits=[y_ar[blk]] + B_ybuf.wwaits(), sem="s_yo")
        B_ybuf.wrote(ld)
        for q in range(4):
            tt = blk * 4 + q
            bx = tt % 2
            xw = p.dma("sync", xbuf[bx][:, :], x_in[tt * 128:(tt + 1) * 128, :],
                       waits=B_xbuf[bx].wwaits(), sem=f"s_x{bx}")
            B_xbuf[bx].wrote(xw)
            yb = ybuf[:, q * H:(q + 1) * H]
            t1 = p.op("vector", lambda v, yb=yb: v.tensor_tensor(sc1[:, :], yb, yb, ALU.mult),
                      waits=[ld] + B_sc1.wwaits())
            B_sc1.wrote(t1)
            r8 = p.op("vector", lambda v: v.reduce_sum(stat[:, 8:9], sc1[:, :], axis=AX.X), waits=[t1])
            v4b = p.op("vector", lambda v: v.tensor_scalar(stat[:, 9:10], stat[:, 8:9], 1.0 / H, 1e-6,
                                                           ALU.mult, ALU.add), waits=[r8])
            tsd = p.op("scalar", lambda s: s.activation(stat[:, 10:11], stat[:, 9:10], AF.Sqrt),
                       waits=[t1, v4b])
            vrc2 = p.op("vector", lambda v: v.reciprocal(stat[:, 11:12], stat[:, 10:11]), waits=[tsd])
            t2 = p.op("vector", lambda v, yb=yb: v.tensor_scalar(sc2[:, :], yb, stat[:, 11:12], None,
                                                                 ALU.mult), waits=[vrc2] + B_sc2.wwaits())
            B_sc2.wrote(t2)
            tadd = p.op("vector", lambda v, bx=bx: v.tensor_tensor(sc2[:, :], sc2[:, :],
                                                                   xbuf[bx][:, :], ALU.add),
                        waits=[xw])
            B_sc2.wrote(tadd)
            B_xbuf[bx].read(tadd)
            tok = p.dma("gpsimd", spill_d[tt * 128:(tt + 1) * 128, :], sc2[:, :],
                        waits=[tadd], sem="s_sp")
            B_sc2.read(tok)
            if DBG:
                tok = p.dma("gpsimd", dbg_sp[tt * 128:(tt + 1) * 128, :], sc2[:, :],
                            waits=[tadd], sem="s_sp")
                B_sc2.read(tok)
            B_ybuf.read(tok)
            sp_tok[tt] = tok
    spill_done = sp_tok[15]

    vdma_tok = {}
    qk_store = {"q": qhT, "k": khT}
    PROJ = {"q": (wq_in, qc_in), "k": (wk_in, kc_in), "v": (wv_in, vc_in)}
    B_cv2 = Buf()
    B_vstg = Buf()

    def proj_ct(name, ct, slot):
        w_in, c_in = PROJ[name]
        b = slot
        banks = (pG, pF[0]) if slot == 0 else (pDW[0], pDW[1])
        bbs = (B_pG, B_pF[0]) if slot == 0 else (B_pDW[0], B_pDW[1])
        mycvb, mycva = (cvb, cva) if slot == 0 else (cvb2, cva2)
        B_mycv = B_cv if slot == 0 else B_cv2
        B_mycw = B_cwb if slot == 0 else B_cw2
        wt = ws3_pre.pop((name, ct), None)
        if wt is None:
            wt = p.dma("gpsimd", wslab3[b][:, :].rearrange("p (a n) -> p a n", a=16),
                       w_in.rearrange("(a p) n -> p a n", p=128)[:, :, ct * 128:(ct + 1) * 128],
                       waits=B_wslab[b].wwaits() + (B_dslab[0].wwaits() if ct < 2 else []),
                       sem=f"s_ws{b}")
        B_wslab[b].wrote(wt)
        cwt = p.dma("sync", cw[:, 8 * slot:8 * slot + 8], c_in[ct * 128:(ct + 1) * 128, :],
                    waits=B_mycw.wwaits(), sem=f"s_cw{slot}")
        B_mycw.wrote(cwt)
        evs = []
        for tc in range(4):
            bank = banks[tc % 2]
            bb = bbs[tc % 2]
            mw = [wt, xn_ready] + bb.wwaits()
            lm = None
            for ht in range(16):
                lm = p.op("tensor", lambda t, bank=bank, b=b, ht=ht, tc=tc: t.matmul(
                    bank[:, 0:512], wslab3[b][:, ht * 128:(ht + 1) * 128],
                    xnT[:, ht * T + tc * 512: ht * T + (tc + 1) * 512],
                    start=(ht == 0), stop=(ht == 15)), waits=(mw if ht == 0 else []),
                    inc=(ht == 15))
            bb.wrote(lm)
            B_wslab[b].read(lm)
            evc = p.op("scalar", lambda s, bank=bank, tc=tc, mycvb=mycvb: s.copy(
                mycvb[:, 3 + tc * 512: 3 + (tc + 1) * 512], bank[:, 0:512]),
                waits=[lm] + (B_mycv.wwaits() if tc == 0 else []))
            bb.read(evc)
            evs.append(evc)
            yield
        B_mycv.wrote(evs[-1])
        p.op("vector", lambda v, mycvb=mycvb: v.memset(mycvb[:, 0:3], 0.0),
             waits=[evs[-1], cwt], inc=False)
        p.op("vector", lambda v, mycvb=mycvb, mycva=mycva, slot=slot: v.tensor_scalar(
            mycva[:, :], mycvb[:, 0:2048], cw[:, 8 * slot:8 * slot + 1], None, ALU.mult), inc=False)
        for j in (1, 2):
            p.op("vector", lambda v, j=j, mycvb=mycvb, mycva=mycva, slot=slot: v.scalar_tensor_tensor(
                mycva[:, :], mycvb[:, j:j + 2048], cw[:, 8 * slot + j:8 * slot + j + 1],
                mycva[:, :], ALU.mult, ALU.add), inc=False)
        tconv = p.op("vector", lambda v, mycvb=mycvb, mycva=mycva, slot=slot: v.scalar_tensor_tensor(
            mycva[:, :], mycvb[:, 3:3 + 2048], cw[:, 8 * slot + 3:8 * slot + 4],
            mycva[:, :], ALU.mult, ALU.add))
        B_mycv.read(tconv)
        B_mycw.read(tconv)
        yield
        if name == "v":
            tsil = p.op("scalar", lambda s: s.activation(vstg[:, :], cva2[:, :], AF.Silu,
                                                         bias=cw[:, 12:13], scale=1.0),
                        waits=[tconv] + B_vstg.wwaits())
            B_mycw.read(tsil)
            B_vstg.wrote(tsil)
            tok = p.dma("gpsimd", vt_d[ct * 128:(ct + 1) * 128, :], vstg[:, :],
                        waits=[tsil], sem="s_vt")
            B_vstg.read(tok)
            if DBG:
                tok = p.dma("gpsimd", dbg_v[ct * 128:(ct + 1) * 128, :], vstg[:, :],
                            waits=[tsil], sem="s_vt")
                B_vstg.read(tok)
            vdma_tok[ct] = tok
            yield
            return
        tsil = p.op("scalar", lambda s: s.activation(sc1[:, :], cva[:, :], AF.Silu,
                                                     bias=cw[:, 4:5], scale=1.0),
                    waits=[tconv] + B_sc1.wwaits())
        B_sc1.wrote(tsil)
        B_mycw.read(tsil)
        store = qk_store[name]
        for tt4 in range(4):
            bank = pXP[tt4 % 2]
            bb = B_pXP[tt4 % 2]
            ltr = None
            for k4 in range(4):
                ttt = tt4 * 4 + k4
                trw = ([tsil] if (tt4 == 0 and k4 == 0) else []) + (bb.wwaits() if k4 == 0 else [])
                ltr = p.op("tensor", lambda t, ttt=ttt, k4=k4, bank=bank: t.transpose(
                    bank[:, k4 * 128:(k4 + 1) * 128],
                    sc1[:, ttt * 128:(ttt + 1) * 128], IDF), waits=trw,
                    inc=(k4 == 3))
            bb.wrote(ltr)
            B_sc1.read(ltr)
            sq = p.op("scalar", lambda s, bank=bank: s.activation(
                sc2[:, 0:512], bank[:, 0:512], AF.Square),
                waits=[ltr] + B_sc2.wwaits())
            B_sc2.wrote(sq)
            rs = p.op("vector", lambda v: v.reduce_sum(
                stat[:, 16:20], sc2[:, 0:512].rearrange("p (a n) -> p a n", a=4), axis=AX.X),
                waits=[sq])
            B_sc2.read(rs)
            vr = p.op("vector", lambda v: v.reciprocal(stat[:, 20:24], stat[:, 16:20]),
                      waits=[rs])
            sr = p.op("scalar", lambda s: s.activation(stat[:, 24:28], stat[:, 20:24], AF.Sqrt),
                      waits=[rs, vr])
            tmn = p.op("vector", lambda v: v.tensor_scalar_min(stat[:, 28:32], stat[:, 24:28], 1e12),
                       waits=[sr])
            nm = None
            for k4 in range(4):
                nm = p.op("vector", lambda v, k4=k4, bank=bank: v.tensor_scalar(
                    sc2[:, 512 + k4 * 64:512 + (k4 + 1) * 64].bitcast(BF16)[:, 0:128],
                    bank[:, k4 * 128:(k4 + 1) * 128],
                    stat[:, 28 + k4:29 + k4], None, ALU.mult),
                    waits=([tmn] if k4 == 0 else []), inc=(k4 == 3))
                bb.read(nm)
            yield
            tb = None
            for k4 in range(4):
                slotp = k4 + (4 if tt4 % 2 else 0)
                tb = p.op("tensor", lambda t, k4=k4, slotp=slotp: t.transpose(
                    pTB[:, slotp * 128:(slotp + 1) * 128],
                    sc2[:, 512 + k4 * 64:512 + (k4 + 1) * 64].bitcast(BF16)[:, 0:128],
                    IDB), waits=([nm] + B_pTB.wwaits()) if k4 == 0 else [],
                    inc=(k4 == 3))
            B_pTB.wrote(tb)
            for k4 in range(4):
                slotp = k4 + (4 if tt4 % 2 else 0)
                ttt = tt4 * 4 + k4
                aw = ((B_ybuf.wwaits() + B_wslab[0].wwaits() + B_wslab[1].wwaits())
                      if (name == "k" and ct == 0 and tt4 == 0 and k4 == 0) else [])
                ev = p.op("scalar", lambda s, slotp=slotp, store=store, ct=ct, ttt=ttt: s.copy(
                    store[:, ct * T + ttt * 128: ct * T + ttt * 128 + 128],
                    pTB[:, slotp * 128:(slotp + 1) * 128]),
                    waits=([tb] if k4 == 0 else []) + aw)
                B_pTB.read(ev)
                B_sc2.read(ev)
            B_sc1.read(ev)
            GD[0] = ev
            yield

    GD = [None]
    vdma_tok = {}
    qk_store = {"q": qhT, "k": khT}
    plan_slots = [("q", list(range(8))), ("k", list(range(8)))]
    qk_seq = [("q", c) for c in range(8)] + [("k", c) for c in range(8)]
    v_seq = [("v", c) for c in range(16)]
    g0 = g1 = None
    i0 = i1 = 0
    while i0 < len(qk_seq) or i1 < len(v_seq) or g0 is not None or g1 is not None:
        if g0 is None and i0 < len(qk_seq):
            nm0, c0 = qk_seq[i0]
            g0 = proj_ct(nm0, c0, 0)
            i0 += 1
        if g1 is None and i1 < len(v_seq):
            nm1, c1 = v_seq[i1]
            g1 = proj_ct(nm1, c1, 1)
            i1 += 1
        for gref in (0, 1):
            g = g0 if gref == 0 else g1
            if g is not None:
                try:
                    next(g)
                except StopIteration:
                    if gref == 0:
                        g0 = None
                    else:
                        g1 = None
    qk_done = GD[0]

    wt = p.dma("gpsimd", wslab3[0][:, 0:16 * 40].rearrange("p (a n) -> p a n", a=16),
               wab_in.rearrange("(a p) n -> p a n", p=128), waits=B_wslab[0].wwaits(), sem="s_ws0")
    B_wslab[0].wrote(wt)
    for tc in range(4):
        mw = [wt, xn_ready] + B_pG.wwaits()
        lm = None
        for ht in range(16):
            lm = p.op("tensor", lambda t, ht=ht, tc=tc: t.matmul(
                pG[0:40, 0:512], wslab3[0][:, ht * 40:(ht + 1) * 40],
                xnT[:, ht * T + tc * 512: ht * T + (tc + 1) * 512],
                start=(ht == 0), stop=(ht == 15)), waits=(mw if ht == 0 else []), inc=(ht == 15))
        B_pG.wrote(lm)
        B_wslab[0].read(lm)
        ev = p.op("scalar", lambda s, tc=tc: s.copy(ab_sb[0:40, tc * 512:(tc + 1) * 512],
                                                    pG[0:40, 0:512]),
                  waits=[lm] + (B_xbuf[0].wwaits() if tc == 0 else []))
        B_pG.read(ev)
    e1 = p.op("scalar", lambda s: s.activation(lab_sb[0:40, :], ab_sb[0:40, :], AF.Exp, scale=-1.0),
              waits=[ev] + B_xbuf[1].wwaits())
    e2 = p.op("vector", lambda v: v.tensor_scalar(lab_sb[0:40, :], lab_sb[0:40, :], 1.0, None, ALU.add),
              waits=[e1])
    e3 = p.op("scalar", lambda s: s.activation(lab_sb[0:8, :], lab_sb[0:8, :], AF.Ln), waits=[e2])
    e4 = p.op("vector", lambda v: v.tensor_scalar(lab_sb[0:8, :], lab_sb[0:8, :], -1.0, None, ALU.mult),
              waits=[e3])
    e5 = p.op("vector", lambda v: v.reciprocal(lab_sb[32:40, :], lab_sb[32:40, :]), waits=[e3])
    prev_dc = None
    for ci in range(NCHUNK):
        tr = p.op("tensor", lambda t, ci=ci: t.transpose(
            PTF[:, 0:40], lab_sb[0:40, ci * 128:(ci + 1) * 128], IDF[0:40, 0:40]),
            waits=[e4, e5] + B_pTB.wwaits())
        B_pTB.wrote(tr)
        evt = p.op("scalar", lambda s: s.copy(abT_sb[:, 0:40], PTF[:, 0:40]), waits=[tr])
        B_pTB.read(evt)
        btk = p.op("vector", lambda v, ci=ci: v.tensor_copy(Btarr[:, ci * NH:(ci + 1) * NH],
                                                            abT_sb[:, 32:40]), waits=[evt])
        nb = p.op("vector", lambda v, ci=ci: v.tensor_scalar(
            nBarr[:, ci * NH:(ci + 1) * NH], abT_sb[:, 32:40], -1.0, None, ALU.mult))
        tm = p.op("tensor", lambda t: t.matmul(pG[:, 0:8], TRI, abT_sb[:, 0:8],
                                               start=True, stop=True),
                  waits=[evt] + B_pG.wwaits())
        B_pG.wrote(tm)
        evl = p.op("scalar", lambda s, ci=ci: s.copy(Larr[:, ci * NH:(ci + 1) * NH],
                                                     pG[:, 0:8]), waits=[tm])
        B_pG.read(evl)
        nl = p.op("vector", lambda v, ci=ci: v.tensor_scalar(
            nLarr[:, ci * NH:(ci + 1) * NH], Larr[:, ci * NH:(ci + 1) * NH], -1.0, None, ALU.mult),
            waits=[evl])
        gm = p.op("scalar", lambda s, ci=ci: s.activation(Garr[:, ci * NH:(ci + 1) * NH],
                                                          Larr[:, ci * NH:(ci + 1) * NH], AF.Exp),
                  waits=[evl])
        p.op("vector", lambda v, ci=ci: v.tensor_tensor(
            BGarr[:, ci * NH:(ci + 1) * NH], Garr[:, ci * NH:(ci + 1) * NH],
            Btarr[:, ci * NH:(ci + 1) * NH], ALU.mult), waits=[gm, btk], inc=False)
        prev_dc = gm if nl is None else nl
        prev_dc = gm
    ab_ready = prev_dc

    zS = p.op("vector", lambda v: v.memset(Sm[:, :], 0.0), waits=[ab_ready])
    zS2 = p.op("vector", lambda v: v.memset(Sbb[:, :], 0.0))
    blk_done = {}
    rec_guard = {}
    for ci in range(NCHUNK):
        b = ci % 2
        vw = (([vdma_tok[vt] for vt in range(16)] + [ab_ready] + B_cv.wwaits())
              if ci == 0 else []) + B_vbuf[b].wwaits()
        vt_ld = p.dma("sync", vbuf[b][:, :].rearrange("p (a n) -> p a n", a=16),
                      vt_d.rearrange("(a p) n -> p a n", p=128)[:, :, ci * 128:(ci + 1) * 128],
                      waits=vw, sem=f"s_vl{b}")
        B_vbuf[b].wrote(vt_ld)
        ev = None
        for bh in range(2):
            trv = None
            for vt in range(8 * bh, 8 * bh + 8):
                slot = vt % 8
                tw = ([vt_ld, zS2, qk_done] + B_pTB.wwaits() + B_vtok.wwaits()) if vt == 0 else \
                     (B_pTB.wwaits() if vt == 8 else [])
                trv = p.op("tensor", lambda t, vt=vt, slot=slot, b=b: t.transpose(
                    pTB[:, slot * 128:(slot + 1) * 128], vbuf[b][:, vt * 128:(vt + 1) * 128], IDB),
                    waits=tw, inc=(slot == 7))
            B_pTB.wrote(trv)
            for vt in range(8 * bh, 8 * bh + 8):
                slot = vt % 8
                ev = p.op("scalar", lambda s, vt=vt, slot=slot: s.copy(
                    vtok[:, vt * 128:(vt + 1) * 128], pTB[:, slot * 128:(slot + 1) * 128]),
                    waits=([trv] if slot == 0 else []))
                B_pTB.read(ev)
        B_vbuf[b].read(ev)
        B_vtok.wrote(ev)
        vtok_ready = ev
        trk = None
        for h in range(NH):
            tw = (B_pTB.wwaits() + B_ktok.wwaits()) if h == 0 else []
            trk = p.op("tensor", lambda t, h=h, ci=ci: t.transpose(
                pTB[:, h * 128:(h + 1) * 128], khT[:, h * T + ci * 128: h * T + ci * 128 + 128],
                IDB), waits=tw, inc=(h == NH - 1))
        B_pTB.wrote(trk)
        ev = None
        for h in range(NH):
            ev = p.op("scalar", lambda s, h=h: s.copy(
                ktok[:, h * 128:(h + 1) * 128], pTB[:, h * 128:(h + 1) * 128]),
                waits=([trk] if h == 0 else []))
            B_pTB.read(ev)
        B_ktok.wrote(ev)
        ktok_ready = ev
        def rec_head(ci, h):
            par = h % 2
            cs = ci * NH + h
            Lc = Larr[:, cs:cs + 1]
            nLc = nLarr[:, cs:cs + 1]
            nBc = nBarr[:, cs:cs + 1]
            Bc = Btarr[:, cs:cs + 1]
            Gc = Garr[:, cs:cs + 1]
            BGc = BGarr[:, cs:cs + 1]
            dCc = dCarr[:, cs:cs + 1]
            gCc = gCarr[:, cs:cs + 1]
            kT = khT[:, h * T + ci * 128: h * T + ci * 128 + 128]
            qT = qhT[:, h * T + ci * 128: h * T + ci * 128 + 128]
            trL = p.op("tensor", lambda t, cs=cs: t.transpose(
                PTF[0:1, 384:512], Larr[:, cs:cs + 1], IDF), waits=B_pTB.wwaits())
            B_pTB.wrote(trL)
            evL = p.op("scalar", lambda s, par=par: s.copy(LrowS[par][0:1, :], PTF[0:1, 384:512]),
                       waits=[trL])
            B_pTB.read(evL)
            nLr = p.op("scalar", lambda s, par=par: s.mul(nLrowS[par][0:1, :], LrowS[par][0:1, :],
                                                          -1.0), waits=[evL])
            m1 = p.op("tensor", lambda t, par=par: t.matmul(
                pG[:, par * 2:par * 2 + 1], ONER, LrowS[par][0:1, 127:128], start=True, stop=True),
                waits=[evL] + B_pG.wwaits())
            B_pG.wrote(m1)
            gc = p.op("scalar", lambda s, cs=cs, par=par: s.activation(
                gCarr[:, cs:cs + 1], pG[:, par * 2:par * 2 + 1], AF.Exp), waits=[m1])
            lcb = p.op("scalar", lambda s, par=par: s.copy(LCbs[par][:, 0:1], pG[:, par * 2:par * 2 + 1]),
                       waits=[m1])
            B_pG.read(gc)
            B_pG.read(lcb)
            dc = p.op("scalar", lambda s, cs=cs, par=par: s.activation(
                dCarr[:, cs:cs + 1], Larr[:, cs:cs + 1], AF.Exp, bias=LCbs[par][:, 0:1],
                scale=-1.0), waits=[lcb])
            yield
            f1 = p.op("tensor", lambda t, kT=kT, par=par: t.matmul(
                pF[par][:, 0:128], kT, kT, start=True, stop=True),
                waits=B_pF[par].wwaits() + ([ktok_ready, vtok_ready] if h < 2 else []))
            p.op("tensor", lambda t, kT=kT, qT=qT, par=par: t.matmul(
                pF[par][:, 128:256], kT, qT, start=True, stop=True), inc=False)
            f3 = p.op("tensor", lambda t, kT=kT, par=par, h=h: t.matmul(
                pF[par][:, 256:512], kT, Sbb[:, h * 256:(h + 1) * 256], start=True, stop=True))
            B_pF[par].wrote(f3)
            yield
            d1 = p.op("tensor", lambda t, par=par: t.matmul(
                pDW[par][:, 0:128], ONER, nLrowS[par][0:1, :],
                start=True, stop=True), waits=[nLr] + B_pDW[par].wwaits())
            d2 = p.op("tensor", lambda t, par=par: t.matmul(
                pDW[par][:, 0:128], IDF, MSK_S, start=False, stop=True))
            dex = p.op("scalar", lambda s, par=par, Lc=Lc: s.activation(
                Dsb[par][:, :], pDW[par][:, 0:128], AF.Exp, bias=Lc, scale=1.0), waits=[d2])
            nbu = p.op("vector", lambda v, par=par, nBc=nBc: v.scalar_tensor_tensor(
                Nsb[par][:, :], pF[par][:, 0:128], nBc, Dsb[par][:, :], ALU.mult, ALU.mult),
                waits=[dex, f3])
            B_pF[par].read(nbu)
            yield
            d3 = p.op("tensor", lambda t, par=par: t.matmul(
                pDW[par][:, 128:256], ONER, LrowS[par][0:1, :],
                start=True, stop=True), waits=[dex, evL])
            d4 = p.op("tensor", lambda t, par=par: t.matmul(
                pDW[par][:, 128:256], IDF, MSK_IT, start=False, stop=True))
            wex = p.op("scalar", lambda s, par=par, nLc=nLc: s.activation(
                Dsb[par][:, :], pDW[par][:, 128:256], AF.Exp, bias=nLc, scale=1.0), waits=[d4])
            wbu = p.op("vector", lambda v, par=par: v.tensor_tensor(
                Wsb[par][:, :], pF[par][:, 128:256], Dsb[par][:, :], ALU.mult), waits=[wex])
            B_pF[par].read(wbu)
            B_pDW[par].wrote(d4)
            B_pDW[par].read(wex)
            yield
            nt1 = p.op("tensor", lambda t, par=par: t.transpose(
                pTB[:, 0:128], Nsb[par][:, :], IDB), waits=[nbu] + B_pTB.wwaits())
            B_pTB.wrote(nt1)
            nt2 = p.op("scalar", lambda s, par=par: s.copy(NTsb[par][:, :], pTB[:, 0:128]),
                       waits=[nt1])
            B_pTB.read(nt2)
            pv = p.op("vector", lambda v, par=par: v.tensor_tensor(
                PTsb[par][:, 0:128], NTsb[par][:, :], IDB, ALU.add), waits=[nt2])
            yield
            xsrc = {1: (Nsb[par][:, :], NTsb[par][:, :])}
            prev_ev = None
            for li, j in enumerate([1, 2, 4, 8, 16]):
                Xj, XTj = xsrc[j]
                m = p.op("tensor", lambda t, Xj=Xj, XTj=XTj, par=par: t.matmul(
                    pXP[par][:, 0:128], XTj, Xj, start=True, stop=True),
                    waits=([nt2] if li == 0 else [prev_ev]) + (B_pXP[par].wwaits() if li == 0 else []),
                    inc=False)
                m2 = p.op("tensor", lambda t, Xj=Xj, XTj=XTj, par=par: t.matmul(
                    pXP[par][:, 128:256], Xj, XTj, start=True, stop=True))
                prev_ev = p.op("vector", lambda v, par=par, li=li: v.tensor_copy(
                    Xsb[par][:, li * 256:(li + 1) * 256], pXP[par][:, 0:256]), waits=[m2])
                xsrc[2 * j] = (Xsb[par][:, li * 256:li * 256 + 128],
                               Xsb[par][:, li * 256 + 128:(li + 1) * 256])
                yield
            X32, XT32 = xsrc[32]
            m = p.op("tensor", lambda t, X32=X32, XT32=XT32, par=par: t.matmul(
                pXP[par][:, 0:128], XT32, X32, start=True, stop=True), waits=[prev_ev])
            xev = p.op("vector", lambda v, par=par: v.tensor_copy(
                Xsb[par][:, 5 * 256:5 * 256 + 128], pXP[par][:, 0:128]), waits=[m])
            xsrc[64] = (Xsb[par][:, 5 * 256:5 * 256 + 128], None)
            yield
            pp = 0
            for li, j in enumerate([2, 4, 8, 16, 32, 64]):
                src = PTsb[par][:, pp * 128:(pp + 1) * 128]
                dstp = 1 - pp
                Xj = Xsb[par][:, (li) * 256:(li) * 256 + 128] if j != 64 else xsrc[64][0]
                m = p.op("tensor", lambda t, par=par, Xj=Xj, src=src: t.matmul(
                    pXP[par][:, 256:384], Xj, src, start=True, stop=True),
                    waits=[pv] + ([xev] if li == 0 else []))
                pv = p.op("vector", lambda v, par=par, dstp=dstp, src=src: v.scalar_tensor_tensor(
                    PTsb[par][:, dstp * 128:(dstp + 1) * 128], pXP[par][:, 256:384], 1.0, src,
                    ALU.mult, ALU.add), waits=[m])
                pp = dstp
                yield
            B_pXP[par].wrote(m)
            B_pXP[par].read(pv)
            MT = PTsb[par][:, pp * 128:(pp + 1) * 128]
            t1 = p.op("vector", lambda v, par=par, BGc=BGc: v.tensor_scalar(
                T1sb[par][:, :], pF[par][:, 256:512], BGc, None, ALU.mult), waits=[f3])
            B_pF[par].read(t1)
            rb = p.op("vector", lambda v, par=par, h=h, Bc=Bc: v.scalar_tensor_tensor(
                Rsb[par][:, :], vtok[:, h * 256:(h + 1) * 256], Bc, T1sb[par][:, :],
                ALU.mult, ALU.subtract), waits=[t1, vtok_ready])
            B_vtok.read(rb)
            yield
            mu = p.op("tensor", lambda t, MT=MT, par=par: t.matmul(
                pDW[par][:, 256:512], MT, Rsb[par][:, :], start=True, stop=True),
                waits=[rb, pv, wex] + B_pDW[par].wwaits())
            B_pDW[par].wrote(mu)
            ue = p.op("vector", lambda v, par=par: v.tensor_copy(Usb[par][:, :],
                                                                 pDW[par][:, 256:512]), waits=[mu])
            B_pDW[par].read(ue)
            yield
            o1 = p.op("tensor", lambda t, qT=qT, par=par, h=h: t.matmul(
                pF[par][:, 256:512], qT, Sbb[:, h * 256:(h + 1) * 256], start=True, stop=True),
                waits=[t1, rb, nbu, wbu])
            o2 = p.op("tensor", lambda t, par=par: t.matmul(
                pF[par][:, 256:512], Wsb[par][:, :], Usb[par][:, :], start=False, stop=True),
                waits=[ue])
            B_pF[par].wrote(o2)
            oev = p.op("vector", lambda v, par=par, h=h, Gc=Gc: v.tensor_scalar(
                otk[:, h * 256:(h + 1) * 256], pF[par][:, 256:512], Gc, None, ALU.mult),
                waits=[o2] + (B_otk.wwaits() if h < 2 else []))
            B_pF[par].read(oev)
            rec_guard[("oev", ci, h)] = oev
            yield
            du = p.op("vector", lambda v, par=par, dCc=dCc: v.tensor_scalar(
                dUsb[par][:, :], Usb[par][:, :], dCc, None, ALU.mult), waits=[ue, dc])
            ms = p.op("tensor", lambda t, par=par, h=h: t.matmul(
                pDW[par][:, 256:512], ktok[:, h * 128:(h + 1) * 128], dUsb[par][:, :],
                start=True, stop=True), waits=[du, ue, ktok_ready] + B_pDW[par].wwaits())
            B_pDW[par].wrote(ms)
            B_ktok.read(ms)
            su = p.op("vector", lambda v, par=par, h=h, gCc=gCc: v.scalar_tensor_tensor(
                Sm[:, h * 256:(h + 1) * 256], Sm[:, h * 256:(h + 1) * 256], gCc,
                pDW[par][:, 256:512], ALU.mult, ALU.add), waits=[ms] + ([zS] if ci == 0 else []))
            B_pDW[par].read(su)
            sb2 = p.op("vector", lambda v, h=h: v.tensor_copy(Sbb[:, h * 256:(h + 1) * 256],
                                                              Sm[:, h * 256:(h + 1) * 256]),
                       waits=[su])
            if h >= NH - 2:
                rec_guard[("otk_done", ci, h)] = oev

        for hp in range(NH // 2):
            gens = [rec_head(ci, 2 * hp), rec_head(ci, 2 * hp + 1)]
            alive = [True, True]
            while any(alive):
                for gi in range(2):
                    if alive[gi]:
                        try:
                            next(gens[gi])
                        except StopIteration:
                            alive[gi] = False
        oev = rec_guard[("oev", ci, NH - 1)]
        B_otk.wrote(oev)
        evo = None
        for bh in range(2):
            tro = None
            for vt in range(8 * bh, 8 * bh + 8):
                slot = vt % 8
                tw = (B_pTB.wwaits() if slot == 0 else [])
                if vt == 0:
                    tw = tw + [oev]
                tro = p.op("tensor", lambda t, vt=vt, slot=slot: t.transpose(
                    pTB[:, slot * 128:(slot + 1) * 128], otk[:, vt * 128:(vt + 1) * 128], IDB),
                    waits=tw, inc=(slot == 7))
            B_pTB.wrote(tro)
            for vt in range(8 * bh, 8 * bh + 8):
                slot = vt % 8
                evo = p.op("scalar", lambda s, vt=vt, slot=slot, ci=ci: s.copy(
                    oTa[:, vt * T + ci * 128: vt * T + ci * 128 + 128],
                    pTB[:, slot * 128:(slot + 1) * 128]), waits=([tro] if slot == 0 else []))
                B_pTB.read(evo)
        B_otk.read(evo)
        if ci % 4 == 3:
            blk_done[ci // 4] = evo

    if DBG:
        dq_tok = p.dma("gpsimd", dbg_q, qhT[:, :], waits=[blk_done[3]], sem="s_st")
        dk_tok = p.dma("gpsimd", dbg_k, khT[:, :], waits=[blk_done[3]], sem="s_st")
        do_tok = p.dma("gpsimd", dbg_o, oTa[:, :], waits=[blk_done[3]], sem="s_st")
        B_ybuf.read(dk_tok)
        dl_tok = p.dma("gpsimd", dbg_L.rearrange("p (a n) -> p a n", a=16)[:, 0, :], Larr,
                       waits=[blk_done[3]], sem="s_st")
        for nmi, arr in enumerate([nLarr, nBarr, Btarr, Garr, BGarr, dCarr, gCarr]):
            dl_tok = p.dma("gpsimd", dbg_L.rearrange("p (a n) -> p a n", a=16)[:, nmi + 1, :], arr,
                           waits=[blk_done[3]], sem="s_st")
    at_ar = []
    B_ws5f = [Buf(), Buf()]
    for blk in range(4):
        oready = blk_done[blk]
        for woct in range(16):
            b = woct % 2
            alias_w = (B_vbuf[0].wwaits() + B_vbuf[1].wwaits() + B_vtok.wwaits()
                       ) if (blk == 0 and woct < 2) else []
            wf = p.dma("sync", ws5f[b][:, :].rearrange("p (a n) -> p a n", a=16),
                       wo_in.rearrange("(a p) n -> p a n", p=128)[:, :, woct * 128:(woct + 1) * 128],
                       waits=B_ws5f[b].wwaits() + alias_w
                             + ([blk_done[3]] if (blk == 0 and woct < 2) else []),
                       sem=f"s_wf{b}")
            B_ws5f[b].wrote(wf)
            wt = p.op("scalar", lambda sc, b=b: sc.copy(wslab5[b][:, :], ws5f[b][:, :]),
                      waits=[wf] + B_wslab[b].wwaits())
            B_ws5f[b].read(wt)
            B_wslab[b].wrote(wt)
            bank = pG if woct % 2 == 0 else pF[0]
            bb = B_pG if woct % 2 == 0 else B_pF[0]
            mw = [wt, oready] + bb.wwaits()
            lm = None
            for vt in range(16):
                lm = p.op("tensor", lambda t, bank=bank, b=b, vt=vt, blk=blk: t.matmul(
                    bank[:, 0:512],
                    wslab5[b][:, vt * 128:(vt + 1) * 128],
                    oTa[:, vt * T + blk * 512: vt * T + (blk + 1) * 512],
                    start=(vt == 0), stop=(vt == 15)), waits=(mw if vt == 0 else []),
                    inc=(vt == 15))
            bb.wrote(lm)
            B_wslab[b].read(lm)
            ev = p.op("vector", lambda v, bank=bank, woct=woct: v.tensor_copy(
                ybuf[:, woct * 512:(woct + 1) * 512], bank[:, 0:512]),
                waits=[lm] + (B_ybuf.wwaits() if woct == 0 else []))
            bb.read(ev)
        B_ybuf.wrote(ev)
        ad = p.dma("gpsimd", cca_in[blk].rearrange("(p a) n -> p a n", p=128),
                   ybuf[:, :].rearrange("p (a n) -> p a n", a=4), waits=[ev], sem="s_st")
        B_ybuf.read(ad)
        ar = p.coll(lambda g, blk=blk: g.collective_compute(
            "AllReduce", ALU.add, replica_groups=GROUPS, ins=[cca_in[blk]], outs=[cca_out[blk]]),
            waits=[ad])
        at_ar.append(ar)
    fin = []
    B_outb = Buf()
    for blk in range(4):
        ld = p.dma("sync", ybuf[:, :].rearrange("p (a n) -> p a n", a=4),
                   cca_out[blk].rearrange("(p a) n -> p a n", p=128),
                   waits=[at_ar[blk], spill_done] + B_ybuf.wwaits(), sem="s_yo")
        B_ybuf.wrote(ld)
        if DBG:
            dat = p.dma("gpsimd", dbg_att[blk * 512:(blk + 1) * 512, :].rearrange(
                "(a p) n -> p a n", p=128), ybuf[:, :].rearrange("p (a n) -> p a n", a=4),
                waits=[ld], sem="s_st")
            B_ybuf.read(dat)
            fin.append(dat)
        for q in range(4):
            tt = blk * 4 + q
            bx = tt % 2
            sw = p.dma("sync", xbuf[bx][:, :], spill_d[tt * 128:(tt + 1) * 128, :],
                       waits=B_xbuf[bx].wwaits() + ([at_ar[0]] if tt < 2 else []), sem=f"s_x{bx}")
            B_xbuf[bx].wrote(sw)
            fa = None
            for half in range(2):
                ttr = None
                for k in range(8):
                    woct = half * 8 + k
                    tw = ([ld] + B_pTB.wwaits() + (B_outb.wwaits() if half == 0 else [])
                          ) if k == 0 else []
                    ttr = p.op("tensor", lambda t, woct=woct, k=k, q=q: t.transpose(
                        pTB[:, k * 128:(k + 1) * 128],
                        ybuf[:, woct * 512 + q * 128: woct * 512 + q * 128 + 128], IDB),
                        waits=tw, inc=(k == 7))
                B_pTB.wrote(ttr)
                B_ybuf.read(ttr)
                for k in range(8):
                    woct = half * 8 + k
                    fa = p.op("vector", lambda v, woct=woct, k=k, bx=bx: v.tensor_tensor(
                        outb[:, woct * 128:(woct + 1) * 128], pTB[:, k * 128:(k + 1) * 128],
                        xbuf[bx][:, woct * 128:(woct + 1) * 128], ALU.add),
                        waits=([ttr, sw] if k == 0 else []), inc=(k == 7))
                    B_pTB.read(fa)
            B_outb.wrote(fa)
            B_xbuf[bx].read(fa)
            tok = p.dma("gpsimd", y_out[tt * 128:(tt + 1) * 128, :], outb[:, :],
                        waits=[fa], sem="s_st")
            fin.append(tok)
            B_outb.read(tok)
            B_xbuf[bx].read(tok)
    if DBG:
        fin.append(p.dma("gpsimd", dbg_xn, xnT[:, :], waits=[xn_ready, blk_done[3]], sem="s_st"))


    p.op("gpsimd", lambda g: g.memset(stat[0:1, 40:41], 0.0), waits=fin, inc=False)

    with nc.Block() as block:
        p.assemble(block)
    ctx.close()
    return nc


_NC_CACHE = [None]


def _host_prep(inputs, core):
    role = core % 2
    batch = core // 2
    f32 = np.float32
    x = np.asarray(inputs["L0_attn_x"][batch])
    d = {}
    d["x_pi"] = np.ascontiguousarray(x[:, PIH], dtype=f32)
    cs = slice(role * NH * DK, (role + 1) * NH * DK)
    vs = slice(role * NH * VE, (role + 1) * NH * VE)
    d["wq"] = np.ascontiguousarray(np.asarray(inputs["L0_attn_Wq"])[PIH][:, cs], f32)
    d["wk"] = np.ascontiguousarray(np.asarray(inputs["L0_attn_Wk"])[PIH][:, cs], f32)
    d["wv"] = np.ascontiguousarray(np.asarray(inputs["L0_attn_Wv"])[PIH][:, vs], f32)
    wa = np.asarray(inputs["L0_attn_Wa"])[PIH][:, 8 * role:8 * role + 8]
    wb = np.asarray(inputs["L0_attn_Wb"])[PIH][:, 8 * role:8 * role + 8]
    wab = np.zeros((H, 40), f32)
    wab[:, 0:8] = wa
    wab[:, 32:40] = wb
    d["wab"] = wab
    wo = np.asarray(inputs["L0_attn_Wo"])[:, PIH]
    d["wo"] = np.ascontiguousarray(wo[role * 2048:(role + 1) * 2048], f32)
    for nm, wkey, bkey, sl in (("qconv", "L0_attn_qcw", "L0_attn_qcb", cs),
                               ("kconv", "L0_attn_kcw", "L0_attn_kcb", cs),
                               ("vconv", "L0_attn_vcw", "L0_attn_vcb", vs)):
        cw_ = np.asarray(inputs[wkey])[sl]
        cb_ = np.asarray(inputs[bkey])[sl]
        arr = np.zeros((cw_.shape[0], 8), f32)
        arr[:, 0:4] = cw_
        arr[:, 4] = cb_
        d[nm] = arr
    idx = _pi_i(role)
    ok = idx >= 0
    gup = np.asarray(inputs["L0_ffn_gate_up_packed"])
    gate_rows = np.zeros((I_PAD, gup.shape[1]), np.int32)
    gate_rows[ok] = gup[idx[ok]]
    up_rows = np.zeros((I_PAD, gup.shape[1]), np.int32)
    up_rows[ok] = gup[idx[ok] + I_FULL]
    rows = np.concatenate([gate_rows, up_rows], axis=0)
    tr = np.ascontiguousarray(rows.T)
    hw = tr.view(np.int16).reshape(gup.shape[1], 2 * I_PAD, 2)
    d["gupT_hw0"] = np.ascontiguousarray(hw[:, :, 0])
    d["gupT_hw1"] = np.ascontiguousarray(hw[:, :, 1])
    gsc = np.asarray(inputs["L0_ffn_gate_up_scales"]).astype(np.float16)
    sc_cols = np.zeros((16, 2 * I_PAD), np.float16)
    sc_cols[:, 0:I_PAD][:, ok] = gsc[:, idx[ok]]
    sc_cols[:, I_PAD:][:, ok] = gsc[:, idx[ok] + I_FULL]
    hgrp = (8 * (np.arange(H) % 256) + (np.arange(H) // 256)) // 128
    d["gup_scale"] = np.ascontiguousarray(sc_cols[hgrp])
    dp = np.asarray(inputs["L0_ffn_down_packed"])
    dpc = dp[:, 352 * role:352 * role + 352]
    dpt = np.zeros((384, H), np.int32)
    dpt[0:352] = np.ascontiguousarray(dpc.T)[:, PIH]
    hwd = np.ascontiguousarray(dpt).view(np.int16).reshape(384, H, 2)
    d["dwnT_hw0"] = np.ascontiguousarray(hwd[:, :, 0])
    d["dwnT_hw1"] = np.ascontiguousarray(hwd[:, :, 1])
    dsc = np.asarray(inputs["L0_ffn_down_scales"]).astype(np.float16)
    dsc_rows = np.zeros((I_PAD, H), np.float16)
    dsc_rows[ok] = dsc[(idx[ok] // 128)][:, PIH]
    d["dwn_scale"] = np.ascontiguousarray(dsc_rows)
    cf = np.zeros((128, 640), f32)
    cf[:, 0:128] = np.eye(128)
    cf[:, 128:256] = (np.arange(128)[:, None] <= np.arange(128)[None, :])
    cf[:, 256:384] = np.where(np.arange(128)[None, :] < np.arange(128)[:, None], 0.0, NEG_BIG)
    cf[:, 384:512] = np.where(np.arange(128)[:, None] <= np.arange(128)[None, :], 0.0, NEG_BIG)
    cf[0, 512:640] = 1.0
    d["consts_f"] = cf
    d["consts_bf"] = np.eye(128).astype(BF)
    return d


def kernel(**inputs):
    if _NC_CACHE[0] is None:
        _NC_CACHE[0] = build_program()
    nc = _NC_CACHE[0]
    in_maps = [_host_prep(inputs, core) for core in range(8)]
    res = run_bass_kernel_spmd(nc, in_maps, core_ids=list(range(8)))
    out = np.zeros((4, T, H), np.float32)
    inv = np.empty(H, dtype=np.int64)
    inv[PIH] = np.arange(H)
    for core in range(8):
        role, batch = core % 2, core // 2
        y = res.results[core]["y2"]
        out[batch, role * THALF:(role + 1) * THALF, :] = \
            y[role * THALF:(role + 1) * THALF][:, inv]
    return out


# version 17
# speedup vs baseline: 1.0275x; 1.0275x over previous
import numpy as np
import ml_dtypes
import concourse.bass as bass
import concourse.mybir as mybir
from concourse.bass_utils import run_bass_kernel_spmd

F32 = mybir.dt.float32
BF16 = mybir.dt.bfloat16
F16 = mybir.dt.float16
I16 = mybir.dt.int16
AF = mybir.ActivationFunctionType
ALU = mybir.AluOpType
AX = mybir.AxisListType
BF = ml_dtypes.bfloat16

H = 2048
T = 2048
DK = 128
VE = 256
NH = 8
I_FULL = 5632
I_PAD = 3072
C = 128
NCHUNK = T // C
THALF = T // 2
NEG_BIG = -1.0e30
GROUPS = [[0, 1], [2, 3], [4, 5], [6, 7]]

PIH = np.array([8 * j + cls for cls in range(8) for j in range(H // 8)], dtype=np.int64)
_DEBUG_OUT = [False]


def _pi_i(role):
    idx = np.full(I_PAD, -1, dtype=np.int64)
    k = 0
    for hw in range(2):
        for c2 in range(4):
            cls = 4 * hw + c2
            for j in range(352 * role, 352 * role + 352):
                idx[k] = 8 * j + cls
                k += 1
            k += 32
    return idx


class P:

    ESEM = {"tensor": "s_pe", "vector": "s_ve", "scalar": "s_se", "gpsimd": "s_ge",
            "sync": "s_sy"}

    def __init__(self, sems):
        self.sems = sems
        self.cnt = {n: 0 for n in sems}
        self.plan = {e: [] for e in ("sync", "tensor", "vector", "scalar", "gpsimd")}

    def _emit(self, engine, fn, waits, inc, inc_n):
        ws = []
        seen = {}
        for t in waits:
            if t is None:
                continue
            s, v = t
            if seen.get(s, -1) < v:
                seen[s] = v
        ws = tuple(seen.items())
        self.plan[engine].append((fn, ws, inc, inc_n))
        if inc is not None:
            self.cnt[inc] += inc_n
            return (inc, self.cnt[inc])
        return None

    def dma(self, engine, dst, src, waits=(), sem=None):
        assert sem is not None
        return self._emit(engine, lambda e, dst=dst, src=src: e.dma_start(out=dst, in_=src),
                          waits=waits, inc=sem, inc_n=16)

    def op(self, engine, fn, waits=(), inc=True):
        sem = self.ESEM[engine] if inc else None
        return self._emit(engine, fn, waits=waits, inc=sem, inc_n=1)

    def coll(self, fn, waits=()):
        return self._emit("gpsimd", fn, waits=waits, inc="s_coll", inc_n=1)

    def assemble(self, block):
        for engine in self.plan:
            def make(engine):
                def body(eng):
                    for fn, waits, inc, inc_n in self.plan[engine]:
                        for (s, v) in waits:
                            eng.wait_ge(self.sems[s], v)
                        ins = fn(eng)
                        if inc is not None:
                            ins.then_inc(self.sems[inc], inc_n)
                return body
            getattr(block, engine)(make(engine))


class Buf:

    def __init__(self):
        self.writer = None
        self.readers = []

    def wwaits(self):
        w = list(self.readers)
        if self.writer is not None:
            w.append(self.writer)
        return w

    def wrote(self, tok):
        self.writer = tok
        self.readers = []

    def rwaits(self):
        return [self.writer] if self.writer is not None else []

    def read(self, tok):
        if tok is not None:
            self.readers.append(tok)


_DEFER = [True]


def build_program():
    nc = bass.Bass("TRN2", target_bir_lowering=False, debug=False)
    dt = nc.dram_tensor

    x_in = dt("x_pi", [T, H], F32, kind="ExternalInput").ap()
    wq_in = dt("wq", [H, NH * DK], F32, kind="ExternalInput").ap()
    wk_in = dt("wk", [H, NH * DK], F32, kind="ExternalInput").ap()
    wv_in = dt("wv", [H, NH * VE], F32, kind="ExternalInput").ap()
    wab_in = dt("wab", [H, 40], F32, kind="ExternalInput").ap()
    wo_in = dt("wo", [2048, H], F32, kind="ExternalInput").ap()
    qc_in = dt("qconv", [NH * DK, 8], F32, kind="ExternalInput").ap()
    kc_in = dt("kconv", [NH * DK, 8], F32, kind="ExternalInput").ap()
    vc_in = dt("vconv", [NH * VE, 8], F32, kind="ExternalInput").ap()
    gup0_in = dt("gupT_hw0", [H // 8, 2 * I_PAD], I16, kind="ExternalInput").ap()
    gup1_in = dt("gupT_hw1", [H // 8, 2 * I_PAD], I16, kind="ExternalInput").ap()
    gsc_in = dt("gup_scale", [H, 2 * I_PAD], F16, kind="ExternalInput").ap()
    dwn0_in = dt("dwnT_hw0", [384, H], I16, kind="ExternalInput").ap()
    dwn1_in = dt("dwnT_hw1", [384, H], I16, kind="ExternalInput").ap()
    dsc_in = dt("dwn_scale", [I_PAD, H], F16, kind="ExternalInput").ap()
    cst_in = dt("consts_f", [128, 640], F32, kind="ExternalInput").ap()
    cbf_in = dt("consts_bf", [128, 128], BF16, kind="ExternalInput").ap()

    y_out = dt("y2", [T, H], F32, kind="ExternalOutput").ap()
    DBG = bool(globals().get("_DEBUG_OUT", [False])[0]) if False else _DEBUG_OUT[0]
    if DBG:
        dbg_xn = dt("dbg_xn", [128, 16 * T], BF16, kind="ExternalOutput").ap()
        dbg_q = dt("dbg_q", [128, NH * T], BF16, kind="ExternalOutput").ap()
        dbg_k = dt("dbg_k", [128, NH * T], BF16, kind="ExternalOutput").ap()
        dbg_v = dt("dbg_v", [NH * VE, T], BF16, kind="ExternalOutput").ap()
        dbg_sp = dt("dbg_sp", [T, H], F32, kind="ExternalOutput").ap()
        dbg_o = dt("dbg_o", [128, 16 * T], BF16, kind="ExternalOutput").ap()
        dbg_L = dt("dbg_L", [128, 16 * NCHUNK * NH], F32, kind="ExternalOutput").ap()
        dbg_att = dt("dbg_att", [T, H], BF16, kind="ExternalOutput").ap()
        dbg_xn2 = dt("dbg_xn2", [T, H], BF16, kind="ExternalOutput").ap()
        dbg_z = dt("dbg_z", [128, 2052], F32, kind="ExternalOutput").ap()
        dbg_h = dt("dbg_h", [128, 24 * 512], BF16, kind="ExternalOutput").ap()
        dbg_w = dt("dbg_w", [128, 24 * 512], BF16, kind="ExternalOutput").ap()

    wgu_d = dt("wgu_d", [H, 2 * I_PAD], BF16, kind="Internal").ap()
    dwn_d = dt("dwn_d", [I_PAD, H], BF16, kind="Internal").ap()
    vt_d = dt("vt_d", [NH * VE, T], BF16, kind="Internal").ap()
    spill_d = dt("spill_d", [T, H], F32, kind="Internal").ap()
    ccy_in = [dt(f"ccy_in{i}", [512, H], BF16, kind="Internal").ap() for i in range(4)]
    ccy_out = [dt(f"ccy_out{i}", [512, H], BF16, kind="Internal").ap() for i in range(4)]
    cca_in = [dt(f"cca_in{i}", [512, H], BF16, kind="Internal").ap() for i in range(4)]
    cca_out = [dt(f"cca_out{i}", [512, H], BF16, kind="Internal").ap() for i in range(4)]

    sbt = nc.sbuf_tensor
    pst = nc.psum_tensor
    import contextlib
    ctx = contextlib.ExitStack()

    cstf = ctx.enter_context(sbt("cstf", [128, 640], F32))
    cstb = ctx.enter_context(sbt("cstb", [128, 128], BF16))
    IDF = cstf[:, 0:128]
    TRI = cstf[:, 128:256]
    MSK_S = cstf[:, 256:384]
    MSK_IT = cstf[:, 384:512]
    ONER = cstf[0:1, 512:640]
    IDB = cstb[:, :]
    stat = ctx.enter_context(sbt("stat", [128, 64], F32))
    cw = ctx.enter_context(sbt("cw", [128, 16], F32))

    U8 = mybir.dt.uint8
    A_ar = ctx.enter_context(sbt("A_ar", [128, 65536], U8))
    B_ar = ctx.enter_context(sbt("B_ar", [128, 32768], U8))
    C_ar = ctx.enter_context(sbt("C_ar", [128, 32768], U8))
    D_ar = ctx.enter_context(sbt("D_ar", [128, 78336], U8))

    def cv(ar, off, nel, dtp):
        sz = {F32: 4, BF16: 2, F16: 2, I16: 2}[dtp]
        return ar[:, off:off + nel * sz].bitcast(dtp)

    xnT = cv(A_ar, 0, 16 * T, BF16)
    oTa = cv(A_ar, 0, 16 * T, BF16)
    qhT = cv(B_ar, 0, NH * T, BF16)
    khT = cv(C_ar, 0, NH * T, BF16)
    dq_in = [cv(B_ar, 0, 3 * 1024, I16), cv(B_ar, 6144, 3 * 1024, I16)]
    dq_sc = [cv(B_ar, 12288, 16 * 512, F16), cv(C_ar, 0, 16 * 512, F16)]
    dq_t2 = cv(B_ar, 28672, 512, I16)
    dq_n2 = cv(B_ar, 29696, 512, I16)
    harena = cv(B_ar, 0, 24 * 512, BF16)
    ybuf = cv(C_ar, 0, 4 * H, BF16)
    xbuf = [cv(D_ar, 0, H, F32), cv(D_ar, 8192, H, F32)]
    sc1 = cv(D_ar, 16384, H, F32)
    sc2 = cv(D_ar, 24576, H, F32)
    Z = 32768
    dq_o = [cv(D_ar, Z, 16 * 512, BF16), cv(D_ar, Z + 16384, 16 * 512, BF16)]
    dq_o_d = cv(D_ar, Z, 24 * 512, BF16)
    dq_t = cv(B_ar, 30720, 512, I16)
    dq_n = cv(B_ar, 31744, 512, I16)
    dq_in_d = cv(D_ar, 24576, 3 * 1024, I16)
    dq_sc_d = cv(C_ar, 0, 12 * 512, F16)

    wslab2 = [cv(C_ar, 16384, 16 * 128, BF16), cv(C_ar, 20480, 16 * 128, BF16)]
    dslab = [cv(D_ar, Z + 8192, 24 * 512, BF16), cv(D_ar, Z + 8192, 24 * 512, BF16)]
    cvb = cv(D_ar, Z, 2052, BF16)
    cva = cv(D_ar, Z + 4160, 2048, BF16)
    wslab3 = [cv(D_ar, Z + 8320, 16 * 128, BF16), cv(D_ar, Z + 12416, 16 * 128, BF16)]
    ab_sb = cv(D_ar, 0, T, F32)[0:40, :]
    lab_sb = cv(D_ar, 8192, T, F32)[0:40, :]
    abT_sb = cv(D_ar, Z + 16512, 40, F32)
    cvb2 = cv(D_ar, Z + 16704, 2052, BF16)
    cva2 = cv(D_ar, Z + 20864, 2048, BF16)
    vstg = cv(D_ar, Z + 24960, 2048, BF16)
    vbuf = [cv(D_ar, 0, 16 * 128, BF16), cv(D_ar, 4096, 16 * 128, BF16)]
    vtok = cv(D_ar, 8192, 16 * 128, BF16)
    ktok = cv(D_ar, 12288, NH * 128, BF16)
    otk = cv(D_ar, 14336, NH * 256, BF16)
    Sm = cv(D_ar, 18432, NH * 256, F32)
    Sbb = cv(D_ar, 26624, NH * 256, BF16)
    MOFF = Z
    Nsb, NTsb, Dsb, Wsb, Xsb, XTsb, PTsb, Rsb, Usb, dUsb, T1sb = ([], [], [], [], [], [],
                                                                  [], [], [], [], [])
    LCbs = []
    mo = MOFF
    for i in range(2):
        Nsb.append(cv(D_ar, mo, 128, BF16)); mo += 256
        NTsb.append(cv(D_ar, mo, 128, BF16)); mo += 256
        Dsb.append(cv(D_ar, mo, 128, F32)); mo += 512
        Wsb.append(cv(D_ar, mo, 128, BF16)); mo += 256
        Xsb.append(cv(D_ar, mo, 11 * 128, BF16)); mo += 2816
        PTsb.append(cv(D_ar, mo, 2 * 128, BF16)); mo += 512
        Rsb.append(cv(D_ar, mo, 256, BF16)); mo += 512
        Usb.append(cv(D_ar, mo, 256, BF16)); mo += 512
        dUsb.append(cv(D_ar, mo, 256, BF16)); mo += 512
        T1sb.append(cv(D_ar, mo, 256, F32)); mo += 1024
        LCbs.append(cv(D_ar, mo, 8, F32)); mo += 64
    assert mo <= Z + 32768, mo
    wslab5 = [cv(D_ar, Z + 16384, 16 * 128, BF16), cv(D_ar, Z + 20480, 16 * 128, BF16)]
    ws5f = [cv(C_ar, 16384, 16 * 128, F32), cv(C_ar, 24576, 16 * 128, F32)]
    outb = cv(D_ar, Z + 24576, 2048, F32)
    TAIL = Z + 32768
    Larr = cv(D_ar, TAIL, NCHUNK * NH, F32)
    nLarr = cv(D_ar, TAIL + 512, NCHUNK * NH, F32)
    nBarr = cv(D_ar, TAIL + 1024, NCHUNK * NH, F32)
    Btarr = cv(D_ar, TAIL + 1536, NCHUNK * NH, F32)
    Garr = cv(D_ar, TAIL + 2048, NCHUNK * NH, F32)
    BGarr = cv(D_ar, TAIL + 2560, NCHUNK * NH, F32)
    dCarr = cv(D_ar, TAIL + 3072, NCHUNK * NH, F32)
    gCarr = cv(D_ar, TAIL + 3584, NCHUNK * NH, F32)
    LrowS = [cv(D_ar, TAIL + 4096, 128, F32)[0:1, :], cv(D_ar, TAIL + 4608, 128, F32)[0:1, :]]
    nLrowS = [cv(D_ar, TAIL + 5120, 128, F32)[0:1, :], cv(D_ar, TAIL + 5632, 128, F32)[0:1, :]]
    assert TAIL + 6144 <= 78336

    pF = [ctx.enter_context(pst(f"pF{i}", [128, 512], F32)) for i in range(2)]
    pDW = [ctx.enter_context(pst(f"pDW{i}", [128, 512], F32)) for i in range(2)]
    pXP = [ctx.enter_context(pst(f"pXP{i}", [128, 512], F32)) for i in range(2)]
    pTB = ctx.enter_context(pst("pTB", [128, 1024], BF16))
    pG = ctx.enter_context(pst("pG", [128, 512], F32))
    PTF = pTB[:, :].bitcast(F32)

    sem_names = ["s_cst", "s_x0", "s_x1", "s_dqi0", "s_dqi1", "s_dqo", "s_ws0", "s_ws1",
                 "s_ds", "s_cw0", "s_cw1", "s_wf0", "s_wf1", "s_hm", "s_yo", "s_vl0", "s_vl1", "s_sp", "s_vt", "s_st",
                 "s_pe", "s_ve", "s_se", "s_ge", "s_coll"]
    sems = {n: ctx.enter_context(nc.semaphore(n)) for n in sem_names}
    p = P(sems)

    B_pF = [Buf(), Buf()]
    B_pDW = [Buf(), Buf()]
    B_pXP = [Buf(), Buf()]
    B_pTB = Buf()
    B_pG = Buf()
    B_wslab = [Buf(), Buf()]
    B_dslab = [Buf(), Buf()]
    B_har = Buf()
    B_ybuf = Buf()
    B_xbuf = [Buf(), Buf()]
    B_sc1 = Buf()
    B_sc2 = Buf()
    B_cv = Buf()
    B_vbuf = [Buf(), Buf()]
    B_vtok = Buf()
    B_ktok = Buf()
    B_otk = Buf()
    B_dqin = [Buf(), Buf()]
    B_cwb = Buf()
    B_cw2 = Buf()
    B_dqo = [Buf(), Buf()]
    B_dqo_d = Buf()

    def dequant_block(kind, ib, b, in_toks, ntile_pt, nout_tiles, dram, sc_tile_fn,
                      obuf=None, oB=None, inbuf=None, inB=None, hw1_hook=None,
                      sc_read_cb=None, filler=None):
        obuf = dq_o[b] if obuf is None else obuf
        oB = B_dqo[b] if oB is None else oB
        inbuf = dq_in[b] if inbuf is None else inbuf
        inB = B_dqin[b] if inB is None else inB
        first_w = in_toks + oB.wwaits()
        last = None
        last_v = last_g = None
        hw1w = []
        for hw in range(2):
            eng = "vector"
            myt = dq_t if hw == 0 else dq_t2
            myn = dq_n if hw == 0 else dq_n2
            if hw == 1 and hw1_hook is not None:
                hw1w = hw1_hook()
            for c2 in range(4):
                cls = 4 * hw + c2
                sa = 12 - 4 * c2
                sbs = 9 - 4 * c2
                for pt in range(ntile_pt):
                    src = inbuf[:, (hw * ntile_pt + pt) * 512:(hw * ntile_pt + pt + 1) * 512]
                    src32 = src.bitcast(mybir.dt.int32)
                    t32 = myt[:, :].bitcast(mybir.dt.int32)
                    n32 = myn[:, :].bitcast(mybir.dt.int32)
                    fw = (first_w if hw == 0 else hw1w) if (c2 == 0 and pt == 0) else []
                    if sa > 0:
                        p.op(eng, lambda v, src32=src32, sa=sa, t32=t32: v.tensor_scalar(
                            t32, src32, sa, 0x80008000 - 2**32, ALU.logical_shift_left,
                            ALU.bitwise_and), waits=fw, inc=False)
                    else:
                        p.op(eng, lambda v, src32=src32, t32=t32: v.tensor_scalar(
                            t32, src32, 0x80008000 - 2**32, None, ALU.bitwise_and),
                            waits=fw, inc=False)
                    if sbs > 0:
                        p.op(eng, lambda v, src32=src32, sbs=sbs, n32=n32: v.tensor_scalar(
                            n32, src32, sbs, 0x0E000E00, ALU.logical_shift_left,
                            ALU.bitwise_and), inc=False)
                    else:
                        p.op(eng, lambda v, src32=src32, sbs=sbs, n32=n32: v.tensor_scalar(
                            n32, src32, -sbs, 0x0E000E00, ALU.logical_shift_right,
                            ALU.bitwise_and), inc=False)
                    p.op(eng, lambda v, n32=n32, t32=t32: v.tensor_tensor(
                        n32, n32, t32, ALU.bitwise_or), inc=False)
                    tile = cls * ntile_pt + pt
                    dst = obuf[:, tile * 512:(tile + 1) * 512]
                    scs = sc_tile_fn(tile)
                    tokx = p.op(eng, lambda v, dst=dst, scs=scs, myn=myn: v.scalar_tensor_tensor(
                        dst, myn[:, :].bitcast(F16), 16384.0, scs, ALU.mult, ALU.mult))
                    if sc_read_cb is not None:
                        sc_read_cb(tokx)
                    if filler is not None:
                        filler()
                    if hw == 0:
                        last_v = tokx
                    else:
                        last_g = tokx
        inB.read(last_v)
        inB.read(last_g)
        oB.wrote(last_v)
        oB.wrote(last_g)
        last = last_g
        tok = p.dma("gpsimd", dram, obuf[:, 0:nout_tiles * 512].rearrange("p (a n) -> p a n", a=nout_tiles),
                    waits=[last_v, last_g], sem="s_dqo")
        oB.read(tok)
        if DBG and kind == "g" and ib == 0:
            tok = p.dma("gpsimd", dbg_w.rearrange("p (a n) -> p a n", a=24),
                        obuf[:, :].rearrange("p (a n) -> p a n", a=24), waits=[last], sem="s_dqo")
            oB.read(tok)
        return tok

    NBLK_G = 2 * I_PAD // 512
    dq_g_blk = {}
    dq_d_blk = {}
    _gorder = [x for pair in zip(range(6), range(6, 12)) for x in pair]
    _dq_b = [0]
    _gorder = [x for pair in zip(range(6), range(6, 12)) for x in pair]
    DQ_AFTER_TILE = {}

    def load_gate(ib, b):
        dw = B_dqin[b].wwaits()
        w0 = p.dma("gpsimd", dq_in[b][:, 0:1024].rearrange("p (a n) -> p a n", a=2),
                   gup0_in.rearrange("(a p) n -> p a n", p=128)[:, :, ib * 512:(ib + 1) * 512],
                   waits=dw, sem=f"s_dqi{b}")
        w1 = p.dma("gpsimd", dq_in[b][:, 1024:2048].rearrange("p (a n) -> p a n", a=2),
                   gup1_in.rearrange("(a p) n -> p a n", p=128)[:, :, ib * 512:(ib + 1) * 512],
                   sem=f"s_dqi{b}")
        wsc = p.dma("gpsimd", dq_sc[b][:, :].rearrange("p (a n) -> p a n", a=16),
                    gsc_in.rearrange("(a p) n -> p a n", p=128)[:, :, ib * 512:(ib + 1) * 512],
                    sem=f"s_dqi{b}")
        B_dqin[b].wrote(wsc)
        return [w0, w1, wsc]

    def ops_gate(ib, b, toks):
        tok = dequant_block(
            "g", ib, b, toks, 2, 16,
            wgu_d.rearrange("(a p) n -> p a n", p=128)[:, :, ib * 512:(ib + 1) * 512],
            lambda tile, b=b: dq_sc[b][:, tile * 512:(tile + 1) * 512])
        dq_g_blk[ib] = tok

    B_dqind = Buf()
    B_dscd = Buf()

    def emit_dq_gates():
        pend = {}
        for i, ib in enumerate(_gorder):
            if i == 0:
                pend[0] = load_gate(_gorder[0], 0)
                pend[1] = load_gate(_gorder[1], 1)
            ops_gate(ib, i % 2, pend.pop(i))
            if i + 2 < len(_gorder):
                pend[i + 2] = load_gate(_gorder[i + 2], (i + 2) % 2)

    def emit_dq_downs(fillers):
        B_dqo_d.read(dq_g_blk[5])
        B_dqo_d.read(dq_g_blk[11])
        fq = list(fillers)
        fi = [0]
        cnt = [0]

        def filler():
            cnt[0] += 1
            if cnt[0] % 4 == 0 and fi[0] < len(fq):
                fq[fi[0]]()
                fi[0] += 1

        def load_in(ib):
            dwi = B_dqind.wwaits()
            w0 = p.dma("gpsimd", dq_in_d[:, 0:3 * 512].rearrange("p (a n) -> p a n", a=3),
                       dwn0_in.rearrange("(a p) n -> p a n", p=128)[:, :, ib * 512:(ib + 1) * 512],
                       waits=dwi, sem="s_dqi0")
            w1 = p.dma("gpsimd", dq_in_d[:, 3 * 512:6 * 512].rearrange("p (a n) -> p a n", a=3),
                       dwn1_in.rearrange("(a p) n -> p a n", p=128)[:, :, ib * 512:(ib + 1) * 512],
                       sem="s_dqi0")
            B_dqind.wrote(w1)
            return [w0, w1]

        pend_in = load_in(0)
        for ib in range(4):
            sc0 = p.dma("gpsimd", dq_sc_d[:, :].rearrange("p (a n) -> p a n", a=12),
                        dsc_in.rearrange("(a p) n -> p a n", p=128)[:, 0:12, ib * 512:(ib + 1) * 512],
                        waits=B_dscd.wwaits() + (B_dqin[1].wwaits() if ib == 0 else []),
                        sem="s_dqi1")
            B_dscd.wrote(sc0)

            def hw1_hook(ib=ib):
                tok = p.dma("gpsimd", dq_sc_d[:, :].rearrange("p (a n) -> p a n", a=12),
                            dsc_in.rearrange("(a p) n -> p a n", p=128)[:, 12:24, ib * 512:(ib + 1) * 512],
                            waits=B_dscd.wwaits(), sem="s_dqi1")
                B_dscd.wrote(tok)
                return [tok]
            tok = dequant_block(
                "d", ib, 0, pend_in + [sc0], 3, 24,
                dwn_d.rearrange("(a p) n -> p a n", p=128)[:, :, ib * 512:(ib + 1) * 512],
                lambda tile: dq_sc_d[:, (tile % 12) * 512:(tile % 12 + 1) * 512],
                obuf=dq_o_d, oB=B_dqo_d, inbuf=dq_in_d, inB=B_dqind,
                hw1_hook=hw1_hook, sc_read_cb=B_dscd.read, filler=filler)
            dq_d_blk[ib] = tok
            if ib + 1 < 4:
                pend_in = load_in(ib + 1)
        while fi[0] < len(fq):
            fq[fi[0]]()
            fi[0] += 1

    t_cst = p.dma("sync", cstf[:, :], cst_in, sem="s_cst")
    t_cstb = p.dma("sync", cstb[:, :], cbf_in, sem="s_cst")
    x_tok = []
    xn_tile_tok = []

    def emit_x_dma(tt):
        b = tt % 2
        tok = p.dma("sync", xbuf[b][:, :], x_in[tt * 128:(tt + 1) * 128, :],
                    waits=B_xbuf[b].wwaits(), sem=f"s_x{b}")
        B_xbuf[b].wrote(tok)
        x_tok.append(tok)
    emit_x_dma(0)
    emit_x_dma(1)

    evac_last = None
    for tt in range(16):
        b = tt % 2
        xb = xbuf[b]
        r1 = p.op("vector", lambda v, xb=xb: v.reduce_sum(stat[:, 0:1], xb[:, :], axis=AX.X),
                  waits=[x_tok[tt]])
        r2 = p.op("vector", lambda v: v.tensor_scalar(stat[:, 1:2], stat[:, 0:1], 1.0 / H, None, ALU.mult),
                  waits=[r1])
        t_sc1 = p.op("vector", lambda v, xb=xb: v.tensor_scalar(sc1[:, :], xb[:, :], stat[:, 1:2], None, ALU.subtract),
                     waits=[r2] + B_sc1.wwaits())
        B_sc1.wrote(t_sc1)
        tsq = p.op("scalar", lambda s: s.activation(sc2[:, :], sc1[:, :], AF.Square),
                   waits=B_sc2.wwaits() + B_sc1.rwaits())
        B_sc1.read(tsq)
        B_sc2.wrote(tsq)
        r3 = p.op("vector", lambda v: v.reduce_sum(stat[:, 2:3], sc2[:, :], axis=AX.X), waits=[tsq])
        v4 = p.op("vector", lambda v: v.tensor_scalar(stat[:, 3:4], stat[:, 2:3], 1.0 / H, 1e-5, ALU.mult, ALU.add),
                  waits=[r3])
        tsd = p.op("scalar", lambda s: s.activation(stat[:, 4:5], stat[:, 3:4], AF.Sqrt), waits=[v4])
        vrc = p.op("vector", lambda v: v.reciprocal(stat[:, 5:6], stat[:, 4:5]), waits=[tsd])
        tln = p.op("vector", lambda v: v.tensor_scalar(sc2[:, 0:1024].bitcast(BF16)[:, 0:H], sc1[:, :],
                                                       stat[:, 5:6], None, ALU.mult),
                   waits=[tsq, vrc] + B_sc2.wwaits())
        B_sc2.wrote(tln)
        B_xbuf[b].read(tln)
        B_sc1.read(tln)
        if tt + 2 < 16:
            emit_x_dma(tt + 2)
        if DBG:
            dtk = p.dma("gpsimd", dbg_xn2[tt * 128:(tt + 1) * 128, :],
                        sc2[:, 0:1024].bitcast(BF16)[:, 0:H], waits=[tln], sem="s_st")
            B_sc2.read(dtk)
        for bh in range(2):
            ttr = None
            for hh in range(8 * bh, 8 * bh + 8):
                slot = hh % 8
                tw = ([tln, t_cstb] + B_pTB.wwaits()) if slot == 0 else []
                ttr = p.op("tensor", lambda t, hh=hh, slot=slot: t.transpose(
                    pTB[:, slot * 128:(slot + 1) * 128],
                    sc2[:, 0:1024].bitcast(BF16)[:, hh * 128:(hh + 1) * 128], IDB), waits=tw,
                    inc=(slot == 7))
            B_pTB.wrote(ttr)
            for hh in range(8 * bh, 8 * bh + 8):
                slot = hh % 8
                ev = p.op("scalar", lambda s, hh=hh, slot=slot, tt=tt: s.copy(
                    xnT[:, hh * T + tt * 128: hh * T + tt * 128 + 128],
                    pTB[:, slot * 128:(slot + 1) * 128]), waits=([ttr] if slot == 0 else []))
                B_pTB.read(ev)
            evac_last = ev
        B_sc2.read(evac_last)
        xn_tile_tok.append(evac_last)
        if tt in DQ_AFTER_TILE:
            for _ib in DQ_AFTER_TILE[tt]:
                emit_dq_gate(_ib)
    xn_ready = evac_last
    xn_tc = [xn_tile_tok[3], xn_tile_tok[7], xn_tile_tok[11], xn_tile_tok[15]]
    emit_dq_gates()

    y_ar = []
    ws3_pre = {}
    hm_fill = []

    def ffn_gates(tc):
        defer = (tc == 0) and _DEFER[0]
        hm_last = None
        for ipair in range(24):
            mm_done = []
            for which in range(2):
                it = ipair + 24 * which
                b = (tc * 48 + ipair * 2 + which) % 2
                wt = p.dma("sync", wslab2[b][:, :].rearrange("p (a n) -> p a n", a=16),
                           wgu_d.rearrange("(a p) n -> p a n", p=128)[:, :, it * 128:(it + 1) * 128],
                           waits=[dq_g_blk[it // 4 if it < 24 else 6 + (it - 24) // 4]]
                                 + B_wslab[b].wwaits(), sem=f"s_ws{b}")
                B_wslab[b].wrote(wt)
                bank = pF[which]
                mw = [wt, xn_tc[tc]] + B_pF[which].wwaits()
                if defer and which == 1 and ipair > 0:
                    mw.append(("s_hm", ipair))
                lm = None
                for ht in range(16):
                    lm = p.op("tensor", lambda t, bank=bank, b=b, ht=ht, tc=tc: t.matmul(
                        bank[:, 0:512], wslab2[b][:, ht * 128:(ht + 1) * 128],
                        xnT[:, ht * T + tc * 512: ht * T + (tc + 1) * 512],
                        start=(ht == 0), stop=(ht == 15)), waits=(mw if ht == 0 else []),
                        inc=(ht == 15))
                B_pF[which].wrote(lm)
                B_wslab[b].read(lm)
                mm_done.append(lm)
            sil_w = [mm_done[0]] + B_sc1.wwaits()
            if defer and ipair > 0:
                sil_w.append(("s_hm", ipair))
            tsil = p.op("scalar", lambda s: s.activation(sc1[:, 0:512], pF[0][:, 0:512], AF.Silu),
                        waits=sil_w)
            B_pF[0].read(tsil)
            B_sc1.wrote(tsil)
            if defer:
                def mk(ipair=ipair, tsil=tsil, mm1=mm_done[1],
                       harw=tuple(B_har.wwaits() if ipair == 0 else [])):
                    def emit():
                        w = [tsil, mm1] + list(harw)
                        tok = p._emit("vector", lambda v, ipair=ipair: v.tensor_tensor(
                            harena[:, ipair * 512:(ipair + 1) * 512], sc1[:, 0:512],
                            pF[1][:, 0:512], ALU.mult), waits=w, inc="s_hm", inc_n=1)
                        B_pF[1].read(tok)
                        B_sc1.read(tok)
                    return emit
                hm_fill.append(mk())
                hm_last = ("s_hm", ipair + 1)
            else:
                hm = p.op("vector", lambda v, ipair=ipair: v.tensor_tensor(
                    harena[:, ipair * 512:(ipair + 1) * 512], sc1[:, 0:512], pF[1][:, 0:512],
                    ALU.mult), waits=[tsil, mm_done[1]] + (B_har.wwaits() if ipair == 0 else []))
                B_pF[1].read(hm)
                B_sc1.read(hm)
                hm_last = hm
        B_har.wrote(hm_last)
        return hm_last

    def ffn_down(tc, hm_last):
        ev_last = None
        for hc in range(4):
            b = 0
            wt = p.dma("sync", dslab[b][:, :].rearrange("p (a n) -> p a n", a=24),
                       dwn_d.rearrange("(a p) n -> p a n", p=128)[:, :, hc * 512:(hc + 1) * 512],
                       waits=[dq_d_blk[3]] + B_dslab[0].wwaits(), sem="s_ds")
            B_dslab[0].wrote(wt)
            for tt4 in range(4):
                bank = pDW[tt4 % 2]
                mw = [wt, hm_last] + B_pDW[tt4 % 2].wwaits()
                lm = None
                for it in range(24):
                    lm = p.op("tensor", lambda t, bank=bank, it=it, tt4=tt4, b=b: t.matmul(
                        bank[:, 0:512],
                        harena[:, it * 512 + tt4 * 128: it * 512 + tt4 * 128 + 128],
                        dslab[b][:, it * 512:(it + 1) * 512],
                        start=(it == 0), stop=(it == 23)), waits=(mw if it == 0 else []),
                        inc=(it == 23))
                B_pDW[tt4 % 2].wrote(lm)
                B_dslab[0].read(lm)
                B_har.read(lm)
                ev = p.op("vector", lambda v, bank=bank, tt4=tt4, hc=hc: v.tensor_copy(
                    ybuf[:, tt4 * H + hc * 512: tt4 * H + (hc + 1) * 512], bank[:, 0:512]),
                    waits=[lm] + (B_ybuf.wwaits() if (hc == 0 and tt4 == 0) else []))
                B_pDW[tt4 % 2].read(ev)
                ev_last = ev
        B_ybuf.wrote(ev_last)
        if tc == 3:
            for _nm, _sl, _wi in (("q", 0, wq_in), ("v", 1, wv_in)):
                ws3_pre[(_nm, 0)] = p.dma(
                    "gpsimd", wslab3[_sl][:, :].rearrange("p (a n) -> p a n", a=16),
                    _wi.rearrange("(a p) n -> p a n", p=128)[:, :, 0:128],
                    waits=B_dslab[0].wwaits(), sem=f"s_ws{_sl}")
        yd = p.dma("gpsimd", ccy_in[tc].rearrange("(a p) n -> p a n", p=128),
                   ybuf[:, :].rearrange("p (a n) -> p a n", a=4), waits=[ev_last], sem="s_st")
        B_ybuf.read(yd)
        ar = p.coll(lambda g, tc=tc: g.collective_compute(
            "AllReduce", ALU.add, replica_groups=GROUPS, ins=[ccy_in[tc]], outs=[ccy_out[tc]]),
            waits=[yd])
        y_ar.append(ar)

    if _DEFER[0]:
        ffn_gates(0)
        emit_dq_downs(hm_fill)
        ffn_down(0, ("s_hm", 24))
        for tc in range(1, 4):
            hm_last_t = ffn_gates(tc)
            ffn_down(tc, hm_last_t)
    else:
        emit_dq_downs([])
        for tc in range(4):
            hm_last_t = ffn_gates(tc)
            ffn_down(tc, hm_last_t)

    sp_tok = {}
    for blk in range(4):
        ld = p.dma("sync", ybuf[:, :].rearrange("p (a n) -> p a n", a=4),
                   ccy_out[blk].rearrange("(a p) n -> p a n", p=128),
                   waits=[y_ar[blk]] + B_ybuf.wwaits(), sem="s_yo")
        B_ybuf.wrote(ld)
        for q in range(4):
            tt = blk * 4 + q
            bx = tt % 2
            xw = p.dma("sync", xbuf[bx][:, :], x_in[tt * 128:(tt + 1) * 128, :],
                       waits=B_xbuf[bx].wwaits(), sem=f"s_x{bx}")
            B_xbuf[bx].wrote(xw)
            yb = ybuf[:, q * H:(q + 1) * H]
            t1 = p.op("vector", lambda v, yb=yb: v.tensor_tensor(sc1[:, :], yb, yb, ALU.mult),
                      waits=[ld] + B_sc1.wwaits())
            B_sc1.wrote(t1)
            r8 = p.op("vector", lambda v: v.reduce_sum(stat[:, 8:9], sc1[:, :], axis=AX.X), waits=[t1])
            v4b = p.op("vector", lambda v: v.tensor_scalar(stat[:, 9:10], stat[:, 8:9], 1.0 / H, 1e-6,
                                                           ALU.mult, ALU.add), waits=[r8])
            tsd = p.op("scalar", lambda s: s.activation(stat[:, 10:11], stat[:, 9:10], AF.Sqrt),
                       waits=[t1, v4b])
            vrc2 = p.op("vector", lambda v: v.reciprocal(stat[:, 11:12], stat[:, 10:11]), waits=[tsd])
            t2 = p.op("vector", lambda v, yb=yb: v.tensor_scalar(sc2[:, :], yb, stat[:, 11:12], None,
                                                                 ALU.mult), waits=[vrc2] + B_sc2.wwaits())
            B_sc2.wrote(t2)
            tadd = p.op("vector", lambda v, bx=bx: v.tensor_tensor(sc2[:, :], sc2[:, :],
                                                                   xbuf[bx][:, :], ALU.add),
                        waits=[xw])
            B_sc2.wrote(tadd)
            B_xbuf[bx].read(tadd)
            tok = p.dma("gpsimd", spill_d[tt * 128:(tt + 1) * 128, :], sc2[:, :],
                        waits=[tadd], sem="s_sp")
            B_sc2.read(tok)
            if DBG:
                tok = p.dma("gpsimd", dbg_sp[tt * 128:(tt + 1) * 128, :], sc2[:, :],
                            waits=[tadd], sem="s_sp")
                B_sc2.read(tok)
            B_ybuf.read(tok)
            sp_tok[tt] = tok
    spill_done = sp_tok[15]

    vdma_tok = {}
    qk_store = {"q": qhT, "k": khT}
    PROJ = {"q": (wq_in, qc_in), "k": (wk_in, kc_in), "v": (wv_in, vc_in)}
    B_cv2 = Buf()
    B_vstg = Buf()

    def proj_ct(name, ct, slot):
        w_in, c_in = PROJ[name]
        b = slot
        banks = (pG, pF[0]) if slot == 0 else (pDW[0], pDW[1])
        bbs = (B_pG, B_pF[0]) if slot == 0 else (B_pDW[0], B_pDW[1])
        mycvb, mycva = (cvb, cva) if slot == 0 else (cvb2, cva2)
        B_mycv = B_cv if slot == 0 else B_cv2
        B_mycw = B_cwb if slot == 0 else B_cw2
        wt = ws3_pre.pop((name, ct), None)
        if wt is None:
            wt = p.dma("gpsimd", wslab3[b][:, :].rearrange("p (a n) -> p a n", a=16),
                       w_in.rearrange("(a p) n -> p a n", p=128)[:, :, ct * 128:(ct + 1) * 128],
                       waits=B_wslab[b].wwaits() + (B_dslab[0].wwaits() if ct < 2 else []),
                       sem=f"s_ws{b}")
        B_wslab[b].wrote(wt)
        cwt = p.dma("sync", cw[:, 8 * slot:8 * slot + 8], c_in[ct * 128:(ct + 1) * 128, :],
                    waits=B_mycw.wwaits(), sem=f"s_cw{slot}")
        B_mycw.wrote(cwt)
        evs = []
        for tc in range(4):
            bank = banks[tc % 2]
            bb = bbs[tc % 2]
            mw = [wt, xn_ready] + bb.wwaits()
            lm = None
            for ht in range(16):
                lm = p.op("tensor", lambda t, bank=bank, b=b, ht=ht, tc=tc: t.matmul(
                    bank[:, 0:512], wslab3[b][:, ht * 128:(ht + 1) * 128],
                    xnT[:, ht * T + tc * 512: ht * T + (tc + 1) * 512],
                    start=(ht == 0), stop=(ht == 15)), waits=(mw if ht == 0 else []),
                    inc=(ht == 15))
            bb.wrote(lm)
            B_wslab[b].read(lm)
            evc = p.op("scalar", lambda s, bank=bank, tc=tc, mycvb=mycvb: s.copy(
                mycvb[:, 3 + tc * 512: 3 + (tc + 1) * 512], bank[:, 0:512]),
                waits=[lm] + (B_mycv.wwaits() if tc == 0 else []))
            bb.read(evc)
            evs.append(evc)
            yield
        B_mycv.wrote(evs[-1])
        p.op("vector", lambda v, mycvb=mycvb: v.memset(mycvb[:, 0:3], 0.0),
             waits=[evs[-1], cwt], inc=False)
        p.op("vector", lambda v, mycvb=mycvb, mycva=mycva, slot=slot: v.tensor_scalar(
            mycva[:, :], mycvb[:, 0:2048], cw[:, 8 * slot:8 * slot + 1], None, ALU.mult), inc=False)
        for j in (1, 2):
            p.op("vector", lambda v, j=j, mycvb=mycvb, mycva=mycva, slot=slot: v.scalar_tensor_tensor(
                mycva[:, :], mycvb[:, j:j + 2048], cw[:, 8 * slot + j:8 * slot + j + 1],
                mycva[:, :], ALU.mult, ALU.add), inc=False)
        tconv = p.op("vector", lambda v, mycvb=mycvb, mycva=mycva, slot=slot: v.scalar_tensor_tensor(
            mycva[:, :], mycvb[:, 3:3 + 2048], cw[:, 8 * slot + 3:8 * slot + 4],
            mycva[:, :], ALU.mult, ALU.add))
        B_mycv.read(tconv)
        B_mycw.read(tconv)
        yield
        if name == "v":
            tsil = p.op("scalar", lambda s: s.activation(vstg[:, :], cva2[:, :], AF.Silu,
                                                         bias=cw[:, 12:13], scale=1.0),
                        waits=[tconv] + B_vstg.wwaits())
            B_mycw.read(tsil)
            B_vstg.wrote(tsil)
            tok = p.dma("gpsimd", vt_d[ct * 128:(ct + 1) * 128, :], vstg[:, :],
                        waits=[tsil], sem="s_vt")
            B_vstg.read(tok)
            if DBG:
                tok = p.dma("gpsimd", dbg_v[ct * 128:(ct + 1) * 128, :], vstg[:, :],
                            waits=[tsil], sem="s_vt")
                B_vstg.read(tok)
            vdma_tok[ct] = tok
            yield
            return
        tsil = p.op("scalar", lambda s: s.activation(sc1[:, :], cva[:, :], AF.Silu,
                                                     bias=cw[:, 4:5], scale=1.0),
                    waits=[tconv] + B_sc1.wwaits())
        B_sc1.wrote(tsil)
        B_mycw.read(tsil)
        store = qk_store[name]
        for tt4 in range(4):
            bank = pXP[tt4 % 2]
            bb = B_pXP[tt4 % 2]
            ltr = None
            for k4 in range(4):
                ttt = tt4 * 4 + k4
                trw = ([tsil] if (tt4 == 0 and k4 == 0) else []) + (bb.wwaits() if k4 == 0 else [])
                ltr = p.op("tensor", lambda t, ttt=ttt, k4=k4, bank=bank: t.transpose(
                    bank[:, k4 * 128:(k4 + 1) * 128],
                    sc1[:, ttt * 128:(ttt + 1) * 128], IDF), waits=trw,
                    inc=(k4 == 3))
            bb.wrote(ltr)
            B_sc1.read(ltr)
            sq = p.op("scalar", lambda s, bank=bank: s.activation(
                sc2[:, 0:512], bank[:, 0:512], AF.Square),
                waits=[ltr] + B_sc2.wwaits())
            B_sc2.wrote(sq)
            rs = p.op("vector", lambda v: v.reduce_sum(
                stat[:, 16:20], sc2[:, 0:512].rearrange("p (a n) -> p a n", a=4), axis=AX.X),
                waits=[sq])
            B_sc2.read(rs)
            vr = p.op("vector", lambda v: v.reciprocal(stat[:, 20:24], stat[:, 16:20]),
                      waits=[rs])
            sr = p.op("scalar", lambda s: s.activation(stat[:, 24:28], stat[:, 20:24], AF.Sqrt),
                      waits=[rs, vr])
            tmn = p.op("vector", lambda v: v.tensor_scalar_min(stat[:, 28:32], stat[:, 24:28], 1e12),
                       waits=[sr])
            nm = None
            for k4 in range(4):
                nm = p.op("vector", lambda v, k4=k4, bank=bank: v.tensor_scalar(
                    sc2[:, 512 + k4 * 64:512 + (k4 + 1) * 64].bitcast(BF16)[:, 0:128],
                    bank[:, k4 * 128:(k4 + 1) * 128],
                    stat[:, 28 + k4:29 + k4], None, ALU.mult),
                    waits=([tmn] if k4 == 0 else []), inc=(k4 == 3))
                bb.read(nm)
            yield
            tb = None
            for k4 in range(4):
                slotp = k4 + (4 if tt4 % 2 else 0)
                tb = p.op("tensor", lambda t, k4=k4, slotp=slotp: t.transpose(
                    pTB[:, slotp * 128:(slotp + 1) * 128],
                    sc2[:, 512 + k4 * 64:512 + (k4 + 1) * 64].bitcast(BF16)[:, 0:128],
                    IDB), waits=([nm] + B_pTB.wwaits()) if k4 == 0 else [],
                    inc=(k4 == 3))
            B_pTB.wrote(tb)
            for k4 in range(4):
                slotp = k4 + (4 if tt4 % 2 else 0)
                ttt = tt4 * 4 + k4
                aw = ((B_ybuf.wwaits() + B_wslab[0].wwaits() + B_wslab[1].wwaits())
                      if (name == "k" and ct == 0 and tt4 == 0 and k4 == 0) else [])
                ev = p.op("scalar", lambda s, slotp=slotp, store=store, ct=ct, ttt=ttt: s.copy(
                    store[:, ct * T + ttt * 128: ct * T + ttt * 128 + 128],
                    pTB[:, slotp * 128:(slotp + 1) * 128]),
                    waits=([tb] if k4 == 0 else []) + aw)
                B_pTB.read(ev)
                B_sc2.read(ev)
            B_sc1.read(ev)
            GD[0] = ev
            yield

    GD = [None]
    vdma_tok = {}
    qk_store = {"q": qhT, "k": khT}
    plan_slots = [("q", list(range(8))), ("k", list(range(8)))]
    qk_seq = [("q", c) for c in range(8)] + [("k", c) for c in range(8)]
    v_seq = [("v", c) for c in range(16)]
    g0 = g1 = None
    i0 = i1 = 0
    while i0 < len(qk_seq) or i1 < len(v_seq) or g0 is not None or g1 is not None:
        if g0 is None and i0 < len(qk_seq):
            nm0, c0 = qk_seq[i0]
            g0 = proj_ct(nm0, c0, 0)
            i0 += 1
        if g1 is None and i1 < len(v_seq):
            nm1, c1 = v_seq[i1]
            g1 = proj_ct(nm1, c1, 1)
            i1 += 1
        for gref in (0, 1):
            g = g0 if gref == 0 else g1
            if g is not None:
                try:
                    next(g)
                except StopIteration:
                    if gref == 0:
                        g0 = None
                    else:
                        g1 = None
    qk_done = GD[0]

    wt = p.dma("gpsimd", wslab3[0][:, 0:16 * 40].rearrange("p (a n) -> p a n", a=16),
               wab_in.rearrange("(a p) n -> p a n", p=128), waits=B_wslab[0].wwaits(), sem="s_ws0")
    B_wslab[0].wrote(wt)
    for tc in range(4):
        mw = [wt, xn_ready] + B_pG.wwaits()
        lm = None
        for ht in range(16):
            lm = p.op("tensor", lambda t, ht=ht, tc=tc: t.matmul(
                pG[0:40, 0:512], wslab3[0][:, ht * 40:(ht + 1) * 40],
                xnT[:, ht * T + tc * 512: ht * T + (tc + 1) * 512],
                start=(ht == 0), stop=(ht == 15)), waits=(mw if ht == 0 else []), inc=(ht == 15))
        B_pG.wrote(lm)
        B_wslab[0].read(lm)
        ev = p.op("scalar", lambda s, tc=tc: s.copy(ab_sb[0:40, tc * 512:(tc + 1) * 512],
                                                    pG[0:40, 0:512]),
                  waits=[lm] + (B_xbuf[0].wwaits() if tc == 0 else []))
        B_pG.read(ev)
    e1 = p.op("scalar", lambda s: s.activation(lab_sb[0:40, :], ab_sb[0:40, :], AF.Exp, scale=-1.0),
              waits=[ev] + B_xbuf[1].wwaits())
    e2 = p.op("vector", lambda v: v.tensor_scalar(lab_sb[0:40, :], lab_sb[0:40, :], 1.0, None, ALU.add),
              waits=[e1])
    e3 = p.op("scalar", lambda s: s.activation(lab_sb[0:8, :], lab_sb[0:8, :], AF.Ln), waits=[e2])
    e4 = p.op("vector", lambda v: v.tensor_scalar(lab_sb[0:8, :], lab_sb[0:8, :], -1.0, None, ALU.mult),
              waits=[e3])
    e5 = p.op("vector", lambda v: v.reciprocal(lab_sb[32:40, :], lab_sb[32:40, :]), waits=[e3])
    prev_dc = None
    for ci in range(NCHUNK):
        tr = p.op("tensor", lambda t, ci=ci: t.transpose(
            PTF[:, 0:40], lab_sb[0:40, ci * 128:(ci + 1) * 128], IDF[0:40, 0:40]),
            waits=[e4, e5] + B_pTB.wwaits())
        B_pTB.wrote(tr)
        evt = p.op("scalar", lambda s: s.copy(abT_sb[:, 0:40], PTF[:, 0:40]), waits=[tr])
        B_pTB.read(evt)
        btk = p.op("vector", lambda v, ci=ci: v.tensor_copy(Btarr[:, ci * NH:(ci + 1) * NH],
                                                            abT_sb[:, 32:40]), waits=[evt])
        nb = p.op("vector", lambda v, ci=ci: v.tensor_scalar(
            nBarr[:, ci * NH:(ci + 1) * NH], abT_sb[:, 32:40], -1.0, None, ALU.mult))
        tm = p.op("tensor", lambda t: t.matmul(pG[:, 0:8], TRI, abT_sb[:, 0:8],
                                               start=True, stop=True),
                  waits=[evt] + B_pG.wwaits())
        B_pG.wrote(tm)
        evl = p.op("scalar", lambda s, ci=ci: s.copy(Larr[:, ci * NH:(ci + 1) * NH],
                                                     pG[:, 0:8]), waits=[tm])
        B_pG.read(evl)
        nl = p.op("vector", lambda v, ci=ci: v.tensor_scalar(
            nLarr[:, ci * NH:(ci + 1) * NH], Larr[:, ci * NH:(ci + 1) * NH], -1.0, None, ALU.mult),
            waits=[evl])
        gm = p.op("scalar", lambda s, ci=ci: s.activation(Garr[:, ci * NH:(ci + 1) * NH],
                                                          Larr[:, ci * NH:(ci + 1) * NH], AF.Exp),
                  waits=[evl])
        p.op("vector", lambda v, ci=ci: v.tensor_tensor(
            BGarr[:, ci * NH:(ci + 1) * NH], Garr[:, ci * NH:(ci + 1) * NH],
            Btarr[:, ci * NH:(ci + 1) * NH], ALU.mult), waits=[gm, btk], inc=False)
        prev_dc = gm if nl is None else nl
        prev_dc = gm
    ab_ready = prev_dc

    zS = p.op("vector", lambda v: v.memset(Sm[:, :], 0.0), waits=[ab_ready])
    zS2 = p.op("vector", lambda v: v.memset(Sbb[:, :], 0.0))
    blk_done = {}
    rec_guard = {}
    for ci in range(NCHUNK):
        b = ci % 2
        vw = (([vdma_tok[vt] for vt in range(16)] + [ab_ready] + B_cv.wwaits())
              if ci == 0 else []) + B_vbuf[b].wwaits()
        vt_ld = p.dma("sync", vbuf[b][:, :].rearrange("p (a n) -> p a n", a=16),
                      vt_d.rearrange("(a p) n -> p a n", p=128)[:, :, ci * 128:(ci + 1) * 128],
                      waits=vw, sem=f"s_vl{b}")
        B_vbuf[b].wrote(vt_ld)
        ev = None
        for bh in range(2):
            trv = None
            for vt in range(8 * bh, 8 * bh + 8):
                slot = vt % 8
                tw = ([vt_ld, zS2, qk_done] + B_pTB.wwaits() + B_vtok.wwaits()) if vt == 0 else \
                     (B_pTB.wwaits() if vt == 8 else [])
                trv = p.op("tensor", lambda t, vt=vt, slot=slot, b=b: t.transpose(
                    pTB[:, slot * 128:(slot + 1) * 128], vbuf[b][:, vt * 128:(vt + 1) * 128], IDB),
                    waits=tw, inc=(slot == 7))
            B_pTB.wrote(trv)
            for vt in range(8 * bh, 8 * bh + 8):
                slot = vt % 8
                ev = p.op("scalar", lambda s, vt=vt, slot=slot: s.copy(
                    vtok[:, vt * 128:(vt + 1) * 128], pTB[:, slot * 128:(slot + 1) * 128]),
                    waits=([trv] if slot == 0 else []))
                B_pTB.read(ev)
        B_vbuf[b].read(ev)
        B_vtok.wrote(ev)
        vtok_ready = ev
        trk = None
        for h in range(NH):
            tw = (B_pTB.wwaits() + B_ktok.wwaits()) if h == 0 else []
            trk = p.op("tensor", lambda t, h=h, ci=ci: t.transpose(
                pTB[:, h * 128:(h + 1) * 128], khT[:, h * T + ci * 128: h * T + ci * 128 + 128],
                IDB), waits=tw, inc=(h == NH - 1))
        B_pTB.wrote(trk)
        ev = None
        for h in range(NH):
            ev = p.op("scalar", lambda s, h=h: s.copy(
                ktok[:, h * 128:(h + 1) * 128], pTB[:, h * 128:(h + 1) * 128]),
                waits=([trk] if h == 0 else []))
            B_pTB.read(ev)
        B_ktok.wrote(ev)
        ktok_ready = ev
        def rec_head(ci, h):
            par = h % 2
            cs = ci * NH + h
            Lc = Larr[:, cs:cs + 1]
            nLc = nLarr[:, cs:cs + 1]
            nBc = nBarr[:, cs:cs + 1]
            Bc = Btarr[:, cs:cs + 1]
            Gc = Garr[:, cs:cs + 1]
            BGc = BGarr[:, cs:cs + 1]
            dCc = dCarr[:, cs:cs + 1]
            gCc = gCarr[:, cs:cs + 1]
            kT = khT[:, h * T + ci * 128: h * T + ci * 128 + 128]
            qT = qhT[:, h * T + ci * 128: h * T + ci * 128 + 128]
            trL = p.op("tensor", lambda t, cs=cs: t.transpose(
                PTF[0:1, 384:512], Larr[:, cs:cs + 1], IDF), waits=B_pTB.wwaits())
            B_pTB.wrote(trL)
            evL = p.op("scalar", lambda s, par=par: s.copy(LrowS[par][0:1, :], PTF[0:1, 384:512]),
                       waits=[trL])
            B_pTB.read(evL)
            nLr = p.op("scalar", lambda s, par=par: s.mul(nLrowS[par][0:1, :], LrowS[par][0:1, :],
                                                          -1.0), waits=[evL])
            m1 = p.op("tensor", lambda t, par=par: t.matmul(
                pG[:, par * 2:par * 2 + 1], ONER, LrowS[par][0:1, 127:128], start=True, stop=True),
                waits=[evL] + B_pG.wwaits())
            B_pG.wrote(m1)
            gc = p.op("scalar", lambda s, cs=cs, par=par: s.activation(
                gCarr[:, cs:cs + 1], pG[:, par * 2:par * 2 + 1], AF.Exp), waits=[m1])
            lcb = p.op("scalar", lambda s, par=par: s.copy(LCbs[par][:, 0:1], pG[:, par * 2:par * 2 + 1]),
                       waits=[m1])
            B_pG.read(gc)
            B_pG.read(lcb)
            dc = p.op("scalar", lambda s, cs=cs, par=par: s.activation(
                dCarr[:, cs:cs + 1], Larr[:, cs:cs + 1], AF.Exp, bias=LCbs[par][:, 0:1],
                scale=-1.0), waits=[lcb])
            yield
            f1 = p.op("tensor", lambda t, kT=kT, par=par: t.matmul(
                pF[par][:, 0:128], kT, kT, start=True, stop=True),
                waits=B_pF[par].wwaits() + ([ktok_ready, vtok_ready] if h < 2 else []))
            p.op("tensor", lambda t, kT=kT, qT=qT, par=par: t.matmul(
                pF[par][:, 128:256], kT, qT, start=True, stop=True), inc=False)
            f3 = p.op("tensor", lambda t, kT=kT, par=par, h=h: t.matmul(
                pF[par][:, 256:512], kT, Sbb[:, h * 256:(h + 1) * 256], start=True, stop=True))
            B_pF[par].wrote(f3)
            yield
            d1 = p.op("tensor", lambda t, par=par: t.matmul(
                pDW[par][:, 0:128], ONER, nLrowS[par][0:1, :],
                start=True, stop=True), waits=[nLr] + B_pDW[par].wwaits())
            d2 = p.op("tensor", lambda t, par=par: t.matmul(
                pDW[par][:, 0:128], IDF, MSK_S, start=False, stop=True))
            dex = p.op("scalar", lambda s, par=par, Lc=Lc: s.activation(
                Dsb[par][:, :], pDW[par][:, 0:128], AF.Exp, bias=Lc, scale=1.0), waits=[d2])
            nbu = p.op("vector", lambda v, par=par, nBc=nBc: v.scalar_tensor_tensor(
                Nsb[par][:, :], pF[par][:, 0:128], nBc, Dsb[par][:, :], ALU.mult, ALU.mult),
                waits=[dex, f3])
            B_pF[par].read(nbu)
            yield
            d3 = p.op("tensor", lambda t, par=par: t.matmul(
                pDW[par][:, 128:256], ONER, LrowS[par][0:1, :],
                start=True, stop=True), waits=[dex, evL])
            d4 = p.op("tensor", lambda t, par=par: t.matmul(
                pDW[par][:, 128:256], IDF, MSK_IT, start=False, stop=True))
            wex = p.op("scalar", lambda s, par=par, nLc=nLc: s.activation(
                Dsb[par][:, :], pDW[par][:, 128:256], AF.Exp, bias=nLc, scale=1.0), waits=[d4])
            wbu = p.op("vector", lambda v, par=par: v.tensor_tensor(
                Wsb[par][:, :], pF[par][:, 128:256], Dsb[par][:, :], ALU.mult), waits=[wex])
            B_pF[par].read(wbu)
            B_pDW[par].wrote(d4)
            B_pDW[par].read(wex)
            yield
            nt1 = p.op("tensor", lambda t, par=par: t.transpose(
                pTB[:, 0:128], Nsb[par][:, :], IDB), waits=[nbu] + B_pTB.wwaits())
            B_pTB.wrote(nt1)
            nt2 = p.op("scalar", lambda s, par=par: s.copy(NTsb[par][:, :], pTB[:, 0:128]),
                       waits=[nt1])
            B_pTB.read(nt2)
            pv = p.op("vector", lambda v, par=par: v.tensor_tensor(
                PTsb[par][:, 0:128], NTsb[par][:, :], IDB, ALU.add), waits=[nt2])
            yield
            xsrc = {1: (Nsb[par][:, :], NTsb[par][:, :])}
            prev_ev = None
            for li, j in enumerate([1, 2, 4, 8, 16]):
                Xj, XTj = xsrc[j]
                m = p.op("tensor", lambda t, Xj=Xj, XTj=XTj, par=par: t.matmul(
                    pXP[par][:, 0:128], XTj, Xj, start=True, stop=True),
                    waits=([nt2] if li == 0 else [prev_ev]) + (B_pXP[par].wwaits() if li == 0 else []),
                    inc=False)
                m2 = p.op("tensor", lambda t, Xj=Xj, XTj=XTj, par=par: t.matmul(
                    pXP[par][:, 128:256], Xj, XTj, start=True, stop=True))
                prev_ev = p.op("vector", lambda v, par=par, li=li: v.tensor_copy(
                    Xsb[par][:, li * 256:(li + 1) * 256], pXP[par][:, 0:256]), waits=[m2])
                xsrc[2 * j] = (Xsb[par][:, li * 256:li * 256 + 128],
                               Xsb[par][:, li * 256 + 128:(li + 1) * 256])
                yield
            X32, XT32 = xsrc[32]
            m = p.op("tensor", lambda t, X32=X32, XT32=XT32, par=par: t.matmul(
                pXP[par][:, 0:128], XT32, X32, start=True, stop=True), waits=[prev_ev])
            xev = p.op("vector", lambda v, par=par: v.tensor_copy(
                Xsb[par][:, 5 * 256:5 * 256 + 128], pXP[par][:, 0:128]), waits=[m])
            xsrc[64] = (Xsb[par][:, 5 * 256:5 * 256 + 128], None)
            yield
            pp = 0
            for li, j in enumerate([2, 4, 8, 16, 32, 64]):
                src = PTsb[par][:, pp * 128:(pp + 1) * 128]
                dstp = 1 - pp
                Xj = Xsb[par][:, (li) * 256:(li) * 256 + 128] if j != 64 else xsrc[64][0]
                m = p.op("tensor", lambda t, par=par, Xj=Xj, src=src: t.matmul(
                    pXP[par][:, 256:384], Xj, src, start=True, stop=True),
                    waits=[pv] + ([xev] if li == 0 else []))
                pv = p.op("vector", lambda v, par=par, dstp=dstp, src=src: v.scalar_tensor_tensor(
                    PTsb[par][:, dstp * 128:(dstp + 1) * 128], pXP[par][:, 256:384], 1.0, src,
                    ALU.mult, ALU.add), waits=[m])
                pp = dstp
                yield
            B_pXP[par].wrote(m)
            B_pXP[par].read(pv)
            MT = PTsb[par][:, pp * 128:(pp + 1) * 128]
            t1 = p.op("vector", lambda v, par=par, BGc=BGc: v.tensor_scalar(
                T1sb[par][:, :], pF[par][:, 256:512], BGc, None, ALU.mult), waits=[f3])
            B_pF[par].read(t1)
            rb = p.op("vector", lambda v, par=par, h=h, Bc=Bc: v.scalar_tensor_tensor(
                Rsb[par][:, :], vtok[:, h * 256:(h + 1) * 256], Bc, T1sb[par][:, :],
                ALU.mult, ALU.subtract), waits=[t1, vtok_ready])
            B_vtok.read(rb)
            yield
            mu = p.op("tensor", lambda t, MT=MT, par=par: t.matmul(
                pDW[par][:, 256:512], MT, Rsb[par][:, :], start=True, stop=True),
                waits=[rb, pv, wex] + B_pDW[par].wwaits())
            B_pDW[par].wrote(mu)
            ue = p.op("vector", lambda v, par=par: v.tensor_copy(Usb[par][:, :],
                                                                 pDW[par][:, 256:512]), waits=[mu])
            B_pDW[par].read(ue)
            yield
            o1 = p.op("tensor", lambda t, qT=qT, par=par, h=h: t.matmul(
                pF[par][:, 256:512], qT, Sbb[:, h * 256:(h + 1) * 256], start=True, stop=True),
                waits=[t1, rb, nbu, wbu])
            o2 = p.op("tensor", lambda t, par=par: t.matmul(
                pF[par][:, 256:512], Wsb[par][:, :], Usb[par][:, :], start=False, stop=True),
                waits=[ue])
            B_pF[par].wrote(o2)
            oev = p.op("vector", lambda v, par=par, h=h, Gc=Gc: v.tensor_scalar(
                otk[:, h * 256:(h + 1) * 256], pF[par][:, 256:512], Gc, None, ALU.mult),
                waits=[o2] + (B_otk.wwaits() if h < 2 else []))
            B_pF[par].read(oev)
            rec_guard[("oev", ci, h)] = oev
            yield
            du = p.op("vector", lambda v, par=par, dCc=dCc: v.tensor_scalar(
                dUsb[par][:, :], Usb[par][:, :], dCc, None, ALU.mult), waits=[ue, dc])
            ms = p.op("tensor", lambda t, par=par, h=h: t.matmul(
                pDW[par][:, 256:512], ktok[:, h * 128:(h + 1) * 128], dUsb[par][:, :],
                start=True, stop=True), waits=[du, ue, ktok_ready] + B_pDW[par].wwaits())
            B_pDW[par].wrote(ms)
            B_ktok.read(ms)
            su = p.op("vector", lambda v, par=par, h=h, gCc=gCc: v.scalar_tensor_tensor(
                Sm[:, h * 256:(h + 1) * 256], Sm[:, h * 256:(h + 1) * 256], gCc,
                pDW[par][:, 256:512], ALU.mult, ALU.add), waits=[ms] + ([zS] if ci == 0 else []))
            B_pDW[par].read(su)
            sb2 = p.op("vector", lambda v, h=h: v.tensor_copy(Sbb[:, h * 256:(h + 1) * 256],
                                                              Sm[:, h * 256:(h + 1) * 256]),
                       waits=[su])
            if h >= NH - 2:
                rec_guard[("otk_done", ci, h)] = oev

        for hp in range(NH // 2):
            gens = [rec_head(ci, 2 * hp), rec_head(ci, 2 * hp + 1)]
            alive = [True, True]
            while any(alive):
                for gi in range(2):
                    if alive[gi]:
                        try:
                            next(gens[gi])
                        except StopIteration:
                            alive[gi] = False
        oev = rec_guard[("oev", ci, NH - 1)]
        B_otk.wrote(oev)
        evo = None
        for bh in range(2):
            tro = None
            for vt in range(8 * bh, 8 * bh + 8):
                slot = vt % 8
                tw = (B_pTB.wwaits() if slot == 0 else [])
                if vt == 0:
                    tw = tw + [oev]
                tro = p.op("tensor", lambda t, vt=vt, slot=slot: t.transpose(
                    pTB[:, slot * 128:(slot + 1) * 128], otk[:, vt * 128:(vt + 1) * 128], IDB),
                    waits=tw, inc=(slot == 7))
            B_pTB.wrote(tro)
            for vt in range(8 * bh, 8 * bh + 8):
                slot = vt % 8
                evo = p.op("scalar", lambda s, vt=vt, slot=slot, ci=ci: s.copy(
                    oTa[:, vt * T + ci * 128: vt * T + ci * 128 + 128],
                    pTB[:, slot * 128:(slot + 1) * 128]), waits=([tro] if slot == 0 else []))
                B_pTB.read(evo)
        B_otk.read(evo)
        if ci % 4 == 3:
            blk_done[ci // 4] = evo

    if DBG:
        dq_tok = p.dma("gpsimd", dbg_q, qhT[:, :], waits=[blk_done[3]], sem="s_st")
        dk_tok = p.dma("gpsimd", dbg_k, khT[:, :], waits=[blk_done[3]], sem="s_st")
        do_tok = p.dma("gpsimd", dbg_o, oTa[:, :], waits=[blk_done[3]], sem="s_st")
        B_ybuf.read(dk_tok)
        dl_tok = p.dma("gpsimd", dbg_L.rearrange("p (a n) -> p a n", a=16)[:, 0, :], Larr,
                       waits=[blk_done[3]], sem="s_st")
        for nmi, arr in enumerate([nLarr, nBarr, Btarr, Garr, BGarr, dCarr, gCarr]):
            dl_tok = p.dma("gpsimd", dbg_L.rearrange("p (a n) -> p a n", a=16)[:, nmi + 1, :], arr,
                           waits=[blk_done[3]], sem="s_st")
    at_ar = []
    B_ws5f = [Buf(), Buf()]
    for blk in range(4):
        oready = blk_done[blk]
        for woct in range(16):
            b = woct % 2
            alias_w = (B_vbuf[0].wwaits() + B_vbuf[1].wwaits() + B_vtok.wwaits()
                       ) if (blk == 0 and woct < 2) else []
            wf = p.dma("sync", ws5f[b][:, :].rearrange("p (a n) -> p a n", a=16),
                       wo_in.rearrange("(a p) n -> p a n", p=128)[:, :, woct * 128:(woct + 1) * 128],
                       waits=B_ws5f[b].wwaits() + alias_w
                             + ([blk_done[3]] if (blk == 0 and woct < 2) else []),
                       sem=f"s_wf{b}")
            B_ws5f[b].wrote(wf)
            wt = p.op("scalar", lambda sc, b=b: sc.copy(wslab5[b][:, :], ws5f[b][:, :]),
                      waits=[wf] + B_wslab[b].wwaits())
            B_ws5f[b].read(wt)
            B_wslab[b].wrote(wt)
            bank = pG if woct % 2 == 0 else pF[0]
            bb = B_pG if woct % 2 == 0 else B_pF[0]
            mw = [wt, oready] + bb.wwaits()
            lm = None
            for vt in range(16):
                lm = p.op("tensor", lambda t, bank=bank, b=b, vt=vt, blk=blk: t.matmul(
                    bank[:, 0:512],
                    wslab5[b][:, vt * 128:(vt + 1) * 128],
                    oTa[:, vt * T + blk * 512: vt * T + (blk + 1) * 512],
                    start=(vt == 0), stop=(vt == 15)), waits=(mw if vt == 0 else []),
                    inc=(vt == 15))
            bb.wrote(lm)
            B_wslab[b].read(lm)
            ev = p.op("vector", lambda v, bank=bank, woct=woct: v.tensor_copy(
                ybuf[:, woct * 512:(woct + 1) * 512], bank[:, 0:512]),
                waits=[lm] + (B_ybuf.wwaits() if woct == 0 else []))
            bb.read(ev)
        B_ybuf.wrote(ev)
        ad = p.dma("gpsimd", cca_in[blk].rearrange("(p a) n -> p a n", p=128),
                   ybuf[:, :].rearrange("p (a n) -> p a n", a=4), waits=[ev], sem="s_st")
        B_ybuf.read(ad)
        ar = p.coll(lambda g, blk=blk: g.collective_compute(
            "AllReduce", ALU.add, replica_groups=GROUPS, ins=[cca_in[blk]], outs=[cca_out[blk]]),
            waits=[ad])
        at_ar.append(ar)
    fin = []
    B_outb = Buf()
    for blk in range(4):
        ld = p.dma("sync", ybuf[:, :].rearrange("p (a n) -> p a n", a=4),
                   cca_out[blk].rearrange("(p a) n -> p a n", p=128),
                   waits=[at_ar[blk], spill_done] + B_ybuf.wwaits(), sem="s_yo")
        B_ybuf.wrote(ld)
        if DBG:
            dat = p.dma("gpsimd", dbg_att[blk * 512:(blk + 1) * 512, :].rearrange(
                "(a p) n -> p a n", p=128), ybuf[:, :].rearrange("p (a n) -> p a n", a=4),
                waits=[ld], sem="s_st")
            B_ybuf.read(dat)
            fin.append(dat)
        for q in range(4):
            tt = blk * 4 + q
            bx = tt % 2
            sw = p.dma("sync", xbuf[bx][:, :], spill_d[tt * 128:(tt + 1) * 128, :],
                       waits=B_xbuf[bx].wwaits() + ([at_ar[0]] if tt < 2 else []), sem=f"s_x{bx}")
            B_xbuf[bx].wrote(sw)
            fa = None
            for half in range(2):
                ttr = None
                for k in range(8):
                    woct = half * 8 + k
                    tw = ([ld] + B_pTB.wwaits() + (B_outb.wwaits() if half == 0 else [])
                          ) if k == 0 else []
                    ttr = p.op("tensor", lambda t, woct=woct, k=k, q=q: t.transpose(
                        pTB[:, k * 128:(k + 1) * 128],
                        ybuf[:, woct * 512 + q * 128: woct * 512 + q * 128 + 128], IDB),
                        waits=tw, inc=(k == 7))
                B_pTB.wrote(ttr)
                B_ybuf.read(ttr)
                for k in range(8):
                    woct = half * 8 + k
                    fa = p.op("vector", lambda v, woct=woct, k=k, bx=bx: v.tensor_tensor(
                        outb[:, woct * 128:(woct + 1) * 128], pTB[:, k * 128:(k + 1) * 128],
                        xbuf[bx][:, woct * 128:(woct + 1) * 128], ALU.add),
                        waits=([ttr, sw] if k == 0 else []), inc=(k == 7))
                    B_pTB.read(fa)
            B_outb.wrote(fa)
            B_xbuf[bx].read(fa)
            tok = p.dma("gpsimd", y_out[tt * 128:(tt + 1) * 128, :], outb[:, :],
                        waits=[fa], sem="s_st")
            fin.append(tok)
            B_outb.read(tok)
            B_xbuf[bx].read(tok)
    if DBG:
        fin.append(p.dma("gpsimd", dbg_xn, xnT[:, :], waits=[xn_ready, blk_done[3]], sem="s_st"))


    p.op("gpsimd", lambda g: g.memset(stat[0:1, 40:41], 0.0), waits=fin, inc=False)

    with nc.Block() as block:
        p.assemble(block)
    ctx.close()
    return nc


_NC_CACHE = [None]


def _host_prep(inputs, core):
    role = core % 2
    batch = core // 2
    f32 = np.float32
    x = np.asarray(inputs["L0_attn_x"][batch])
    d = {}
    d["x_pi"] = np.ascontiguousarray(x[:, PIH], dtype=f32)
    cs = slice(role * NH * DK, (role + 1) * NH * DK)
    vs = slice(role * NH * VE, (role + 1) * NH * VE)
    d["wq"] = np.ascontiguousarray(np.asarray(inputs["L0_attn_Wq"])[PIH][:, cs], f32)
    d["wk"] = np.ascontiguousarray(np.asarray(inputs["L0_attn_Wk"])[PIH][:, cs], f32)
    d["wv"] = np.ascontiguousarray(np.asarray(inputs["L0_attn_Wv"])[PIH][:, vs], f32)
    wa = np.asarray(inputs["L0_attn_Wa"])[PIH][:, 8 * role:8 * role + 8]
    wb = np.asarray(inputs["L0_attn_Wb"])[PIH][:, 8 * role:8 * role + 8]
    wab = np.zeros((H, 40), f32)
    wab[:, 0:8] = wa
    wab[:, 32:40] = wb
    d["wab"] = wab
    wo = np.asarray(inputs["L0_attn_Wo"])[:, PIH]
    d["wo"] = np.ascontiguousarray(wo[role * 2048:(role + 1) * 2048], f32)
    for nm, wkey, bkey, sl in (("qconv", "L0_attn_qcw", "L0_attn_qcb", cs),
                               ("kconv", "L0_attn_kcw", "L0_attn_kcb", cs),
                               ("vconv", "L0_attn_vcw", "L0_attn_vcb", vs)):
        cw_ = np.asarray(inputs[wkey])[sl]
        cb_ = np.asarray(inputs[bkey])[sl]
        arr = np.zeros((cw_.shape[0], 8), f32)
        arr[:, 0:4] = cw_
        arr[:, 4] = cb_
        d[nm] = arr
    idx = _pi_i(role)
    ok = idx >= 0
    gup = np.asarray(inputs["L0_ffn_gate_up_packed"])
    gate_rows = np.zeros((I_PAD, gup.shape[1]), np.int32)
    gate_rows[ok] = gup[idx[ok]]
    up_rows = np.zeros((I_PAD, gup.shape[1]), np.int32)
    up_rows[ok] = gup[idx[ok] + I_FULL]
    rows = np.concatenate([gate_rows, up_rows], axis=0)
    tr = np.ascontiguousarray(rows.T)
    hw = tr.view(np.int16).reshape(gup.shape[1], 2 * I_PAD, 2)
    d["gupT_hw0"] = np.ascontiguousarray(hw[:, :, 0])
    d["gupT_hw1"] = np.ascontiguousarray(hw[:, :, 1])
    gsc = np.asarray(inputs["L0_ffn_gate_up_scales"]).astype(np.float16)
    sc_cols = np.zeros((16, 2 * I_PAD), np.float16)
    sc_cols[:, 0:I_PAD][:, ok] = gsc[:, idx[ok]]
    sc_cols[:, I_PAD:][:, ok] = gsc[:, idx[ok] + I_FULL]
    hgrp = (8 * (np.arange(H) % 256) + (np.arange(H) // 256)) // 128
    d["gup_scale"] = np.ascontiguousarray(sc_cols[hgrp])
    dp = np.asarray(inputs["L0_ffn_down_packed"])
    dpc = dp[:, 352 * role:352 * role + 352]
    dpt = np.zeros((384, H), np.int32)
    dpt[0:352] = np.ascontiguousarray(dpc.T)[:, PIH]
    hwd = np.ascontiguousarray(dpt).view(np.int16).reshape(384, H, 2)
    d["dwnT_hw0"] = np.ascontiguousarray(hwd[:, :, 0])
    d["dwnT_hw1"] = np.ascontiguousarray(hwd[:, :, 1])
    dsc = np.asarray(inputs["L0_ffn_down_scales"]).astype(np.float16)
    dsc_rows = np.zeros((I_PAD, H), np.float16)
    dsc_rows[ok] = dsc[(idx[ok] // 128)][:, PIH]
    d["dwn_scale"] = np.ascontiguousarray(dsc_rows)
    cf = np.zeros((128, 640), f32)
    cf[:, 0:128] = np.eye(128)
    cf[:, 128:256] = (np.arange(128)[:, None] <= np.arange(128)[None, :])
    cf[:, 256:384] = np.where(np.arange(128)[None, :] < np.arange(128)[:, None], 0.0, NEG_BIG)
    cf[:, 384:512] = np.where(np.arange(128)[:, None] <= np.arange(128)[None, :], 0.0, NEG_BIG)
    cf[0, 512:640] = 1.0
    d["consts_f"] = cf
    d["consts_bf"] = np.eye(128).astype(BF)
    return d


def kernel(**inputs):
    if _NC_CACHE[0] is None:
        _NC_CACHE[0] = build_program()
    nc = _NC_CACHE[0]
    in_maps = [_host_prep(inputs, core) for core in range(8)]
    res = run_bass_kernel_spmd(nc, in_maps, core_ids=list(range(8)))
    out = np.zeros((4, T, H), np.float32)
    inv = np.empty(H, dtype=np.int64)
    inv[PIH] = np.arange(H)
    for core in range(8):
        role, batch = core % 2, core // 2
        y = res.results[core]["y2"]
        out[batch, role * THALF:(role + 1) * THALF, :] = \
            y[role * THALF:(role + 1) * THALF][:, inv]
    return out


# version 19
# speedup vs baseline: 1.0316x; 1.0040x over previous
import numpy as np
import ml_dtypes
import concourse.bass as bass
import concourse.mybir as mybir
from concourse.bass_utils import run_bass_kernel_spmd

F32 = mybir.dt.float32
BF16 = mybir.dt.bfloat16
F16 = mybir.dt.float16
I16 = mybir.dt.int16
AF = mybir.ActivationFunctionType
ALU = mybir.AluOpType
AX = mybir.AxisListType
BF = ml_dtypes.bfloat16

H = 2048
T = 2048
DK = 128
VE = 256
NH = 8
I_FULL = 5632
I_PAD = 3072
C = 128
NCHUNK = T // C
THALF = T // 2
NEG_BIG = -1.0e30
GROUPS = [[0, 1], [2, 3], [4, 5], [6, 7]]

PIH = np.array([8 * j + cls for cls in range(8) for j in range(H // 8)], dtype=np.int64)
_DEBUG_OUT = [False]


def _pi_i(role):
    idx = np.full(I_PAD, -1, dtype=np.int64)
    k = 0
    for hw in range(2):
        for c2 in range(4):
            cls = 4 * hw + c2
            for j in range(352 * role, 352 * role + 352):
                idx[k] = 8 * j + cls
                k += 1
            k += 32
    return idx


class P:

    ESEM = {"tensor": "s_pe", "vector": "s_ve", "scalar": "s_se", "gpsimd": "s_ge",
            "sync": "s_sy"}

    def __init__(self, sems):
        self.sems = sems
        self.cnt = {n: 0 for n in sems}
        self.plan = {e: [] for e in ("sync", "tensor", "vector", "scalar", "gpsimd")}

    def _emit(self, engine, fn, waits, inc, inc_n):
        ws = []
        seen = {}
        for t in waits:
            if t is None:
                continue
            s, v = t
            if seen.get(s, -1) < v:
                seen[s] = v
        ws = tuple(seen.items())
        self.plan[engine].append((fn, ws, inc, inc_n))
        if inc is not None:
            self.cnt[inc] += inc_n
            return (inc, self.cnt[inc])
        return None

    def dma(self, engine, dst, src, waits=(), sem=None):
        assert sem is not None
        return self._emit(engine, lambda e, dst=dst, src=src: e.dma_start(out=dst, in_=src),
                          waits=waits, inc=sem, inc_n=16)

    def op(self, engine, fn, waits=(), inc=True):
        sem = self.ESEM[engine] if inc else None
        return self._emit(engine, fn, waits=waits, inc=sem, inc_n=1)

    def coll(self, fn, waits=()):
        return self._emit("gpsimd", fn, waits=waits, inc="s_coll", inc_n=1)

    def assemble(self, block):
        for engine in self.plan:
            def make(engine):
                def body(eng):
                    for fn, waits, inc, inc_n in self.plan[engine]:
                        for (s, v) in waits:
                            eng.wait_ge(self.sems[s], v)
                        ins = fn(eng)
                        if inc is not None:
                            ins.then_inc(self.sems[inc], inc_n)
                return body
            getattr(block, engine)(make(engine))


class Buf:

    def __init__(self):
        self.writer = None
        self.readers = []

    def wwaits(self):
        w = list(self.readers)
        if self.writer is not None:
            w.append(self.writer)
        return w

    def wrote(self, tok):
        self.writer = tok
        self.readers = []

    def rwaits(self):
        return [self.writer] if self.writer is not None else []

    def read(self, tok):
        if tok is not None:
            self.readers.append(tok)


_DEFER = [True]


def build_program():
    nc = bass.Bass("TRN2", target_bir_lowering=False, debug=False)
    dt = nc.dram_tensor

    x_in = dt("x_pi", [T, H], F32, kind="ExternalInput").ap()
    wq_in = dt("wq", [H, NH * DK], F32, kind="ExternalInput").ap()
    wk_in = dt("wk", [H, NH * DK], F32, kind="ExternalInput").ap()
    wv_in = dt("wv", [H, NH * VE], F32, kind="ExternalInput").ap()
    wab_in = dt("wab", [H, 40], F32, kind="ExternalInput").ap()
    wo_in = dt("wo", [2048, H], F32, kind="ExternalInput").ap()
    qc_in = dt("qconv", [NH * DK, 8], F32, kind="ExternalInput").ap()
    kc_in = dt("kconv", [NH * DK, 8], F32, kind="ExternalInput").ap()
    vc_in = dt("vconv", [NH * VE, 8], F32, kind="ExternalInput").ap()
    gup0_in = dt("gupT_hw0", [H // 8, 2 * I_PAD], I16, kind="ExternalInput").ap()
    gup1_in = dt("gupT_hw1", [H // 8, 2 * I_PAD], I16, kind="ExternalInput").ap()
    gsc_in = dt("gup_scale", [H, 2 * I_PAD], F16, kind="ExternalInput").ap()
    dwn0_in = dt("dwnT_hw0", [384, H], I16, kind="ExternalInput").ap()
    dwn1_in = dt("dwnT_hw1", [384, H], I16, kind="ExternalInput").ap()
    dsc_in = dt("dwn_scale", [I_PAD, H], F16, kind="ExternalInput").ap()
    cst_in = dt("consts_f", [128, 640], F32, kind="ExternalInput").ap()
    cbf_in = dt("consts_bf", [128, 128], BF16, kind="ExternalInput").ap()

    y_out = dt("y2", [T, H], F32, kind="ExternalOutput").ap()
    DBG = bool(globals().get("_DEBUG_OUT", [False])[0]) if False else _DEBUG_OUT[0]
    if DBG:
        dbg_xn = dt("dbg_xn", [128, 16 * T], BF16, kind="ExternalOutput").ap()
        dbg_q = dt("dbg_q", [128, NH * T], BF16, kind="ExternalOutput").ap()
        dbg_k = dt("dbg_k", [128, NH * T], BF16, kind="ExternalOutput").ap()
        dbg_v = dt("dbg_v", [NH * VE, T], BF16, kind="ExternalOutput").ap()
        dbg_sp = dt("dbg_sp", [T, H], F32, kind="ExternalOutput").ap()
        dbg_o = dt("dbg_o", [128, 16 * T], BF16, kind="ExternalOutput").ap()
        dbg_L = dt("dbg_L", [128, 16 * NCHUNK * NH], F32, kind="ExternalOutput").ap()
        dbg_att = dt("dbg_att", [T, H], BF16, kind="ExternalOutput").ap()
        dbg_xn2 = dt("dbg_xn2", [T, H], BF16, kind="ExternalOutput").ap()
        dbg_z = dt("dbg_z", [128, 2052], F32, kind="ExternalOutput").ap()
        dbg_h = dt("dbg_h", [128, 24 * 512], BF16, kind="ExternalOutput").ap()
        dbg_w = dt("dbg_w", [128, 24 * 512], BF16, kind="ExternalOutput").ap()

    wgu_d = dt("wgu_d", [H, 2 * I_PAD], BF16, kind="Internal").ap()
    dwn_d = dt("dwn_d", [I_PAD, H], BF16, kind="Internal").ap()
    vt_d = dt("vt_d", [NH * VE, T], BF16, kind="Internal").ap()
    spill_d = dt("spill_d", [T, H], F32, kind="Internal").ap()
    ccy_in = [dt(f"ccy_in{i}", [512, H], BF16, kind="Internal").ap() for i in range(4)]
    ccy_out = [dt(f"ccy_out{i}", [512, H], BF16, kind="Internal").ap() for i in range(4)]
    cca_in = [dt(f"cca_in{i}", [512, H], BF16, kind="Internal").ap() for i in range(4)]
    cca_out = [dt(f"cca_out{i}", [512, H], BF16, kind="Internal").ap() for i in range(4)]

    sbt = nc.sbuf_tensor
    pst = nc.psum_tensor
    import contextlib
    ctx = contextlib.ExitStack()

    cstf = ctx.enter_context(sbt("cstf", [128, 640], F32))
    cstb = ctx.enter_context(sbt("cstb", [128, 128], BF16))
    IDF = cstf[:, 0:128]
    TRI = cstf[:, 128:256]
    MSK_S = cstf[:, 256:384]
    MSK_IT = cstf[:, 384:512]
    ONER = cstf[0:1, 512:640]
    IDB = cstb[:, :]
    stat = ctx.enter_context(sbt("stat", [128, 64], F32))
    cw = ctx.enter_context(sbt("cw", [128, 16], F32))

    U8 = mybir.dt.uint8
    A_ar = ctx.enter_context(sbt("A_ar", [128, 65536], U8))
    B_ar = ctx.enter_context(sbt("B_ar", [128, 32768], U8))
    C_ar = ctx.enter_context(sbt("C_ar", [128, 32768], U8))
    D_ar = ctx.enter_context(sbt("D_ar", [128, 78336], U8))

    def cv(ar, off, nel, dtp):
        sz = {F32: 4, BF16: 2, F16: 2, I16: 2}[dtp]
        return ar[:, off:off + nel * sz].bitcast(dtp)

    xnT = cv(A_ar, 0, 16 * T, BF16)
    oTa = cv(A_ar, 0, 16 * T, BF16)
    qhT = cv(B_ar, 0, NH * T, BF16)
    khT = cv(C_ar, 0, NH * T, BF16)
    dq_in = [cv(B_ar, 0, 3 * 1024, I16), cv(B_ar, 6144, 3 * 1024, I16)]
    dq_sc = [cv(B_ar, 12288, 16 * 512, F16), cv(C_ar, 0, 16 * 512, F16)]
    dq_t2 = cv(B_ar, 28672, 512, I16)
    dq_n2 = cv(B_ar, 29696, 512, I16)
    harena = cv(B_ar, 0, 24 * 512, BF16)
    ybuf = cv(C_ar, 0, 4 * H, BF16)
    xbuf = [cv(D_ar, 0, H, F32), cv(D_ar, 8192, H, F32)]
    sc1 = cv(D_ar, 16384, H, F32)
    sc2 = cv(D_ar, 24576, H, F32)
    Z = 32768
    dq_o = [cv(D_ar, Z, 16 * 512, BF16), cv(D_ar, Z + 16384, 16 * 512, BF16)]
    dq_o_d = cv(D_ar, Z, 24 * 512, BF16)
    dq_t = cv(B_ar, 30720, 512, I16)
    dq_n = cv(B_ar, 31744, 512, I16)
    dq_in_d = cv(D_ar, 24576, 3 * 1024, I16)
    dq_sc_d = cv(C_ar, 0, 12 * 512, F16)

    wslab2 = [cv(C_ar, 16384, 16 * 128, BF16), cv(C_ar, 20480, 16 * 128, BF16)]
    dslab = [cv(D_ar, Z + 8192, 24 * 512, BF16), cv(D_ar, Z + 8192, 24 * 512, BF16)]
    cvb = cv(D_ar, Z, 2052, BF16)
    cva = cv(D_ar, Z + 4160, 2048, BF16)
    wslab3 = [cv(D_ar, Z + 8320, 16 * 128, BF16), cv(D_ar, Z + 12416, 16 * 128, BF16)]
    ab_sb = cv(D_ar, 0, T, F32)[0:40, :]
    lab_sb = cv(D_ar, 8192, T, F32)[0:40, :]
    abT_sb = cv(D_ar, Z + 16512, 40, F32)
    cvb2 = cv(D_ar, Z + 16704, 2052, BF16)
    cva2 = cv(D_ar, Z + 20864, 2048, BF16)
    vstg = cv(D_ar, Z + 24960, 2048, BF16)
    vbuf = [cv(D_ar, 0, 16 * 128, BF16), cv(D_ar, 4096, 16 * 128, BF16)]
    vtok = cv(D_ar, 8192, 16 * 128, BF16)
    ktok = cv(D_ar, 12288, NH * 128, BF16)
    otk = cv(D_ar, 14336, NH * 256, BF16)
    Sm = cv(D_ar, 18432, NH * 256, F32)
    Sbb = cv(D_ar, 26624, NH * 256, BF16)
    MOFF = Z
    Nsb, NTsb, Dsb, Wsb, Xsb, XTsb, PTsb, Rsb, Usb, dUsb, T1sb = ([], [], [], [], [], [],
                                                                  [], [], [], [], [])
    LCbs = []
    mo = MOFF
    for i in range(2):
        Nsb.append(cv(D_ar, mo, 128, BF16)); mo += 256
        NTsb.append(cv(D_ar, mo, 128, BF16)); mo += 256
        Dsb.append(cv(D_ar, mo, 128, F32)); mo += 512
        Wsb.append(cv(D_ar, mo, 128, BF16)); mo += 256
        Xsb.append(cv(D_ar, mo, 11 * 128, BF16)); mo += 2816
        PTsb.append(cv(D_ar, mo, 2 * 128, BF16)); mo += 512
        Rsb.append(cv(D_ar, mo, 256, BF16)); mo += 512
        Usb.append(cv(D_ar, mo, 256, BF16)); mo += 512
        dUsb.append(cv(D_ar, mo, 256, BF16)); mo += 512
        T1sb.append(cv(D_ar, mo, 256, F32)); mo += 1024
        LCbs.append(cv(D_ar, mo, 8, F32)); mo += 64
    assert mo <= Z + 32768, mo
    wslab5 = [cv(D_ar, Z + 16384, 16 * 128, BF16), cv(D_ar, Z + 20480, 16 * 128, BF16)]
    ws5f = [cv(C_ar, 16384, 16 * 128, F32), cv(C_ar, 24576, 16 * 128, F32)]
    outb = cv(D_ar, Z + 24576, 2048, F32)
    TAIL = Z + 32768
    Larr = cv(D_ar, TAIL, NCHUNK * NH, F32)
    nLarr = cv(D_ar, TAIL + 512, NCHUNK * NH, F32)
    nBarr = cv(D_ar, TAIL + 1024, NCHUNK * NH, F32)
    Btarr = cv(D_ar, TAIL + 1536, NCHUNK * NH, F32)
    Garr = cv(D_ar, TAIL + 2048, NCHUNK * NH, F32)
    BGarr = cv(D_ar, TAIL + 2560, NCHUNK * NH, F32)
    dCarr = cv(D_ar, TAIL + 3072, NCHUNK * NH, F32)
    gCarr = cv(D_ar, TAIL + 3584, NCHUNK * NH, F32)
    LrowS = [cv(D_ar, TAIL + 4096, 128, F32)[0:1, :], cv(D_ar, TAIL + 4608, 128, F32)[0:1, :]]
    nLrowS = [cv(D_ar, TAIL + 5120, 128, F32)[0:1, :], cv(D_ar, TAIL + 5632, 128, F32)[0:1, :]]
    assert TAIL + 6144 <= 78336

    pF = [ctx.enter_context(pst(f"pF{i}", [128, 512], F32)) for i in range(2)]
    pDW = [ctx.enter_context(pst(f"pDW{i}", [128, 512], F32)) for i in range(2)]
    pXP = [ctx.enter_context(pst(f"pXP{i}", [128, 512], F32)) for i in range(2)]
    pTB = ctx.enter_context(pst("pTB", [128, 1024], BF16))
    pG = ctx.enter_context(pst("pG", [128, 512], F32))
    PTF = pTB[:, :].bitcast(F32)

    sem_names = ["s_cst", "s_x0", "s_x1", "s_dqi0", "s_dqi1", "s_dqo", "s_ws0", "s_ws1",
                 "s_ds", "s_cw0", "s_cw1", "s_wf0", "s_wf1", "s_hm", "s_yo", "s_vl0", "s_vl1", "s_sp", "s_vt", "s_st",
                 "s_pe", "s_ve", "s_se", "s_ge", "s_coll"]
    sems = {n: ctx.enter_context(nc.semaphore(n)) for n in sem_names}
    p = P(sems)

    B_pF = [Buf(), Buf()]
    B_pDW = [Buf(), Buf()]
    B_pXP = [Buf(), Buf()]
    B_pTB = Buf()
    B_pG = Buf()
    B_wslab = [Buf(), Buf()]
    B_dslab = [Buf(), Buf()]
    B_har = Buf()
    B_ybuf = Buf()
    B_xbuf = [Buf(), Buf()]
    B_sc1 = Buf()
    B_sc2 = Buf()
    B_cv = Buf()
    B_vbuf = [Buf(), Buf()]
    B_vtok = Buf()
    B_ktok = Buf()
    B_otk = Buf()
    B_dqin = [Buf(), Buf()]
    B_cwb = Buf()
    B_cw2 = Buf()
    B_dqo = [Buf(), Buf()]
    B_dqo_d = Buf()

    def dequant_block(kind, ib, b, in_toks, ntile_pt, nout_tiles, dram, sc_tile_fn,
                      obuf=None, oB=None, inbuf=None, inB=None, hw1_hook=None,
                      sc_read_cb=None, filler=None):
        obuf = dq_o[b] if obuf is None else obuf
        oB = B_dqo[b] if oB is None else oB
        inbuf = dq_in[b] if inbuf is None else inbuf
        inB = B_dqin[b] if inB is None else inB
        first_w = in_toks + oB.wwaits()
        last = None
        last_v = last_g = None
        hw1w = []
        for hw in range(2):
            eng = "vector"
            myt = dq_t if hw == 0 else dq_t2
            myn = dq_n if hw == 0 else dq_n2
            if hw == 1 and hw1_hook is not None:
                hw1w = hw1_hook()
            for c2 in range(4):
                cls = 4 * hw + c2
                sa = 12 - 4 * c2
                sbs = 9 - 4 * c2
                for pt in range(ntile_pt):
                    src = inbuf[:, (hw * ntile_pt + pt) * 512:(hw * ntile_pt + pt + 1) * 512]
                    src32 = src.bitcast(mybir.dt.int32)
                    t32 = myt[:, :].bitcast(mybir.dt.int32)
                    n32 = myn[:, :].bitcast(mybir.dt.int32)
                    fw = (first_w if hw == 0 else hw1w) if (c2 == 0 and pt == 0) else []
                    if sa > 0:
                        p.op(eng, lambda v, src32=src32, sa=sa, t32=t32: v.tensor_scalar(
                            t32, src32, sa, 0x80008000 - 2**32, ALU.logical_shift_left,
                            ALU.bitwise_and), waits=fw, inc=False)
                    else:
                        p.op(eng, lambda v, src32=src32, t32=t32: v.tensor_scalar(
                            t32, src32, 0x80008000 - 2**32, None, ALU.bitwise_and),
                            waits=fw, inc=False)
                    if sbs > 0:
                        p.op(eng, lambda v, src32=src32, sbs=sbs, n32=n32: v.tensor_scalar(
                            n32, src32, sbs, 0x0E000E00, ALU.logical_shift_left,
                            ALU.bitwise_and), inc=False)
                    else:
                        p.op(eng, lambda v, src32=src32, sbs=sbs, n32=n32: v.tensor_scalar(
                            n32, src32, -sbs, 0x0E000E00, ALU.logical_shift_right,
                            ALU.bitwise_and), inc=False)
                    p.op(eng, lambda v, n32=n32, t32=t32: v.tensor_tensor(
                        n32, n32, t32, ALU.bitwise_or), inc=False)
                    tile = cls * ntile_pt + pt
                    dst = obuf[:, tile * 512:(tile + 1) * 512]
                    scs = sc_tile_fn(tile)
                    tokx = p.op(eng, lambda v, dst=dst, scs=scs, myn=myn: v.scalar_tensor_tensor(
                        dst, myn[:, :].bitcast(F16), 16384.0, scs, ALU.mult, ALU.mult))
                    if sc_read_cb is not None:
                        sc_read_cb(tokx)
                    if filler is not None:
                        filler()
                    if hw == 0:
                        last_v = tokx
                    else:
                        last_g = tokx
        inB.read(last_v)
        inB.read(last_g)
        oB.wrote(last_v)
        oB.wrote(last_g)
        last = last_g
        tok = p.dma("gpsimd", dram, obuf[:, 0:nout_tiles * 512].rearrange("p (a n) -> p a n", a=nout_tiles),
                    waits=[last_v, last_g], sem="s_dqo")
        oB.read(tok)
        if DBG and kind == "g" and ib == 0:
            tok = p.dma("gpsimd", dbg_w.rearrange("p (a n) -> p a n", a=24),
                        obuf[:, :].rearrange("p (a n) -> p a n", a=24), waits=[last], sem="s_dqo")
            oB.read(tok)
        return tok

    NBLK_G = 2 * I_PAD // 512
    dq_g_blk = {}
    dq_d_blk = {}
    _gorder = [x for pair in zip(range(6), range(6, 12)) for x in pair]
    _dq_b = [0]
    _gorder = [x for pair in zip(range(6), range(6, 12)) for x in pair]
    DQ_AFTER_TILE = {}

    def load_gate(ib, b):
        dw = B_dqin[b].wwaits()
        w0 = p.dma("gpsimd", dq_in[b][:, 0:1024].rearrange("p (a n) -> p a n", a=2),
                   gup0_in.rearrange("(a p) n -> p a n", p=128)[:, :, ib * 512:(ib + 1) * 512],
                   waits=dw, sem=f"s_dqi{b}")
        w1 = p.dma("gpsimd", dq_in[b][:, 1024:2048].rearrange("p (a n) -> p a n", a=2),
                   gup1_in.rearrange("(a p) n -> p a n", p=128)[:, :, ib * 512:(ib + 1) * 512],
                   sem=f"s_dqi{b}")
        wsc = p.dma("gpsimd", dq_sc[b][:, :].rearrange("p (a n) -> p a n", a=16),
                    gsc_in.rearrange("(a p) n -> p a n", p=128)[:, :, ib * 512:(ib + 1) * 512],
                    sem=f"s_dqi{b}")
        B_dqin[b].wrote(wsc)
        return [w0, w1, wsc]

    def ops_gate(ib, b, toks):
        tok = dequant_block(
            "g", ib, b, toks, 2, 16,
            wgu_d.rearrange("(a p) n -> p a n", p=128)[:, :, ib * 512:(ib + 1) * 512],
            lambda tile, b=b: dq_sc[b][:, tile * 512:(tile + 1) * 512])
        dq_g_blk[ib] = tok

    B_dqind = Buf()
    B_dscd = Buf()

    def emit_dq_gates():
        pend = {}
        for i, ib in enumerate(_gorder):
            if i == 0:
                pend[0] = load_gate(_gorder[0], 0)
                pend[1] = load_gate(_gorder[1], 1)
            ops_gate(ib, i % 2, pend.pop(i))
            if i + 2 < len(_gorder):
                pend[i + 2] = load_gate(_gorder[i + 2], (i + 2) % 2)

    def emit_dq_downs(fillers):
        B_dqo_d.read(dq_g_blk[5])
        B_dqo_d.read(dq_g_blk[11])
        fq = list(fillers)
        fi = [0]
        cnt = [0]

        def filler():
            cnt[0] += 1
            if cnt[0] % 4 == 0 and fi[0] < len(fq):
                fq[fi[0]]()
                fi[0] += 1

        def load_in(ib):
            dwi = B_dqind.wwaits()
            w0 = p.dma("gpsimd", dq_in_d[:, 0:3 * 512].rearrange("p (a n) -> p a n", a=3),
                       dwn0_in.rearrange("(a p) n -> p a n", p=128)[:, :, ib * 512:(ib + 1) * 512],
                       waits=dwi, sem="s_dqi0")
            w1 = p.dma("gpsimd", dq_in_d[:, 3 * 512:6 * 512].rearrange("p (a n) -> p a n", a=3),
                       dwn1_in.rearrange("(a p) n -> p a n", p=128)[:, :, ib * 512:(ib + 1) * 512],
                       sem="s_dqi0")
            B_dqind.wrote(w1)
            return [w0, w1]

        pend_in = load_in(0)
        for ib in range(4):
            sc0 = p.dma("gpsimd", dq_sc_d[:, :].rearrange("p (a n) -> p a n", a=12),
                        dsc_in.rearrange("(a p) n -> p a n", p=128)[:, 0:12, ib * 512:(ib + 1) * 512],
                        waits=B_dscd.wwaits() + (B_dqin[1].wwaits() if ib == 0 else []),
                        sem="s_dqi1")
            B_dscd.wrote(sc0)

            def hw1_hook(ib=ib):
                tok = p.dma("gpsimd", dq_sc_d[:, :].rearrange("p (a n) -> p a n", a=12),
                            dsc_in.rearrange("(a p) n -> p a n", p=128)[:, 12:24, ib * 512:(ib + 1) * 512],
                            waits=B_dscd.wwaits(), sem="s_dqi1")
                B_dscd.wrote(tok)
                return [tok]
            tok = dequant_block(
                "d", ib, 0, pend_in + [sc0], 3, 24,
                dwn_d.rearrange("(a p) n -> p a n", p=128)[:, :, ib * 512:(ib + 1) * 512],
                lambda tile: dq_sc_d[:, (tile % 12) * 512:(tile % 12 + 1) * 512],
                obuf=dq_o_d, oB=B_dqo_d, inbuf=dq_in_d, inB=B_dqind,
                hw1_hook=hw1_hook, sc_read_cb=B_dscd.read, filler=filler)
            dq_d_blk[ib] = tok
            if ib + 1 < 4:
                pend_in = load_in(ib + 1)
        while fi[0] < len(fq):
            fq[fi[0]]()
            fi[0] += 1

    t_cst = p.dma("sync", cstf[:, :], cst_in, sem="s_cst")
    t_cstb = p.dma("sync", cstb[:, :], cbf_in, sem="s_cst")
    x_tok = []
    xn_tile_tok = []

    def emit_x_dma(tt):
        b = tt % 2
        tok = p.dma("sync", xbuf[b][:, :], x_in[tt * 128:(tt + 1) * 128, :],
                    waits=B_xbuf[b].wwaits(), sem=f"s_x{b}")
        B_xbuf[b].wrote(tok)
        x_tok.append(tok)
    emit_x_dma(0)
    emit_x_dma(1)

    evac_last = None
    for tt in range(16):
        b = tt % 2
        xb = xbuf[b]
        r1 = p.op("vector", lambda v, xb=xb: v.reduce_sum(stat[:, 0:1], xb[:, :], axis=AX.X),
                  waits=[x_tok[tt]])
        r2 = p.op("vector", lambda v: v.tensor_scalar(stat[:, 1:2], stat[:, 0:1], 1.0 / H, None, ALU.mult),
                  waits=[r1])
        t_sc1 = p.op("vector", lambda v, xb=xb: v.tensor_scalar(sc1[:, :], xb[:, :], stat[:, 1:2], None, ALU.subtract),
                     waits=[r2] + B_sc1.wwaits())
        B_sc1.wrote(t_sc1)
        tsq = p.op("scalar", lambda s: s.activation(sc2[:, :], sc1[:, :], AF.Square),
                   waits=B_sc2.wwaits() + B_sc1.rwaits())
        B_sc1.read(tsq)
        B_sc2.wrote(tsq)
        r3 = p.op("vector", lambda v: v.reduce_sum(stat[:, 2:3], sc2[:, :], axis=AX.X), waits=[tsq])
        v4 = p.op("vector", lambda v: v.tensor_scalar(stat[:, 3:4], stat[:, 2:3], 1.0 / H, 1e-5, ALU.mult, ALU.add),
                  waits=[r3])
        tsd = p.op("scalar", lambda s: s.activation(stat[:, 4:5], stat[:, 3:4], AF.Sqrt), waits=[v4])
        vrc = p.op("vector", lambda v: v.reciprocal(stat[:, 5:6], stat[:, 4:5]), waits=[tsd])
        tln = p.op("vector", lambda v: v.tensor_scalar(sc2[:, 0:1024].bitcast(BF16)[:, 0:H], sc1[:, :],
                                                       stat[:, 5:6], None, ALU.mult),
                   waits=[tsq, vrc] + B_sc2.wwaits())
        B_sc2.wrote(tln)
        B_xbuf[b].read(tln)
        B_sc1.read(tln)
        if tt + 2 < 16:
            emit_x_dma(tt + 2)
        if DBG:
            dtk = p.dma("gpsimd", dbg_xn2[tt * 128:(tt + 1) * 128, :],
                        sc2[:, 0:1024].bitcast(BF16)[:, 0:H], waits=[tln], sem="s_st")
            B_sc2.read(dtk)
        for bh in range(2):
            ttr = None
            for hh in range(8 * bh, 8 * bh + 8):
                slot = hh % 8
                tw = ([tln, t_cstb] + B_pTB.wwaits()) if slot == 0 else []
                ttr = p.op("tensor", lambda t, hh=hh, slot=slot: t.transpose(
                    pTB[:, slot * 128:(slot + 1) * 128],
                    sc2[:, 0:1024].bitcast(BF16)[:, hh * 128:(hh + 1) * 128], IDB), waits=tw,
                    inc=(slot == 7))
            B_pTB.wrote(ttr)
            for hh in range(8 * bh, 8 * bh + 8):
                slot = hh % 8
                ev = p.op("scalar", lambda s, hh=hh, slot=slot, tt=tt: s.copy(
                    xnT[:, hh * T + tt * 128: hh * T + tt * 128 + 128],
                    pTB[:, slot * 128:(slot + 1) * 128]), waits=([ttr] if slot == 0 else []))
                B_pTB.read(ev)
            evac_last = ev
        B_sc2.read(evac_last)
        xn_tile_tok.append(evac_last)
        if tt in DQ_AFTER_TILE:
            for _ib in DQ_AFTER_TILE[tt]:
                emit_dq_gate(_ib)
    xn_ready = evac_last
    xn_tc = [xn_tile_tok[3], xn_tile_tok[7], xn_tile_tok[11], xn_tile_tok[15]]
    emit_dq_gates()

    y_ar = []
    ws3_pre = {}
    hm_fill = []
    B_ws2 = [Buf(), Buf()]

    def ffn_gates(tc):
        defer = (tc == 0) and _DEFER[0]
        hm_last = None
        for ipair in range(24):
            mm_done = []
            for which in range(2):
                it = ipair + 24 * which
                b = (ipair * 2 + which) % 2
                wt = p.dma("sync", wslab2[b][:, :].rearrange("p (a n) -> p a n", a=16),
                           wgu_d.rearrange("(a p) n -> p a n", p=128)[:, :, it * 128:(it + 1) * 128],
                           waits=[dq_g_blk[it // 4 if it < 24 else 6 + (it - 24) // 4]]
                                 + B_ws2[b].wwaits(), sem=f"s_ws{b}")
                B_ws2[b].wrote(wt)
                if defer:
                    slot = (2 * ipair + which) % 4
                    bank = (pF[0], pF[1], pXP[0], pXP[1])[slot]
                    bb = (B_pF[0], B_pF[1], B_pXP[0], B_pXP[1])[slot]
                else:
                    bank = pF[which]
                    bb = B_pF[which]
                mw = [wt, xn_tc[tc]] + bb.wwaits()
                if defer and which == 1 and ipair > 1:
                    mw.append(("s_hm", ipair - 1))
                lm = None
                for ht in range(16):
                    lm = p.op("tensor", lambda t, bank=bank, b=b, ht=ht, tc=tc: t.matmul(
                        bank[:, 0:512], wslab2[b][:, ht * 128:(ht + 1) * 128],
                        xnT[:, ht * T + tc * 512: ht * T + (tc + 1) * 512],
                        start=(ht == 0), stop=(ht == 15)), waits=(mw if ht == 0 else []),
                        inc=(ht == 15))
                bb.wrote(lm)
                B_ws2[b].read(lm)
                mm_done.append((lm, bank, bb))
            sil_w = [mm_done[0][0]] + B_sc1.wwaits()
            if defer and ipair > 0:
                sil_w.append(("s_hm", ipair))
            gbank, gbb = mm_done[0][1], mm_done[0][2]
            tsil = p.op("scalar", lambda s, gbank=gbank: s.activation(
                sc1[:, 0:512], gbank[:, 0:512], AF.Silu), waits=sil_w)
            gbb.read(tsil)
            B_sc1.wrote(tsil)
            ubank, ubb = mm_done[1][1], mm_done[1][2]
            if defer:
                def mk(ipair=ipair, tsil=tsil, mm1=mm_done[1][0], ubank=ubank, ubb=ubb,
                       harw=tuple(B_har.wwaits() if ipair == 0 else [])):
                    def emit():
                        w = [tsil, mm1] + list(harw)
                        tok = p._emit("vector", lambda v, ipair=ipair, ubank=ubank: v.tensor_tensor(
                            harena[:, ipair * 512:(ipair + 1) * 512], sc1[:, 0:512],
                            ubank[:, 0:512], ALU.mult), waits=w, inc="s_hm", inc_n=1)
                        ubb.read(tok)
                        B_sc1.read(tok)
                    return emit
                hm_fill.append(mk())
                hm_last = ("s_hm", ipair + 1)
            else:
                hm = p.op("vector", lambda v, ipair=ipair, ubank=ubank: v.tensor_tensor(
                    harena[:, ipair * 512:(ipair + 1) * 512], sc1[:, 0:512], ubank[:, 0:512],
                    ALU.mult), waits=[tsil, mm_done[1][0]] + (B_har.wwaits() if ipair == 0 else []))
                ubb.read(hm)
                B_sc1.read(hm)
                hm_last = hm
        B_har.wrote(hm_last)
        return hm_last

    def ffn_down(tc, hm_last):
        ev_last = None
        for hc in range(4):
            b = 0
            wt = p.dma("sync", dslab[b][:, :].rearrange("p (a n) -> p a n", a=24),
                       dwn_d.rearrange("(a p) n -> p a n", p=128)[:, :, hc * 512:(hc + 1) * 512],
                       waits=[dq_d_blk[3], hm_last] + B_dslab[0].wwaits(), sem="s_ds")
            B_dslab[0].wrote(wt)
            for tt4 in range(4):
                bank = pDW[tt4 % 2]
                mw = [wt, hm_last] + B_pDW[tt4 % 2].wwaits()
                lm = None
                for it in range(24):
                    lm = p.op("tensor", lambda t, bank=bank, it=it, tt4=tt4, b=b: t.matmul(
                        bank[:, 0:512],
                        harena[:, it * 512 + tt4 * 128: it * 512 + tt4 * 128 + 128],
                        dslab[b][:, it * 512:(it + 1) * 512],
                        start=(it == 0), stop=(it == 23)), waits=(mw if it == 0 else []),
                        inc=(it == 23))
                B_pDW[tt4 % 2].wrote(lm)
                B_dslab[0].read(lm)
                B_har.read(lm)
                ev = p.op("vector", lambda v, bank=bank, tt4=tt4, hc=hc: v.tensor_copy(
                    ybuf[:, tt4 * H + hc * 512: tt4 * H + (hc + 1) * 512], bank[:, 0:512]),
                    waits=[lm] + (B_ybuf.wwaits() if (hc == 0 and tt4 == 0) else []))
                B_pDW[tt4 % 2].read(ev)
                ev_last = ev
        B_ybuf.wrote(ev_last)
        if tc == 3:
            for _nm, _sl, _wi in (("q", 0, wq_in), ("v", 1, wv_in)):
                ws3_pre[(_nm, 0)] = p.dma(
                    "gpsimd", wslab3[_sl][:, :].rearrange("p (a n) -> p a n", a=16),
                    _wi.rearrange("(a p) n -> p a n", p=128)[:, :, 0:128],
                    waits=B_dslab[0].wwaits(), sem=f"s_ws{_sl}")
        yd = p.dma("gpsimd", ccy_in[tc].rearrange("(a p) n -> p a n", p=128),
                   ybuf[:, :].rearrange("p (a n) -> p a n", a=4), waits=[ev_last], sem="s_st")
        B_ybuf.read(yd)
        ar = p.coll(lambda g, tc=tc: g.collective_compute(
            "AllReduce", ALU.add, replica_groups=GROUPS, ins=[ccy_in[tc]], outs=[ccy_out[tc]]),
            waits=[yd])
        y_ar.append(ar)

    if _DEFER[0]:
        ffn_gates(0)
        emit_dq_downs(hm_fill)
        ffn_down(0, ("s_hm", 24))
        for tc in range(1, 4):
            hm_last_t = ffn_gates(tc)
            ffn_down(tc, hm_last_t)
    else:
        emit_dq_downs([])
        for tc in range(4):
            hm_last_t = ffn_gates(tc)
            ffn_down(tc, hm_last_t)

    sp_tok = {}
    for blk in range(4):
        ld = p.dma("sync", ybuf[:, :].rearrange("p (a n) -> p a n", a=4),
                   ccy_out[blk].rearrange("(a p) n -> p a n", p=128),
                   waits=[y_ar[blk]] + B_ybuf.wwaits(), sem="s_yo")
        B_ybuf.wrote(ld)
        for q in range(4):
            tt = blk * 4 + q
            bx = tt % 2
            xw = p.dma("sync", xbuf[bx][:, :], x_in[tt * 128:(tt + 1) * 128, :],
                       waits=B_xbuf[bx].wwaits(), sem=f"s_x{bx}")
            B_xbuf[bx].wrote(xw)
            yb = ybuf[:, q * H:(q + 1) * H]
            t1 = p.op("vector", lambda v, yb=yb: v.tensor_tensor(sc1[:, :], yb, yb, ALU.mult),
                      waits=[ld] + B_sc1.wwaits())
            B_sc1.wrote(t1)
            r8 = p.op("vector", lambda v: v.reduce_sum(stat[:, 8:9], sc1[:, :], axis=AX.X), waits=[t1])
            v4b = p.op("vector", lambda v: v.tensor_scalar(stat[:, 9:10], stat[:, 8:9], 1.0 / H, 1e-6,
                                                           ALU.mult, ALU.add), waits=[r8])
            tsd = p.op("scalar", lambda s: s.activation(stat[:, 10:11], stat[:, 9:10], AF.Sqrt),
                       waits=[t1, v4b])
            vrc2 = p.op("vector", lambda v: v.reciprocal(stat[:, 11:12], stat[:, 10:11]), waits=[tsd])
            t2 = p.op("vector", lambda v, yb=yb: v.tensor_scalar(sc2[:, :], yb, stat[:, 11:12], None,
                                                                 ALU.mult), waits=[vrc2] + B_sc2.wwaits())
            B_sc2.wrote(t2)
            tadd = p.op("vector", lambda v, bx=bx: v.tensor_tensor(sc2[:, :], sc2[:, :],
                                                                   xbuf[bx][:, :], ALU.add),
                        waits=[xw])
            B_sc2.wrote(tadd)
            B_xbuf[bx].read(tadd)
            tok = p.dma("gpsimd", spill_d[tt * 128:(tt + 1) * 128, :], sc2[:, :],
                        waits=[tadd], sem="s_sp")
            B_sc2.read(tok)
            if DBG:
                tok = p.dma("gpsimd", dbg_sp[tt * 128:(tt + 1) * 128, :], sc2[:, :],
                            waits=[tadd], sem="s_sp")
                B_sc2.read(tok)
            B_ybuf.read(tok)
            sp_tok[tt] = tok
    spill_done = sp_tok[15]

    vdma_tok = {}
    qk_store = {"q": qhT, "k": khT}
    PROJ = {"q": (wq_in, qc_in), "k": (wk_in, kc_in), "v": (wv_in, vc_in)}
    B_cv2 = Buf()
    B_vstg = Buf()

    def proj_ct(name, ct, slot):
        w_in, c_in = PROJ[name]
        b = slot
        banks = (pG, pF[0]) if slot == 0 else (pDW[0], pDW[1])
        bbs = (B_pG, B_pF[0]) if slot == 0 else (B_pDW[0], B_pDW[1])
        mycvb, mycva = (cvb, cva) if slot == 0 else (cvb2, cva2)
        B_mycv = B_cv if slot == 0 else B_cv2
        B_mycw = B_cwb if slot == 0 else B_cw2
        wt = ws3_pre.pop((name, ct), None)
        if wt is None:
            wt = p.dma("gpsimd", wslab3[b][:, :].rearrange("p (a n) -> p a n", a=16),
                       w_in.rearrange("(a p) n -> p a n", p=128)[:, :, ct * 128:(ct + 1) * 128],
                       waits=B_wslab[b].wwaits() + (B_dslab[0].wwaits() if ct < 2 else []),
                       sem=f"s_ws{b}")
        B_wslab[b].wrote(wt)
        cwt = p.dma("sync", cw[:, 8 * slot:8 * slot + 8], c_in[ct * 128:(ct + 1) * 128, :],
                    waits=B_mycw.wwaits(), sem=f"s_cw{slot}")
        B_mycw.wrote(cwt)
        evs = []
        for tc in range(4):
            bank = banks[tc % 2]
            bb = bbs[tc % 2]
            mw = [wt, xn_ready] + bb.wwaits()
            lm = None
            for ht in range(16):
                lm = p.op("tensor", lambda t, bank=bank, b=b, ht=ht, tc=tc: t.matmul(
                    bank[:, 0:512], wslab3[b][:, ht * 128:(ht + 1) * 128],
                    xnT[:, ht * T + tc * 512: ht * T + (tc + 1) * 512],
                    start=(ht == 0), stop=(ht == 15)), waits=(mw if ht == 0 else []),
                    inc=(ht == 15))
            bb.wrote(lm)
            B_wslab[b].read(lm)
            evc = p.op("scalar", lambda s, bank=bank, tc=tc, mycvb=mycvb: s.copy(
                mycvb[:, 3 + tc * 512: 3 + (tc + 1) * 512], bank[:, 0:512]),
                waits=[lm] + (B_mycv.wwaits() if tc == 0 else []))
            bb.read(evc)
            evs.append(evc)
            yield
        B_mycv.wrote(evs[-1])
        p.op("vector", lambda v, mycvb=mycvb: v.memset(mycvb[:, 0:3], 0.0),
             waits=[evs[-1], cwt], inc=False)
        p.op("vector", lambda v, mycvb=mycvb, mycva=mycva, slot=slot: v.tensor_scalar(
            mycva[:, :], mycvb[:, 0:2048], cw[:, 8 * slot:8 * slot + 1], None, ALU.mult), inc=False)
        for j in (1, 2):
            p.op("vector", lambda v, j=j, mycvb=mycvb, mycva=mycva, slot=slot: v.scalar_tensor_tensor(
                mycva[:, :], mycvb[:, j:j + 2048], cw[:, 8 * slot + j:8 * slot + j + 1],
                mycva[:, :], ALU.mult, ALU.add), inc=False)
        tconv = p.op("vector", lambda v, mycvb=mycvb, mycva=mycva, slot=slot: v.scalar_tensor_tensor(
            mycva[:, :], mycvb[:, 3:3 + 2048], cw[:, 8 * slot + 3:8 * slot + 4],
            mycva[:, :], ALU.mult, ALU.add))
        B_mycv.read(tconv)
        B_mycw.read(tconv)
        yield
        if name == "v":
            tsil = p.op("scalar", lambda s: s.activation(vstg[:, :], cva2[:, :], AF.Silu,
                                                         bias=cw[:, 12:13], scale=1.0),
                        waits=[tconv] + B_vstg.wwaits())
            B_mycw.read(tsil)
            B_vstg.wrote(tsil)
            tok = p.dma("gpsimd", vt_d[ct * 128:(ct + 1) * 128, :], vstg[:, :],
                        waits=[tsil], sem="s_vt")
            B_vstg.read(tok)
            if DBG:
                tok = p.dma("gpsimd", dbg_v[ct * 128:(ct + 1) * 128, :], vstg[:, :],
                            waits=[tsil], sem="s_vt")
                B_vstg.read(tok)
            vdma_tok[ct] = tok
            yield
            return
        tsil = p.op("scalar", lambda s: s.activation(sc1[:, :], cva[:, :], AF.Silu,
                                                     bias=cw[:, 4:5], scale=1.0),
                    waits=[tconv] + B_sc1.wwaits())
        B_sc1.wrote(tsil)
        B_mycw.read(tsil)
        store = qk_store[name]
        for tt4 in range(4):
            bank = pXP[tt4 % 2]
            bb = B_pXP[tt4 % 2]
            ltr = None
            for k4 in range(4):
                ttt = tt4 * 4 + k4
                trw = ([tsil] if (tt4 == 0 and k4 == 0) else []) + (bb.wwaits() if k4 == 0 else [])
                ltr = p.op("tensor", lambda t, ttt=ttt, k4=k4, bank=bank: t.transpose(
                    bank[:, k4 * 128:(k4 + 1) * 128],
                    sc1[:, ttt * 128:(ttt + 1) * 128], IDF), waits=trw,
                    inc=(k4 == 3))
            bb.wrote(ltr)
            B_sc1.read(ltr)
            sq = p.op("scalar", lambda s, bank=bank: s.activation(
                sc2[:, 0:512], bank[:, 0:512], AF.Square),
                waits=[ltr] + B_sc2.wwaits())
            B_sc2.wrote(sq)
            rs = p.op("vector", lambda v: v.reduce_sum(
                stat[:, 16:20], sc2[:, 0:512].rearrange("p (a n) -> p a n", a=4), axis=AX.X),
                waits=[sq])
            B_sc2.read(rs)
            vr = p.op("vector", lambda v: v.reciprocal(stat[:, 20:24], stat[:, 16:20]),
                      waits=[rs])
            sr = p.op("scalar", lambda s: s.activation(stat[:, 24:28], stat[:, 20:24], AF.Sqrt),
                      waits=[rs, vr])
            tmn = p.op("vector", lambda v: v.tensor_scalar_min(stat[:, 28:32], stat[:, 24:28], 1e12),
                       waits=[sr])
            nm = None
            for k4 in range(4):
                nm = p.op("vector", lambda v, k4=k4, bank=bank: v.tensor_scalar(
                    sc2[:, 512 + k4 * 64:512 + (k4 + 1) * 64].bitcast(BF16)[:, 0:128],
                    bank[:, k4 * 128:(k4 + 1) * 128],
                    stat[:, 28 + k4:29 + k4], None, ALU.mult),
                    waits=([tmn] if k4 == 0 else []), inc=(k4 == 3))
                bb.read(nm)
            yield
            tb = None
            for k4 in range(4):
                slotp = k4 + (4 if tt4 % 2 else 0)
                tb = p.op("tensor", lambda t, k4=k4, slotp=slotp: t.transpose(
                    pTB[:, slotp * 128:(slotp + 1) * 128],
                    sc2[:, 512 + k4 * 64:512 + (k4 + 1) * 64].bitcast(BF16)[:, 0:128],
                    IDB), waits=([nm] + B_pTB.wwaits()) if k4 == 0 else [],
                    inc=(k4 == 3))
            B_pTB.wrote(tb)
            for k4 in range(4):
                slotp = k4 + (4 if tt4 % 2 else 0)
                ttt = tt4 * 4 + k4
                aw = ((B_ybuf.wwaits() + B_wslab[0].wwaits() + B_wslab[1].wwaits())
                      if (name == "k" and ct == 0 and tt4 == 0 and k4 == 0) else [])
                ev = p.op("scalar", lambda s, slotp=slotp, store=store, ct=ct, ttt=ttt: s.copy(
                    store[:, ct * T + ttt * 128: ct * T + ttt * 128 + 128],
                    pTB[:, slotp * 128:(slotp + 1) * 128]),
                    waits=([tb] if k4 == 0 else []) + aw)
                B_pTB.read(ev)
                B_sc2.read(ev)
            B_sc1.read(ev)
            GD[0] = ev
            yield

    GD = [None]
    vdma_tok = {}
    qk_store = {"q": qhT, "k": khT}
    plan_slots = [("q", list(range(8))), ("k", list(range(8)))]
    qk_seq = [("q", c) for c in range(8)] + [("k", c) for c in range(8)]
    v_seq = [("v", c) for c in range(16)]
    g0 = g1 = None
    i0 = i1 = 0
    while i0 < len(qk_seq) or i1 < len(v_seq) or g0 is not None or g1 is not None:
        if g0 is None and i0 < len(qk_seq):
            nm0, c0 = qk_seq[i0]
            g0 = proj_ct(nm0, c0, 0)
            i0 += 1
        if g1 is None and i1 < len(v_seq):
            nm1, c1 = v_seq[i1]
            g1 = proj_ct(nm1, c1, 1)
            i1 += 1
        for gref in (0, 1):
            g = g0 if gref == 0 else g1
            if g is not None:
                try:
                    next(g)
                except StopIteration:
                    if gref == 0:
                        g0 = None
                    else:
                        g1 = None
    qk_done = GD[0]

    wt = p.dma("gpsimd", wslab3[0][:, 0:16 * 40].rearrange("p (a n) -> p a n", a=16),
               wab_in.rearrange("(a p) n -> p a n", p=128), waits=B_wslab[0].wwaits(), sem="s_ws0")
    B_wslab[0].wrote(wt)
    for tc in range(4):
        mw = [wt, xn_ready] + B_pG.wwaits()
        lm = None
        for ht in range(16):
            lm = p.op("tensor", lambda t, ht=ht, tc=tc: t.matmul(
                pG[0:40, 0:512], wslab3[0][:, ht * 40:(ht + 1) * 40],
                xnT[:, ht * T + tc * 512: ht * T + (tc + 1) * 512],
                start=(ht == 0), stop=(ht == 15)), waits=(mw if ht == 0 else []), inc=(ht == 15))
        B_pG.wrote(lm)
        B_wslab[0].read(lm)
        ev = p.op("scalar", lambda s, tc=tc: s.copy(ab_sb[0:40, tc * 512:(tc + 1) * 512],
                                                    pG[0:40, 0:512]),
                  waits=[lm] + (B_xbuf[0].wwaits() if tc == 0 else []))
        B_pG.read(ev)
    e1 = p.op("scalar", lambda s: s.activation(lab_sb[0:40, :], ab_sb[0:40, :], AF.Exp, scale=-1.0),
              waits=[ev] + B_xbuf[1].wwaits())
    e2 = p.op("vector", lambda v: v.tensor_scalar(lab_sb[0:40, :], lab_sb[0:40, :], 1.0, None, ALU.add),
              waits=[e1])
    e3 = p.op("scalar", lambda s: s.activation(lab_sb[0:8, :], lab_sb[0:8, :], AF.Ln), waits=[e2])
    e4 = p.op("vector", lambda v: v.tensor_scalar(lab_sb[0:8, :], lab_sb[0:8, :], -1.0, None, ALU.mult),
              waits=[e3])
    e5 = p.op("vector", lambda v: v.reciprocal(lab_sb[32:40, :], lab_sb[32:40, :]), waits=[e3])
    prev_dc = None
    for ci in range(NCHUNK):
        tr = p.op("tensor", lambda t, ci=ci: t.transpose(
            PTF[:, 0:40], lab_sb[0:40, ci * 128:(ci + 1) * 128], IDF[0:40, 0:40]),
            waits=[e4, e5] + B_pTB.wwaits())
        B_pTB.wrote(tr)
        evt = p.op("scalar", lambda s: s.copy(abT_sb[:, 0:40], PTF[:, 0:40]), waits=[tr])
        B_pTB.read(evt)
        btk = p.op("vector", lambda v, ci=ci: v.tensor_copy(Btarr[:, ci * NH:(ci + 1) * NH],
                                                            abT_sb[:, 32:40]), waits=[evt])
        nb = p.op("vector", lambda v, ci=ci: v.tensor_scalar(
            nBarr[:, ci * NH:(ci + 1) * NH], abT_sb[:, 32:40], -1.0, None, ALU.mult))
        tm = p.op("tensor", lambda t: t.matmul(pG[:, 0:8], TRI, abT_sb[:, 0:8],
                                               start=True, stop=True),
                  waits=[evt] + B_pG.wwaits())
        B_pG.wrote(tm)
        evl = p.op("scalar", lambda s, ci=ci: s.copy(Larr[:, ci * NH:(ci + 1) * NH],
                                                     pG[:, 0:8]), waits=[tm])
        B_pG.read(evl)
        nl = p.op("vector", lambda v, ci=ci: v.tensor_scalar(
            nLarr[:, ci * NH:(ci + 1) * NH], Larr[:, ci * NH:(ci + 1) * NH], -1.0, None, ALU.mult),
            waits=[evl])
        gm = p.op("scalar", lambda s, ci=ci: s.activation(Garr[:, ci * NH:(ci + 1) * NH],
                                                          Larr[:, ci * NH:(ci + 1) * NH], AF.Exp),
                  waits=[evl])
        p.op("vector", lambda v, ci=ci: v.tensor_tensor(
            BGarr[:, ci * NH:(ci + 1) * NH], Garr[:, ci * NH:(ci + 1) * NH],
            Btarr[:, ci * NH:(ci + 1) * NH], ALU.mult), waits=[gm, btk], inc=False)
        prev_dc = gm if nl is None else nl
        prev_dc = gm
    ab_ready = prev_dc

    zS = p.op("vector", lambda v: v.memset(Sm[:, :], 0.0), waits=[ab_ready])
    zS2 = p.op("vector", lambda v: v.memset(Sbb[:, :], 0.0))
    blk_done = {}
    rec_guard = {}
    for ci in range(NCHUNK):
        b = ci % 2
        vw = (([vdma_tok[vt] for vt in range(16)] + [ab_ready] + B_cv.wwaits())
              if ci == 0 else []) + B_vbuf[b].wwaits()
        vt_ld = p.dma("sync", vbuf[b][:, :].rearrange("p (a n) -> p a n", a=16),
                      vt_d.rearrange("(a p) n -> p a n", p=128)[:, :, ci * 128:(ci + 1) * 128],
                      waits=vw, sem=f"s_vl{b}")
        B_vbuf[b].wrote(vt_ld)
        ev = None
        for bh in range(2):
            trv = None
            for vt in range(8 * bh, 8 * bh + 8):
                slot = vt % 8
                tw = ([vt_ld, zS2, qk_done] + B_pTB.wwaits() + B_vtok.wwaits()) if vt == 0 else \
                     (B_pTB.wwaits() if vt == 8 else [])
                trv = p.op("tensor", lambda t, vt=vt, slot=slot, b=b: t.transpose(
                    pTB[:, slot * 128:(slot + 1) * 128], vbuf[b][:, vt * 128:(vt + 1) * 128], IDB),
                    waits=tw, inc=(slot == 7))
            B_pTB.wrote(trv)
            for vt in range(8 * bh, 8 * bh + 8):
                slot = vt % 8
                ev = p.op("scalar", lambda s, vt=vt, slot=slot: s.copy(
                    vtok[:, vt * 128:(vt + 1) * 128], pTB[:, slot * 128:(slot + 1) * 128]),
                    waits=([trv] if slot == 0 else []))
                B_pTB.read(ev)
        B_vbuf[b].read(ev)
        B_vtok.wrote(ev)
        vtok_ready = ev
        trk = None
        for h in range(NH):
            tw = (B_pTB.wwaits() + B_ktok.wwaits()) if h == 0 else []
            trk = p.op("tensor", lambda t, h=h, ci=ci: t.transpose(
                pTB[:, h * 128:(h + 1) * 128], khT[:, h * T + ci * 128: h * T + ci * 128 + 128],
                IDB), waits=tw, inc=(h == NH - 1))
        B_pTB.wrote(trk)
        ev = None
        for h in range(NH):
            ev = p.op("scalar", lambda s, h=h: s.copy(
                ktok[:, h * 128:(h + 1) * 128], pTB[:, h * 128:(h + 1) * 128]),
                waits=([trk] if h == 0 else []))
            B_pTB.read(ev)
        B_ktok.wrote(ev)
        ktok_ready = ev
        def rec_head(ci, h):
            par = h % 2
            cs = ci * NH + h
            Lc = Larr[:, cs:cs + 1]
            nLc = nLarr[:, cs:cs + 1]
            nBc = nBarr[:, cs:cs + 1]
            Bc = Btarr[:, cs:cs + 1]
            Gc = Garr[:, cs:cs + 1]
            BGc = BGarr[:, cs:cs + 1]
            dCc = dCarr[:, cs:cs + 1]
            gCc = gCarr[:, cs:cs + 1]
            kT = khT[:, h * T + ci * 128: h * T + ci * 128 + 128]
            qT = qhT[:, h * T + ci * 128: h * T + ci * 128 + 128]
            trL = p.op("tensor", lambda t, cs=cs: t.transpose(
                PTF[0:1, 384:512], Larr[:, cs:cs + 1], IDF), waits=B_pTB.wwaits())
            B_pTB.wrote(trL)
            evL = p.op("scalar", lambda s, par=par: s.copy(LrowS[par][0:1, :], PTF[0:1, 384:512]),
                       waits=[trL])
            B_pTB.read(evL)
            nLr = p.op("scalar", lambda s, par=par: s.mul(nLrowS[par][0:1, :], LrowS[par][0:1, :],
                                                          -1.0), waits=[evL])
            m1 = p.op("tensor", lambda t, par=par: t.matmul(
                pG[:, par * 2:par * 2 + 1], ONER, LrowS[par][0:1, 127:128], start=True, stop=True),
                waits=[evL] + B_pG.wwaits())
            B_pG.wrote(m1)
            gc = p.op("scalar", lambda s, cs=cs, par=par: s.activation(
                gCarr[:, cs:cs + 1], pG[:, par * 2:par * 2 + 1], AF.Exp), waits=[m1])
            lcb = p.op("scalar", lambda s, par=par: s.copy(LCbs[par][:, 0:1], pG[:, par * 2:par * 2 + 1]),
                       waits=[m1])
            B_pG.read(gc)
            B_pG.read(lcb)
            dc = p.op("scalar", lambda s, cs=cs, par=par: s.activation(
                dCarr[:, cs:cs + 1], Larr[:, cs:cs + 1], AF.Exp, bias=LCbs[par][:, 0:1],
                scale=-1.0), waits=[lcb])
            yield
            f1 = p.op("tensor", lambda t, kT=kT, par=par: t.matmul(
                pF[par][:, 0:128], kT, kT, start=True, stop=True),
                waits=B_pF[par].wwaits() + ([ktok_ready, vtok_ready] if h < 2 else []))
            p.op("tensor", lambda t, kT=kT, qT=qT, par=par: t.matmul(
                pF[par][:, 128:256], kT, qT, start=True, stop=True), inc=False)
            f3 = p.op("tensor", lambda t, kT=kT, par=par, h=h: t.matmul(
                pF[par][:, 256:512], kT, Sbb[:, h * 256:(h + 1) * 256], start=True, stop=True))
            B_pF[par].wrote(f3)
            yield
            d1 = p.op("tensor", lambda t, par=par: t.matmul(
                pDW[par][:, 0:128], ONER, nLrowS[par][0:1, :],
                start=True, stop=True), waits=[nLr] + B_pDW[par].wwaits())
            d2 = p.op("tensor", lambda t, par=par: t.matmul(
                pDW[par][:, 0:128], IDF, MSK_S, start=False, stop=True))
            dex = p.op("scalar", lambda s, par=par, Lc=Lc: s.activation(
                Dsb[par][:, :], pDW[par][:, 0:128], AF.Exp, bias=Lc, scale=1.0), waits=[d2])
            nbu = p.op("vector", lambda v, par=par, nBc=nBc: v.scalar_tensor_tensor(
                Nsb[par][:, :], pF[par][:, 0:128], nBc, Dsb[par][:, :], ALU.mult, ALU.mult),
                waits=[dex, f3])
            B_pF[par].read(nbu)
            yield
            d3 = p.op("tensor", lambda t, par=par: t.matmul(
                pDW[par][:, 128:256], ONER, LrowS[par][0:1, :],
                start=True, stop=True), waits=[dex, evL])
            d4 = p.op("tensor", lambda t, par=par: t.matmul(
                pDW[par][:, 128:256], IDF, MSK_IT, start=False, stop=True))
            wex = p.op("scalar", lambda s, par=par, nLc=nLc: s.activation(
                Dsb[par][:, :], pDW[par][:, 128:256], AF.Exp, bias=nLc, scale=1.0), waits=[d4])
            wbu = p.op("vector", lambda v, par=par: v.tensor_tensor(
                Wsb[par][:, :], pF[par][:, 128:256], Dsb[par][:, :], ALU.mult), waits=[wex])
            B_pF[par].read(wbu)
            B_pDW[par].wrote(d4)
            B_pDW[par].read(wex)
            yield
            nt1 = p.op("tensor", lambda t, par=par: t.transpose(
                pTB[:, 0:128], Nsb[par][:, :], IDB), waits=[nbu] + B_pTB.wwaits())
            B_pTB.wrote(nt1)
            nt2 = p.op("scalar", lambda s, par=par: s.copy(NTsb[par][:, :], pTB[:, 0:128]),
                       waits=[nt1])
            B_pTB.read(nt2)
            pv = p.op("vector", lambda v, par=par: v.tensor_tensor(
                PTsb[par][:, 0:128], NTsb[par][:, :], IDB, ALU.add), waits=[nt2])
            yield
            xsrc = {1: (Nsb[par][:, :], NTsb[par][:, :])}
            prev_ev = None
            for li, j in enumerate([1, 2, 4, 8, 16]):
                Xj, XTj = xsrc[j]
                m = p.op("tensor", lambda t, Xj=Xj, XTj=XTj, par=par: t.matmul(
                    pXP[par][:, 0:128], XTj, Xj, start=True, stop=True),
                    waits=([nt2] if li == 0 else [prev_ev]) + (B_pXP[par].wwaits() if li == 0 else []),
                    inc=False)
                m2 = p.op("tensor", lambda t, Xj=Xj, XTj=XTj, par=par: t.matmul(
                    pXP[par][:, 128:256], Xj, XTj, start=True, stop=True))
                prev_ev = p.op("vector", lambda v, par=par, li=li: v.tensor_copy(
                    Xsb[par][:, li * 256:(li + 1) * 256], pXP[par][:, 0:256]), waits=[m2])
                xsrc[2 * j] = (Xsb[par][:, li * 256:li * 256 + 128],
                               Xsb[par][:, li * 256 + 128:(li + 1) * 256])
                yield
            X32, XT32 = xsrc[32]
            m = p.op("tensor", lambda t, X32=X32, XT32=XT32, par=par: t.matmul(
                pXP[par][:, 0:128], XT32, X32, start=True, stop=True), waits=[prev_ev])
            xev = p.op("vector", lambda v, par=par: v.tensor_copy(
                Xsb[par][:, 5 * 256:5 * 256 + 128], pXP[par][:, 0:128]), waits=[m])
            xsrc[64] = (Xsb[par][:, 5 * 256:5 * 256 + 128], None)
            yield
            pp = 0
            for li, j in enumerate([2, 4, 8, 16, 32, 64]):
                src = PTsb[par][:, pp * 128:(pp + 1) * 128]
                dstp = 1 - pp
                Xj = Xsb[par][:, (li) * 256:(li) * 256 + 128] if j != 64 else xsrc[64][0]
                m = p.op("tensor", lambda t, par=par, Xj=Xj, src=src: t.matmul(
                    pXP[par][:, 256:384], Xj, src, start=True, stop=True),
                    waits=[pv] + ([xev] if li == 0 else []))
                pv = p.op("vector", lambda v, par=par, dstp=dstp, src=src: v.scalar_tensor_tensor(
                    PTsb[par][:, dstp * 128:(dstp + 1) * 128], pXP[par][:, 256:384], 1.0, src,
                    ALU.mult, ALU.add), waits=[m])
                pp = dstp
                yield
            B_pXP[par].wrote(m)
            B_pXP[par].read(pv)
            MT = PTsb[par][:, pp * 128:(pp + 1) * 128]
            t1 = p.op("vector", lambda v, par=par, BGc=BGc: v.tensor_scalar(
                T1sb[par][:, :], pF[par][:, 256:512], BGc, None, ALU.mult), waits=[f3])
            B_pF[par].read(t1)
            rb = p.op("vector", lambda v, par=par, h=h, Bc=Bc: v.scalar_tensor_tensor(
                Rsb[par][:, :], vtok[:, h * 256:(h + 1) * 256], Bc, T1sb[par][:, :],
                ALU.mult, ALU.subtract), waits=[t1, vtok_ready])
            B_vtok.read(rb)
            yield
            mu = p.op("tensor", lambda t, MT=MT, par=par: t.matmul(
                pDW[par][:, 256:512], MT, Rsb[par][:, :], start=True, stop=True),
                waits=[rb, pv, wex] + B_pDW[par].wwaits())
            B_pDW[par].wrote(mu)
            ue = p.op("vector", lambda v, par=par: v.tensor_copy(Usb[par][:, :],
                                                                 pDW[par][:, 256:512]), waits=[mu])
            B_pDW[par].read(ue)
            yield
            o1 = p.op("tensor", lambda t, qT=qT, par=par, h=h: t.matmul(
                pF[par][:, 256:512], qT, Sbb[:, h * 256:(h + 1) * 256], start=True, stop=True),
                waits=[t1, rb, nbu, wbu])
            o2 = p.op("tensor", lambda t, par=par: t.matmul(
                pF[par][:, 256:512], Wsb[par][:, :], Usb[par][:, :], start=False, stop=True),
                waits=[ue])
            B_pF[par].wrote(o2)
            oev = p.op("vector", lambda v, par=par, h=h, Gc=Gc: v.tensor_scalar(
                otk[:, h * 256:(h + 1) * 256], pF[par][:, 256:512], Gc, None, ALU.mult),
                waits=[o2] + (B_otk.wwaits() if h < 2 else []))
            B_pF[par].read(oev)
            rec_guard[("oev", ci, h)] = oev
            yield
            du = p.op("vector", lambda v, par=par, dCc=dCc: v.tensor_scalar(
                dUsb[par][:, :], Usb[par][:, :], dCc, None, ALU.mult), waits=[ue, dc])
            ms = p.op("tensor", lambda t, par=par, h=h: t.matmul(
                pDW[par][:, 256:512], ktok[:, h * 128:(h + 1) * 128], dUsb[par][:, :],
                start=True, stop=True), waits=[du, ue, ktok_ready] + B_pDW[par].wwaits())
            B_pDW[par].wrote(ms)
            B_ktok.read(ms)
            su = p.op("vector", lambda v, par=par, h=h, gCc=gCc: v.scalar_tensor_tensor(
                Sm[:, h * 256:(h + 1) * 256], Sm[:, h * 256:(h + 1) * 256], gCc,
                pDW[par][:, 256:512], ALU.mult, ALU.add), waits=[ms] + ([zS] if ci == 0 else []))
            B_pDW[par].read(su)
            sb2 = p.op("vector", lambda v, h=h: v.tensor_copy(Sbb[:, h * 256:(h + 1) * 256],
                                                              Sm[:, h * 256:(h + 1) * 256]),
                       waits=[su])
            if h >= NH - 2:
                rec_guard[("otk_done", ci, h)] = oev

        for hp in range(NH // 2):
            gens = [rec_head(ci, 2 * hp), rec_head(ci, 2 * hp + 1)]
            alive = [True, True]
            while any(alive):
                for gi in range(2):
                    if alive[gi]:
                        try:
                            next(gens[gi])
                        except StopIteration:
                            alive[gi] = False
        oev = rec_guard[("oev", ci, NH - 1)]
        B_otk.wrote(oev)
        evo = None
        for bh in range(2):
            tro = None
            for vt in range(8 * bh, 8 * bh + 8):
                slot = vt % 8
                tw = (B_pTB.wwaits() if slot == 0 else [])
                if vt == 0:
                    tw = tw + [oev]
                tro = p.op("tensor", lambda t, vt=vt, slot=slot: t.transpose(
                    pTB[:, slot * 128:(slot + 1) * 128], otk[:, vt * 128:(vt + 1) * 128], IDB),
                    waits=tw, inc=(slot == 7))
            B_pTB.wrote(tro)
            for vt in range(8 * bh, 8 * bh + 8):
                slot = vt % 8
                evo = p.op("scalar", lambda s, vt=vt, slot=slot, ci=ci: s.copy(
                    oTa[:, vt * T + ci * 128: vt * T + ci * 128 + 128],
                    pTB[:, slot * 128:(slot + 1) * 128]), waits=([tro] if slot == 0 else []))
                B_pTB.read(evo)
        B_otk.read(evo)
        if ci % 4 == 3:
            blk_done[ci // 4] = evo

    if DBG:
        dq_tok = p.dma("gpsimd", dbg_q, qhT[:, :], waits=[blk_done[3]], sem="s_st")
        dk_tok = p.dma("gpsimd", dbg_k, khT[:, :], waits=[blk_done[3]], sem="s_st")
        do_tok = p.dma("gpsimd", dbg_o, oTa[:, :], waits=[blk_done[3]], sem="s_st")
        B_ybuf.read(dk_tok)
        dl_tok = p.dma("gpsimd", dbg_L.rearrange("p (a n) -> p a n", a=16)[:, 0, :], Larr,
                       waits=[blk_done[3]], sem="s_st")
        for nmi, arr in enumerate([nLarr, nBarr, Btarr, Garr, BGarr, dCarr, gCarr]):
            dl_tok = p.dma("gpsimd", dbg_L.rearrange("p (a n) -> p a n", a=16)[:, nmi + 1, :], arr,
                           waits=[blk_done[3]], sem="s_st")
    at_ar = []
    B_ws5f = [Buf(), Buf()]
    for blk in range(4):
        oready = blk_done[blk]
        for woct in range(16):
            b = woct % 2
            alias_w = (B_vbuf[0].wwaits() + B_vbuf[1].wwaits() + B_vtok.wwaits()
                       ) if (blk == 0 and woct < 2) else []
            wf = p.dma("sync", ws5f[b][:, :].rearrange("p (a n) -> p a n", a=16),
                       wo_in.rearrange("(a p) n -> p a n", p=128)[:, :, woct * 128:(woct + 1) * 128],
                       waits=B_ws5f[b].wwaits() + alias_w
                             + ([blk_done[3]] if (blk == 0 and woct < 2) else []),
                       sem=f"s_wf{b}")
            B_ws5f[b].wrote(wf)
            wt = p.op("scalar", lambda sc, b=b: sc.copy(wslab5[b][:, :], ws5f[b][:, :]),
                      waits=[wf] + B_wslab[b].wwaits())
            B_ws5f[b].read(wt)
            B_wslab[b].wrote(wt)
            bank = pG if woct % 2 == 0 else pF[0]
            bb = B_pG if woct % 2 == 0 else B_pF[0]
            mw = [wt, oready] + bb.wwaits()
            lm = None
            for vt in range(16):
                lm = p.op("tensor", lambda t, bank=bank, b=b, vt=vt, blk=blk: t.matmul(
                    bank[:, 0:512],
                    wslab5[b][:, vt * 128:(vt + 1) * 128],
                    oTa[:, vt * T + blk * 512: vt * T + (blk + 1) * 512],
                    start=(vt == 0), stop=(vt == 15)), waits=(mw if vt == 0 else []),
                    inc=(vt == 15))
            bb.wrote(lm)
            B_wslab[b].read(lm)
            ev = p.op("vector", lambda v, bank=bank, woct=woct: v.tensor_copy(
                ybuf[:, woct * 512:(woct + 1) * 512], bank[:, 0:512]),
                waits=[lm] + (B_ybuf.wwaits() if woct == 0 else []))
            bb.read(ev)
        B_ybuf.wrote(ev)
        ad = p.dma("gpsimd", cca_in[blk].rearrange("(p a) n -> p a n", p=128),
                   ybuf[:, :].rearrange("p (a n) -> p a n", a=4), waits=[ev], sem="s_st")
        B_ybuf.read(ad)
        ar = p.coll(lambda g, blk=blk: g.collective_compute(
            "AllReduce", ALU.add, replica_groups=GROUPS, ins=[cca_in[blk]], outs=[cca_out[blk]]),
            waits=[ad])
        at_ar.append(ar)
    fin = []
    B_outb = Buf()
    for blk in range(4):
        ld = p.dma("sync", ybuf[:, :].rearrange("p (a n) -> p a n", a=4),
                   cca_out[blk].rearrange("(p a) n -> p a n", p=128),
                   waits=[at_ar[blk], spill_done] + B_ybuf.wwaits(), sem="s_yo")
        B_ybuf.wrote(ld)
        if DBG:
            dat = p.dma("gpsimd", dbg_att[blk * 512:(blk + 1) * 512, :].rearrange(
                "(a p) n -> p a n", p=128), ybuf[:, :].rearrange("p (a n) -> p a n", a=4),
                waits=[ld], sem="s_st")
            B_ybuf.read(dat)
            fin.append(dat)
        for q in range(4):
            tt = blk * 4 + q
            bx = tt % 2
            sw = p.dma("sync", xbuf[bx][:, :], spill_d[tt * 128:(tt + 1) * 128, :],
                       waits=B_xbuf[bx].wwaits() + ([at_ar[0]] if tt < 2 else []), sem=f"s_x{bx}")
            B_xbuf[bx].wrote(sw)
            fa = None
            for half in range(2):
                ttr = None
                for k in range(8):
                    woct = half * 8 + k
                    tw = ([ld] + B_pTB.wwaits() + (B_outb.wwaits() if half == 0 else [])
                          ) if k == 0 else []
                    ttr = p.op("tensor", lambda t, woct=woct, k=k, q=q: t.transpose(
                        pTB[:, k * 128:(k + 1) * 128],
                        ybuf[:, woct * 512 + q * 128: woct * 512 + q * 128 + 128], IDB),
                        waits=tw, inc=(k == 7))
                B_pTB.wrote(ttr)
                B_ybuf.read(ttr)
                for k in range(8):
                    woct = half * 8 + k
                    fa = p.op("vector", lambda v, woct=woct, k=k, bx=bx: v.tensor_tensor(
                        outb[:, woct * 128:(woct + 1) * 128], pTB[:, k * 128:(k + 1) * 128],
                        xbuf[bx][:, woct * 128:(woct + 1) * 128], ALU.add),
                        waits=([ttr, sw] if k == 0 else []), inc=(k == 7))
                    B_pTB.read(fa)
            B_outb.wrote(fa)
            B_xbuf[bx].read(fa)
            tok = p.dma("gpsimd", y_out[tt * 128:(tt + 1) * 128, :], outb[:, :],
                        waits=[fa], sem="s_st")
            fin.append(tok)
            B_outb.read(tok)
            B_xbuf[bx].read(tok)
    if DBG:
        fin.append(p.dma("gpsimd", dbg_xn, xnT[:, :], waits=[xn_ready, blk_done[3]], sem="s_st"))


    p.op("gpsimd", lambda g: g.memset(stat[0:1, 40:41], 0.0), waits=fin, inc=False)

    with nc.Block() as block:
        p.assemble(block)
    ctx.close()
    return nc


_NC_CACHE = [None]


def _host_prep(inputs, core):
    role = core % 2
    batch = core // 2
    f32 = np.float32
    x = np.asarray(inputs["L0_attn_x"][batch])
    d = {}
    d["x_pi"] = np.ascontiguousarray(x[:, PIH], dtype=f32)
    cs = slice(role * NH * DK, (role + 1) * NH * DK)
    vs = slice(role * NH * VE, (role + 1) * NH * VE)
    d["wq"] = np.ascontiguousarray(np.asarray(inputs["L0_attn_Wq"])[PIH][:, cs], f32)
    d["wk"] = np.ascontiguousarray(np.asarray(inputs["L0_attn_Wk"])[PIH][:, cs], f32)
    d["wv"] = np.ascontiguousarray(np.asarray(inputs["L0_attn_Wv"])[PIH][:, vs], f32)
    wa = np.asarray(inputs["L0_attn_Wa"])[PIH][:, 8 * role:8 * role + 8]
    wb = np.asarray(inputs["L0_attn_Wb"])[PIH][:, 8 * role:8 * role + 8]
    wab = np.zeros((H, 40), f32)
    wab[:, 0:8] = wa
    wab[:, 32:40] = wb
    d["wab"] = wab
    wo = np.asarray(inputs["L0_attn_Wo"])[:, PIH]
    d["wo"] = np.ascontiguousarray(wo[role * 2048:(role + 1) * 2048], f32)
    for nm, wkey, bkey, sl in (("qconv", "L0_attn_qcw", "L0_attn_qcb", cs),
                               ("kconv", "L0_attn_kcw", "L0_attn_kcb", cs),
                               ("vconv", "L0_attn_vcw", "L0_attn_vcb", vs)):
        cw_ = np.asarray(inputs[wkey])[sl]
        cb_ = np.asarray(inputs[bkey])[sl]
        arr = np.zeros((cw_.shape[0], 8), f32)
        arr[:, 0:4] = cw_
        arr[:, 4] = cb_
        d[nm] = arr
    idx = _pi_i(role)
    ok = idx >= 0
    gup = np.asarray(inputs["L0_ffn_gate_up_packed"])
    gate_rows = np.zeros((I_PAD, gup.shape[1]), np.int32)
    gate_rows[ok] = gup[idx[ok]]
    up_rows = np.zeros((I_PAD, gup.shape[1]), np.int32)
    up_rows[ok] = gup[idx[ok] + I_FULL]
    rows = np.concatenate([gate_rows, up_rows], axis=0)
    tr = np.ascontiguousarray(rows.T)
    hw = tr.view(np.int16).reshape(gup.shape[1], 2 * I_PAD, 2)
    d["gupT_hw0"] = np.ascontiguousarray(hw[:, :, 0])
    d["gupT_hw1"] = np.ascontiguousarray(hw[:, :, 1])
    gsc = np.asarray(inputs["L0_ffn_gate_up_scales"]).astype(np.float16)
    sc_cols = np.zeros((16, 2 * I_PAD), np.float16)
    sc_cols[:, 0:I_PAD][:, ok] = gsc[:, idx[ok]]
    sc_cols[:, I_PAD:][:, ok] = gsc[:, idx[ok] + I_FULL]
    hgrp = (8 * (np.arange(H) % 256) + (np.arange(H) // 256)) // 128
    d["gup_scale"] = np.ascontiguousarray(sc_cols[hgrp])
    dp = np.asarray(inputs["L0_ffn_down_packed"])
    dpc = dp[:, 352 * role:352 * role + 352]
    dpt = np.zeros((384, H), np.int32)
    dpt[0:352] = np.ascontiguousarray(dpc.T)[:, PIH]
    hwd = np.ascontiguousarray(dpt).view(np.int16).reshape(384, H, 2)
    d["dwnT_hw0"] = np.ascontiguousarray(hwd[:, :, 0])
    d["dwnT_hw1"] = np.ascontiguousarray(hwd[:, :, 1])
    dsc = np.asarray(inputs["L0_ffn_down_scales"]).astype(np.float16)
    dsc_rows = np.zeros((I_PAD, H), np.float16)
    dsc_rows[ok] = dsc[(idx[ok] // 128)][:, PIH]
    d["dwn_scale"] = np.ascontiguousarray(dsc_rows)
    cf = np.zeros((128, 640), f32)
    cf[:, 0:128] = np.eye(128)
    cf[:, 128:256] = (np.arange(128)[:, None] <= np.arange(128)[None, :])
    cf[:, 256:384] = np.where(np.arange(128)[None, :] < np.arange(128)[:, None], 0.0, NEG_BIG)
    cf[:, 384:512] = np.where(np.arange(128)[:, None] <= np.arange(128)[None, :], 0.0, NEG_BIG)
    cf[0, 512:640] = 1.0
    d["consts_f"] = cf
    d["consts_bf"] = np.eye(128).astype(BF)
    return d


def kernel(**inputs):
    if _NC_CACHE[0] is None:
        _NC_CACHE[0] = build_program()
    nc = _NC_CACHE[0]
    in_maps = [_host_prep(inputs, core) for core in range(8)]
    res = run_bass_kernel_spmd(nc, in_maps, core_ids=list(range(8)))
    out = np.zeros((4, T, H), np.float32)
    inv = np.empty(H, dtype=np.int64)
    inv[PIH] = np.arange(H)
    for core in range(8):
        role, batch = core % 2, core // 2
        y = res.results[core]["y2"]
        out[batch, role * THALF:(role + 1) * THALF, :] = \
            y[role * THALF:(role + 1) * THALF][:, inv]
    return out
